# version 02
# baseline (speedup 1.0000x reference)
import contextlib
import numpy as np

try:
    import concourse.bass as bass
except ImportError:
    import sys
    sys.path.insert(0, "/opt/trn_rl_repo")
    import concourse.bass as bass
import concourse.mybir as mybir
import ml_dtypes
from concourse.bass_utils import run_bass_kernel_spmd
from concourse.alu_op_type import AluOpType as ALU

F32 = mybir.dt.float32
BF16 = mybir.dt.bfloat16
AF = mybir.ActivationFunctionType

BF = ml_dtypes.bfloat16

NCORE = 8
B, S, D = 16, 2048, 1024
SEQN = 2
TCORE = SEQN * S
NDT = D // 128
CH = 512
NCH_SEQ = S // CH
INNER = 2048
NCT = INNER // 128
NST = 16
KCONV = 4
EXP_N = 8
IMOE = 1024
NHEAD = 16
HDIM = 64
IFFN = 2816
NIT = IFFN // 128
ROUND_C = float(3 * (1 << 22))
QMAX = 127.0

ENGS = ("pe", "vec", "sca", "gp", "sy")

_BLOCKS = 4
_LIMIT = None
_LIMIT2 = None
_SKIP_P1 = False
_SKIP_P2 = False
_LIMIT3 = None
_SKIP_P3 = False
_DEBUG = False


class Chan:
    __slots__ = ("name", "step", "count", "h", "engs", "is_dma")
    def __init__(self, name, step=1, is_dma=False):
        self.name, self.step, self.count = name, step, 0
        self.h = None
        self.engs = set()
        self.is_dma = is_dma


class Buf:
    __slots__ = ("ready", "frees")
    def __init__(self):
        self.ready = None
        self.frees = []
    def wr(self):
        w = list(self.frees)
        self.frees = []
        return w


class Prog:
    def __init__(self):
        self.ops = {e: [] for e in ENGS}
        self.chans = []

    def chan(self, name, step=1, is_dma=False):
        c = Chan(name, step, is_dma)
        self.chans.append(c)
        return c

    def add(self, eng, fn, waits=(), inc=None, bar=False):
        ws = [w for w in waits if w is not None]
        if eng in ("vec", "sca"):
            bar = True
        if bar and inc is None:
            inc = self.engine_chan[eng]
        ev = None
        if inc is not None:
            inc.count += inc.step
            inc.engs.add(eng)
            ev = (inc, inc.count)
        self.ops[eng].append((ws, fn, inc, bar, None if ev is None else ev[1]))
        return ev

    def emit(self, eng_name, eng):
        satisfied = {}
        pending_bar = None
        for ws, fn, inc, bar, val in self.ops[eng_name]:
            if pending_bar is not None:
                c, v = pending_bar
                if satisfied.get(c, -1) < v:
                    eng.wait_ge(c.h, v)
                    satisfied[c] = v
                pending_bar = None
            for (c, v) in ws:
                if not c.is_dma and c.engs == {eng_name}:
                    continue
                if satisfied.get(c, -1) >= v:
                    continue
                eng.wait_ge(c.h, v)
                satisfied[c] = v
            inst = fn(eng)
            if inc is not None:
                assert inst is not None
                inst.then_inc(inc.h, inc.step)
            if bar:
                assert inc is not None
                pending_bar = (inc, val)

    def tally(self):
        return {e: len(self.ops[e]) for e in ENGS}


class Carver:
    def __init__(self, total):
        self.off, self.total = 0, total
    def take(self, n, align=8):
        a = (self.off + align - 1) // align * align
        self.off = a + n
        assert self.off <= self.total, f"carve overflow {self.off}>{self.total}"
        return a


def build_program(n_blocks):
    nc = bass.Bass("TRN2", target_bir_lowering=False, debug=False,
                   detect_race_conditions=False)
    p = Prog()

    c_pe = p.chan("pe")
    c_vec = p.chan("vec")
    c_sca = p.chan("sca")
    c_gp = p.chan("gp")
    p.engine_chan = {"pe": c_pe, "vec": c_vec, "sca": c_sca, "gp": c_gp, "sy": None}
    c_dg = p.chan("dg", 16, True)
    c_stg = [p.chan(f"stg{i}", 16, True) for i in range(2)]
    c_wost = p.chan("wost", 16, True)
    c_misc0 = p.chan("m0", 16, True)
    c_wot = [p.chan(f"wot{i}", 16, True) for i in range(2)]
    c_ring = [p.chan(f"rng{i}", 16, True) for i in range(4)]
    c_gub = p.chan("gub", 16, True)
    c_wrow = p.chan("wrow", 16, True)
    c_misc2 = p.chan("m2", 16, True)
    c_out = [p.chan(f"out{i}", 16, True) for i in range(2)]
    c_pre = p.chan("pre", 16, True)

    x_in = nc.dram_tensor("x", [SEQN, S, D], F32, kind="ExternalInput").ap()
    y_out = nc.dram_tensor("y", [SEQN, S, D], F32, kind="ExternalOutput").ap()

    def din(name, shape):
        return nc.dram_tensor(name, shape, F32, kind="ExternalInput").ap()

    w_in = din("L0_attn_W_in", [2 * INNER, D])
    conv_w = din("L0_attn_conv_w", [INNER, 1, KCONV])
    conv_b = din("L0_attn_conv_b", [INNER])
    a_log = din("L0_attn_A_log", [INNER, NST])
    d_vec = din("L0_attn_D", [INNER])
    w_out = din("L0_attn_W_out", [D, INNER])
    rw = din("L0_ffn_router_weight", [EXP_N, D])
    rb = din("L0_ffn_router_bias", [EXP_N])
    gup = din("L0_ffn_gate_up_proj", [EXP_N, D, 2 * IMOE])
    gub = din("L0_ffn_gate_up_bias", [EXP_N, 2 * IMOE])
    dwn = din("L0_ffn_down_proj", [EXP_N, IMOE, D])
    dwb = din("L0_ffn_down_bias", [EXP_N, D])
    wqk = din("L1_attn_Wqk", [D, 2 * D])
    bqk = din("L1_attn_bqk", [2 * D])
    wv = din("L1_attn_Wv", [D, D])
    bv = din("L1_attn_bv", [D])
    wn = din("L1_attn_Wn", [D, NHEAD])
    bn = din("L1_attn_bn", [NHEAD])
    wo = din("L1_attn_Wo", [D, D])
    bo = din("L1_attn_bo", [D])
    clip_x = din("L1_ffn_clip_x", [1])
    clip_down = din("L1_ffn_clip_down", [1])
    w_gate = din("L1_ffn_w_gate", [IFFN, D])
    w_up = din("L1_ffn_w_up", [IFFN, D])
    w_down = din("L1_ffn_w_down", [D, IFFN])

    c_eye = nc.dram_tensor("c_eye", [128, 128], F32, kind="ExternalInput").ap()
    c_ptriu = nc.dram_tensor("c_ptriu", [128, 896], BF16, kind="ExternalInput").ap()
    c_eh = nc.dram_tensor("c_eh", [NHEAD, D], BF16, kind="ExternalInput").ap()
    c_ones = nc.dram_tensor("c_ones", [128, 128], BF16, kind="ExternalInput").ap()

    woutt_h = nc.dram_tensor("woutt_h", [INNER, D], BF16, kind="Internal").ap()
    wqg_h = nc.dram_tensor("wqg_h", [NIT, 128, NDT * 128], BF16, kind="Internal").ap()
    wqu_h = nc.dram_tensor("wqu_h", [NIT, 128, NDT * 128], BF16, kind="Internal").ap()
    wqkb_h = nc.dram_tensor("wqkb_h", [D, 2 * D], BF16, kind="Internal").ap()
    wvb_h = nc.dram_tensor("wvb_h", [D, D], BF16, kind="Internal").ap()
    wob_h = nc.dram_tensor("wob_h", [D, D], BF16, kind="Internal").ap()
    gupb_h = nc.dram_tensor("gupb_h", [EXP_N, D, 2 * IMOE], BF16, kind="Internal").ap()
    dwnb_h = nc.dram_tensor("dwnb_h", [EXP_N, IMOE, D], BF16, kind="Internal").ap()
    wqd_h = nc.dram_tensor("wqd_h", [IFFN, D], BF16, kind="Internal").ap()
    hsp_h = nc.dram_tensor("hsp_h", [IFFN, TCORE], BF16, kind="Internal").ap()
    cc1i = nc.dram_tensor("cc1i", [1, 16], F32, kind="Internal").ap()
    cc1o = nc.dram_tensor("cc1o", [1, 16], F32, kind="Internal", addr_space="Shared").ap()
    cc2i = nc.dram_tensor("cc2i", [1, 16], F32, kind="Internal").ap()
    cc2o = nc.dram_tensor("cc2o", [1, 16], F32, kind="Internal", addr_space="Shared").ap()
    dbg = (nc.dram_tensor("dbg", [24, 128, 512], F32, kind="ExternalOutput").ap()
           if _DEBUG else None)

    with contextlib.ExitStack() as st:
        e = st.enter_context
        RES = e(nc.sbuf_tensor("res", [128, NDT * TCORE], BF16))
        AW = e(nc.sbuf_tensor("aw", [128, 32768], BF16))
        ABF = e(nc.sbuf_tensor("abf", [128, 18944], BF16))
        AF32 = e(nc.sbuf_tensor("af32", [128, 4104], F32))
        STATS = e(nc.sbuf_tensor("stats", [128, 2936], F32))
        LNBF = e(nc.sbuf_tensor("lnbf", [128, 1024], BF16))
        EYE = e(nc.sbuf_tensor("eye", [128, 128], F32))
        ONESB = e(nc.sbuf_tensor("onesb", [128, 128], BF16))
        PTRIU = e(nc.sbuf_tensor("ptriu", [128, 896], BF16))
        EHB = e(nc.sbuf_tensor("ehb", [NHEAD, D], BF16))
        SMALL = e(nc.sbuf_tensor("small", [128, 128], F32))
        DBB = e(nc.sbuf_tensor("dbb", [EXP_N, IMOE], BF16))
        GUBSTG = e(nc.sbuf_tensor("gubstg", [16, 128], F32))
        WTB = e(nc.sbuf_tensor("wtb", [EXP_N, 512], BF16))
        RWTB = e(nc.sbuf_tensor("rwtb", [128, 64], BF16))
        RBROW = e(nc.sbuf_tensor("rbrow", [1, 8], BF16))
        WROW = e(nc.sbuf_tensor("wrow", [1, 512], BF16))
        RNT = e(nc.sbuf_tensor("rnt", [16, 512], BF16))
        WNB = e(nc.sbuf_tensor("wnb", [128, 128], BF16))
        BNROW = e(nc.sbuf_tensor("bnrow", [1, 16], BF16))
        BVROW = e(nc.sbuf_tensor("bvrow", [1, 1024], BF16))

        PS = [e(nc.psum_tensor(f"ps{i}", [128, 512], F32)) for i in range(8)]
        psb = [Buf() for _ in range(8)]

        for c in p.chans:
            c.h = e(nc.semaphore(c.name))

        def dump(slot, ap, ev, buf=None):
            if not _DEBUG:
                return
            pp, ff = ap.shape[0], ap.shape[1]
            evd = p.add("gp", lambda e_, o=dbg[slot, 0:pp, 0:ff], i=ap:
                        e_.dma_start(out=o, in_=i), waits=[ev], inc=c_dg)
            if buf is not None:
                buf.frees.append(evd)

        O_ANEG, O_D, O_CW, O_CB = 0, 256, 272, 336
        O_HST = 352
        LNR = [888 + i * 512 for i in range(4)]
        ln_mean_b = LNBF[0:1, 0:512]
        ln_rstd_b = LNBF[0:1, 512:1024]
        O_TAIL = 16

        p.add("sy", lambda e_: e_.dma_start(out=EYE[:, :], in_=c_eye[:, :]), inc=c_misc0)
        p.add("sy", lambda e_: e_.dma_start(out=ONESB[:, :], in_=c_ones[:, :]), inc=c_misc0)

        for e2 in range(EXP_N):
            p.add("gp", lambda e_, o=gupb_h[e2], i=gup[e2]:
                  e_.dma_start(out=o, in_=i), inc=c_pre)
            p.add("gp", lambda e_, o=dwnb_h[e2], i=dwn[e2]:
                  e_.dma_start(out=o, in_=i), inc=c_pre)
        p.add("gp", lambda e_: e_.dma_start(out=wqkb_h[:, :], in_=wqk[:, :]), inc=c_pre)
        p.add("gp", lambda e_: e_.dma_start(out=wvb_h[:, :], in_=wv[:, :]), inc=c_pre)
        p.add("gp", lambda e_: e_.dma_start(out=wob_h[:, :], in_=wo[:, :]), inc=c_pre)
        ev_pre_all = (c_pre, c_pre.count)
        p.add("sy", lambda e_: e_.dma_start(out=PTRIU[:, :], in_=c_ptriu[:, :]), inc=c_misc0)
        p.add("sy", lambda e_: e_.dma_start(out=EHB[:, :], in_=c_eh[:, :]), inc=c_misc0)

        for ct in range(NCT):
            p.add("sy", lambda e_, o=STATS[:, O_CW + ct * 4: O_CW + ct * 4 + 4],
                  i=conv_w[ct * 128:(ct + 1) * 128, 0, :]:
                  e_.dma_start(out=o, in_=i), inc=c_misc0)
            p.add("sy", lambda e_, o=STATS[:, O_CB + ct: O_CB + ct + 1],
                  i=conv_b[ct * 128:(ct + 1) * 128].rearrange("(p one) -> p one", one=1):
                  e_.dma_start(out=o, in_=i), inc=c_misc0)
            p.add("sy", lambda e_, o=STATS[:, O_D + ct: O_D + ct + 1],
                  i=d_vec[ct * 128:(ct + 1) * 128].rearrange("(p one) -> p one", one=1):
                  e_.dma_start(out=o, in_=i), inc=c_misc0)
            p.add("sy", lambda e_, o=STATS[:, O_ANEG + ct * NST: O_ANEG + (ct + 1) * NST],
                  i=a_log[ct * 128:(ct + 1) * 128, :]:
                  e_.dma_start(out=o, in_=i), inc=c_misc0)
        ev_misc0 = (c_misc0, c_misc0.count)
        ev_eye = ev_ones = ev_cw = ev_misc0
        ev_aneg = None
        for ct in range(NCT):
            evx = p.add("sca", lambda e_, o=SMALL[:, 0:NST],
                        i=STATS[:, O_ANEG + ct * NST: O_ANEG + (ct + 1) * NST]:
                        e_.activation(out=o, in_=i, func=AF.Exp),
                        waits=[ev_misc0, ev_aneg], inc=c_sca)
            ev_aneg = p.add("vec", lambda e_, o=STATS[:, O_ANEG + ct * NST: O_ANEG + (ct + 1) * NST],
                            i=SMALL[:, 0:NST]:
                            e_.tensor_scalar_mul(o, i, -1.0), waits=[evx], inc=c_vec)

        STG = [AF32[:, 0:1024], AF32[:, 1024:2048]]
        stg_b = [Buf(), Buf()]

        def transpose_128(src_ap, dst_ap, pslot, waits, cast_buf=None):
            pst = PS[6 + pslot]
            evt = p.add("pe", lambda e_, o=pst[0:128, 0:128], i=src_ap:
                        e_.transpose(o, i, EYE[:, :]),
                        waits=list(waits) + psb[6 + pslot].wr() + [ev_misc0], inc=c_pe)
            evc = p.add("vec", lambda e_, o=dst_ap, i=pst[0:128, 0:128]:
                        e_.tensor_copy(out=o, in_=i), waits=[evt], inc=c_vec, bar=True)
            psb[6 + pslot].frees.append(evc)
            return evt, evc

        tcount = 0
        for seq in range(SEQN):
            for tt in range(S // 128):
                sl = (seq * (S // 128) + tt) % 2
                evl = p.add("sy", lambda e_, o=STG[sl], i=x_in[seq, tt * 128:(tt + 1) * 128, :]:
                            e_.dma_start(out=o, in_=i),
                            waits=stg_b[sl].wr(), inc=c_stg[sl])
                for dt in range(NDT):
                    evt, evc = transpose_128(
                        STG[sl][:, dt * 128:(dt + 1) * 128],
                        RES[:, dt * TCORE + seq * S + tt * 128:][:, :128],
                        tcount % 2, [evl])
                    tcount += 1
                    if dt == NDT - 1:
                        stg_b[sl].frees.append(evt)
        ev_res_ready = (c_vec, c_vec.count)

        ev_winT = None
        for ct2 in range(2 * NCT):
            sl = ct2 % 2
            evl = p.add("sy", lambda e_, o=STG[sl], i=w_in[ct2 * 128:(ct2 + 1) * 128, :]:
                        e_.dma_start(out=o, in_=i), waits=stg_b[sl].wr(), inc=c_stg[sl])
            for dt in range(NDT):
                evt, evc = transpose_128(
                    STG[sl][:, dt * 128:(dt + 1) * 128],
                    AW[:, dt * 4096 + ct2 * 128:][:, :128],
                    tcount % 2, [evl])
                tcount += 1
                if dt == NDT - 1:
                    stg_b[sl].frees.append(evt)
                ev_winT = evc

        WOTMP = ABF[:, 0:128]
        wotmp_b = Buf()
        ev_woutt = None
        for dt in range(NDT):
            for half in range(2):
                sl = (dt * 2 + half) % 2
                evl = p.add("sy", lambda e_, o=STG[sl],
                            i=w_out[dt * 128:(dt + 1) * 128, half * 1024:(half + 1) * 1024]:
                            e_.dma_start(out=o, in_=i), waits=stg_b[sl].wr(), inc=c_stg[sl])
                for j in range(8):
                    ct2 = half * 8 + j
                    evt, evc = transpose_128(
                        STG[sl][:, j * 128:(j + 1) * 128], WOTMP,
                        tcount % 2, [evl] + wotmp_b.wr())
                    tcount += 1
                    if j == 7:
                        stg_b[sl].frees.append(evt)
                    ev_woutt = p.add("sy", lambda e_,
                                     o=woutt_h[ct2 * 128:(ct2 + 1) * 128, dt * 128:(dt + 1) * 128],
                                     i=WOTMP:
                                     e_.dma_start(out=o, in_=i), waits=[evc], inc=c_wost)
                    wotmp_b.frees.append(ev_woutt)

        cbf = Carver(18944)
        XLN_OFF = cbf.take(8 * CH)
        YF_OFF = cbf.take(NCT * CH)
        HT_OFF = cbf.take(2 * CH)
        TREE_OFF = cbf.take(5 * CH)
        YACC_OFF = cbf.take(CH)
        RSIL_OFF = cbf.take(CH)
        WOT_OFF = cbf.take(2 * 1024)

        XLNT = [ABF[:, XLN_OFF + dt * CH:XLN_OFF + (dt + 1) * CH] for dt in range(NDT)]
        TREE = [ABF[:, TREE_OFF + i * CH:TREE_OFF + (i + 1) * CH] for i in range(5)]
        MB, RBB, TMPB = TREE[0], TREE[1], TREE[2]
        SQ = [ABF[:, YF_OFF + i * CH:YF_OFF + (i + 1) * CH] for i in range(2)]
        HT = [ABF[:, HT_OFF + i * CH:HT_OFF + (i + 1) * CH] for i in range(2)]
        YACC = ABF[:, YACC_OFF:YACC_OFF + CH]
        RSIL = ABF[:, RSIL_OFF:RSIL_OFF + CH]
        WOT = [ABF[:, WOT_OFF + i * 1024:WOT_OFF + (i + 1) * 1024] for i in range(2)]

        caf = Carver(4104)
        caf.off = 2048
        XS_OFF = caf.take(520)
        CACC_OFF = caf.take(512)
        AT_OFF = caf.take(2 * 512)
        XS = AF32[:, XS_OFF:XS_OFF + 515]
        CACC = AF32[:, CACC_OFF:CACC_OFF + 512]
        ATIL = [AF32[:, AT_OFF + i * 512:AT_OFF + (i + 1) * 512] for i in range(2)]
        XCONV = AF32[:, 0:512]
        DELTA = AF32[:, 512:1024]
        BX = AF32[:, 1024:1536]

        sq_b = [Buf(), Buf()]
        atil_b = [Buf(), Buf()]
        ht_b = [Buf(), Buf()]
        tree_b = [Buf() for _ in range(5)]
        yacc_b = Buf()
        rsil_b = Buf()
        cacc_b = Buf()
        xconv_b = Buf()
        xs_b = Buf()
        yf_b = [Buf() for _ in range(NCT)]
        wot_b = [Buf(), Buf()]
        xln_b = Buf()

        def emit_ln(t0, res_ready_evs, rstd_mul_col=None, rstd_mul_ev=None, outs=None):
            bsum, bsq = 0, 1
            w0 = psb[bsum].wr() + res_ready_evs
            ev_s1 = None
            for dt in range(NDT):
                ev_s1 = p.add("pe", lambda e_, o=PS[bsum][0:1, 0:512],
                      l=ONESB[:, 0:1], r=RES[:, dt * TCORE + t0:][:, :CH], s=(dt == 0),
                      st=(dt == NDT - 1):
                      e_.matmul(o, l, r, start=s, stop=st),
                      waits=(w0 + [ev_ones]) if dt == 0 else [],
                      inc=(c_pe if dt == NDT - 1 else None))
            w1 = psb[bsq].wr()
            ev_s2 = None
            for dt in range(NDT):
                evq = p.add("sca", lambda e_, o=SQ[dt % 2], i=RES[:, dt * TCORE + t0:][:, :CH]:
                            e_.activation(out=o, in_=i, func=AF.Square),
                            waits=sq_b[dt % 2].wr() + res_ready_evs, inc=c_sca)
                ev_s2 = p.add("pe", lambda e_, o=PS[bsq][0:1, 0:512], l=ONESB[:, 0:1], r=SQ[dt % 2],
                              s=(dt == 0), st=(dt == NDT - 1):
                              e_.matmul(o, l, r, start=s, stop=st),
                              waits=[evq] + (w1 if dt == 0 else []), inc=c_pe)
                sq_b[dt % 2].frees.append(ev_s2)
            R0, R1, R2, R3 = (STATS[0:1, LNR[i]:LNR[i] + 512] for i in range(4))
            p.add("vec", lambda e_, o=R0, i=PS[bsum][0:1, 0:512]:
                  e_.tensor_scalar_mul(o, i, -1.0 / D), waits=[ev_s1], bar=True)
            p.add("vec", lambda e_, o=R1, a=R0: e_.tensor_mul(o, a, a))
            p.add("vec", lambda e_, o=R2, i=PS[bsq][0:1, 0:512], m=R1:
                  e_.scalar_tensor_tensor(out=o, in0=i, scalar=1.0 / D, in1=m,
                                          op0=ALU.mult, op1=ALU.subtract), waits=[ev_s2], bar=True)
            ev_fr = p.add("vec", lambda e_, o=R3, i=R2:
                          e_.tensor_scalar_add(o, i, 1e-5), inc=c_vec)
            psb[bsum].frees.append(ev_fr)
            psb[bsq].frees.append(ev_fr)
            ev_sq = p.add("sca", lambda e_, o=R1, i=R3:
                          e_.activation(out=o, in_=i, func=AF.Sqrt), waits=[ev_fr], inc=c_sca)
            p.add("vec", lambda e_, o=R2, i=R1: e_.reciprocal(o, i), waits=[ev_sq], bar=True)
            p.add("vec", lambda e_, o=R1, a=R2: e_.tensor_mul(o, a, a))
            p.add("vec", lambda e_, o=R1, a=R3, b=R1: e_.tensor_mul(o, a, b))
            p.add("vec", lambda e_, o=R1, i=R1:
                  e_.tensor_scalar(o, i, -0.5, 1.5, ALU.mult, ALU.add))
            p.add("vec", lambda e_, o=R1, a=R2, b=R1: e_.tensor_mul(o, a, b))
            if rstd_mul_col is not None:
                p.add("vec", lambda e_, o=R1, i=R1, s=rstd_mul_col:
                      e_.tensor_scalar_mul(o, i, s), waits=[rstd_mul_ev])
            ev_nm = p.add("vec", lambda e_, o=ln_mean_b, i=R0:
                          e_.tensor_copy(out=o, in_=i), inc=c_vec)
            ev_rs = p.add("vec", lambda e_, o=ln_rstd_b, i=R1:
                          e_.tensor_copy(out=o, in_=i), inc=c_vec)
            bm, br = 2, 3
            evbm = p.add("pe", lambda e_, o=PS[bm][0:128, 0:512], l=ONESB[0:1, :], r=ln_mean_b:
                         e_.matmul(o, l, r, start=True, stop=True),
                         waits=[ev_nm] + psb[bm].wr(), inc=c_pe)
            evbr = p.add("pe", lambda e_, o=PS[br][0:128, 0:512], l=ONESB[0:1, :], r=ln_rstd_b:
                         e_.matmul(o, l, r, start=True, stop=True),
                         waits=[ev_rs] + psb[br].wr(), inc=c_pe)
            evMB = p.add("vec", lambda e_, o=MB, i=PS[bm][0:128, 0:512]:
                         e_.tensor_copy(out=o, in_=i), waits=[evbm] + tree_b[0].wr(), inc=c_vec,
                         bar=True)
            evRB = p.add("vec", lambda e_, o=RBB, i=PS[br][0:128, 0:512]:
                         e_.tensor_copy(out=o, in_=i), waits=[evbr] + tree_b[1].wr(), inc=c_vec,
                         bar=True)
            psb[bm].frees.append(evMB)
            psb[br].frees.append(evRB)
            ev_last = None
            otiles = outs if outs is not None else XLNT
            for dt in range(NDT):
                p.add("vec", lambda e_, o=TMPB, a=RES[:, dt * TCORE + t0:][:, :CH], b=MB:
                      e_.tensor_add(o, a, b),
                      waits=([evMB, evRB] + tree_b[2].wr()) if dt == 0 else [])
                ev_last = p.add("vec", lambda e_, o=otiles[dt], a=TMPB, b=RBB:
                                e_.tensor_mul(o, a, b),
                                inc=(c_vec if dt == NDT - 1 else None))
            tree_b[0].frees.append(ev_last)
            tree_b[1].frees.append(ev_last)
            tree_b[2].frees.append(ev_last)
            return ev_last

        carry_ev = [[None] * NCT for _ in range(SEQN)]
        tail_ev = [None] * NCT

        ev_hst0 = p.add("vec", lambda e_: e_.memset(STATS[:, O_HST:O_HST + SEQN * NCT * NST], 0.0),
                        inc=c_vec)

        LSEQ, LCHK, LCT = _LIMIT if _LIMIT is not None else (SEQN, NCH_SEQ, NCT)
        if _SKIP_P1:
            LSEQ = 0
        if n_blocks >= 1:
            for seq in range(LSEQ):
                ev_tz = p.add("vec", lambda e_: e_.memset(SMALL[:, O_TAIL:O_TAIL + NCT * 3], 0.0),
                              inc=c_vec)
                for ct in range(NCT):
                    tail_ev[ct] = ev_tz
                for chk in range(LCHK):
                    t0 = seq * S + chk * CH
                    rdy = [ev_res_ready] if (seq == 0 and chk == 0) else []
                    ev_xln = emit_ln(t0, rdy + xln_b.wr())

                    wot_ld = [None] * LCT
                    for ct in range(min(2, LCT)):
                        sl = ct % 2
                        wot_ld[ct] = p.add("sy",
                                           lambda e_, o=WOT[sl], i=woutt_h[ct * 128:(ct + 1) * 128, :]:
                                           e_.dma_start(out=o, in_=i),
                                           waits=wot_b[sl].wr() + [ev_woutt], inc=c_wot[sl])

                    for ct in range(LCT):
                        bx_m = ct % 2
                        bx_r = 4 + ct % 2
                        ev_mmx = None
                        for dt in range(NDT):
                            ev_mmx = p.add("pe", lambda e_, o=PS[bx_m][0:128, 0:512],
                                  l=AW[:, dt * 4096 + ct * 128:][:, :128], r=XLNT[dt], s=(dt == 0),
                                  st=(dt == NDT - 1):
                                  e_.matmul(o, l, r, start=s, stop=st),
                                  waits=([ev_xln, ev_winT] + psb[bx_m].wr()) if dt == 0 else [],
                                  inc=(c_pe if dt == NDT - 1 else None))
                        ev_mmr = None
                        for dt in range(NDT):
                            ev_mmr = p.add("pe", lambda e_, o=PS[bx_r][0:128, 0:512],
                                  l=AW[:, dt * 4096 + (NCT + ct) * 128:][:, :128], r=XLNT[dt],
                                  s=(dt == 0), st=(dt == NDT - 1):
                                  e_.matmul(o, l, r, start=s, stop=st),
                                  waits=psb[bx_r].wr() if dt == 0 else [],
                                  inc=(c_pe if dt == NDT - 1 else None))

                        ev_xs = p.add("vec", lambda e_, o=XS[:, 3:515], i=PS[bx_m][0:128, 0:512]:
                                      e_.tensor_copy(out=o, in_=i),
                                      waits=[ev_mmx] + xs_b.wr(), inc=c_vec, bar=True)
                        if seq == 0 and chk == 2 and ct == 0:
                            dump(18, XS[:, 3:515], ev_xs, xs_b)
                        psb[bx_m].frees.append(ev_xs)
                        p.add("vec", lambda e_, o=XS[:, 0:3],
                              i=SMALL[:, O_TAIL + ct * 3:O_TAIL + ct * 3 + 3]:
                              e_.tensor_copy(out=o, in_=i), waits=[tail_ev[ct]])
                        p.add("vec", lambda e_, o=CACC, i=XS[:, 0:512],
                              s=STATS[:, O_CW + ct * 4:O_CW + ct * 4 + 1]:
                              e_.tensor_scalar_mul(o, i, s),
                              waits=cacc_b.wr() + [ev_cw])
                        for k in (1, 2, 3):
                            p.add("vec", lambda e_, o=CACC, i=XS[:, k:k + 512],
                                  s=STATS[:, O_CW + ct * 4 + k:O_CW + ct * 4 + k + 1], m=CACC:
                                  e_.scalar_tensor_tensor(out=o, in0=i, scalar=s, in1=m,
                                                          op0=ALU.mult, op1=ALU.add),
                                  inc=(c_vec if k == 3 else None))
                        ev_conv = (c_vec, c_vec.count)
                        tail_ev[ct] = p.add("vec", lambda e_,
                                            o=SMALL[:, O_TAIL + ct * 3:O_TAIL + ct * 3 + 3],
                                            i=XS[:, 512:515]:
                                            e_.tensor_copy(out=o, in_=i), inc=c_vec)

                        ev_xc = p.add("sca", lambda e_, o=XCONV, i=CACC,
                                      b=STATS[:, O_CB + ct:O_CB + ct + 1]:
                                      e_.activation(out=o, in_=i, func=AF.Silu, bias=b),
                                      waits=[ev_conv] + xconv_b.wr(), inc=c_sca)
                        if seq == 0 and chk == 2 and ct == 0:
                            dump(19, XCONV, ev_xc, xconv_b)
                        ev_rsl = p.add("sca", lambda e_, o=RSIL, i=PS[bx_r][0:128, 0:512]:
                                       e_.activation(out=o, in_=i, func=AF.Silu),
                                       waits=[ev_mmr] + rsil_b.wr(), inc=c_sca, bar=True)
                        psb[bx_r].frees.append(ev_rsl)
                        SPD = XS[:, 0:512]
                        ev_ab = p.add("sca", lambda e_, o=SPD, i=XCONV:
                                      e_.activation(out=o, in_=i, func=AF.Abs),
                                      waits=[ev_conv], inc=c_sca)
                        ev_u = p.add("sca", lambda e_, o=DELTA, i=SPD:
                                     e_.activation(out=o, in_=i, func=AF.Exp, scale=-1.0),
                                     waits=[ev_ab], inc=c_sca)
                        ev_w = p.add("vec", lambda e_, o=SPD, i=DELTA:
                                     e_.tensor_scalar_add(o, i, 1.0), waits=[ev_u], inc=c_vec)
                        ev_l = p.add("sca", lambda e_, o=DELTA, i=SPD:
                                     e_.activation(out=o, in_=i, func=AF.Ln),
                                     waits=[ev_w], inc=c_sca)
                        ev_r = p.add("vec", lambda e_, o=CACC, i=XCONV:
                                     e_.tensor_scalar_max(o, i, 0.0),
                                     waits=[ev_xc] + cacc_b.wr(), inc=c_vec)
                        ev_dl = p.add("vec", lambda e_, o=SPD, a=DELTA, b=CACC:
                                      e_.tensor_add(o, a, b), waits=[ev_l], inc=c_vec)
                        cacc_b.frees.append(ev_dl)
                        ev_bx = p.add("sca", lambda e_, o=BX, i=XCONV:
                                      e_.activation(out=o, in_=i, func=AF.Square), inc=c_sca)
                        if seq == 0 and chk == 2 and ct == 0:
                            dump(20, SPD, ev_dl, xs_b)
                            dump(21, BX, ev_bx)
                            dump(22, RSIL, ev_rsl, rsil_b)

                        ev_h_even = None
                        ev_ysum = None
                        for n in range(NST):
                            asl = n % 2
                            hsl = n % 2
                            ev_a = p.add("sca", lambda e_, o=ATIL[asl], i=SPD,
                                         s=STATS[:, O_ANEG + ct * NST + n:O_ANEG + ct * NST + n + 1]:
                                         e_.activation(out=o, in_=i, func=AF.Exp, scale=s),
                                         waits=[ev_dl, ev_aneg] + atil_b[asl].wr(), inc=c_sca)
                            ini = STATS[:, O_HST + (seq * NCT + ct) * NST + n:][:, :1]
                            ev_h = p.add("vec", lambda e_, o=HT[hsl], a=ATIL[asl], b=BX, i=ini:
                                         e_.tensor_tensor_scan(out=o, data0=a, data1=b, initial=i,
                                                               op0=ALU.mult, op1=ALU.add),
                                         waits=[ev_a, ev_bx, carry_ev[seq][ct] or ev_hst0]
                                               + ht_b[hsl].wr(),
                                         inc=c_vec)
                            atil_b[asl].frees.append(ev_h)
                            if n % 2 == 0:
                                ev_h_even = ev_h

                            ev_cs = p.add("gp", lambda e_, o=ini, i=HT[hsl][:, CH - 1:CH]:
                                          e_.tensor_copy(out=o, in_=i), waits=[ev_h], inc=c_gp)
                            if n == NST - 1:
                                carry_ev[seq][ct] = ev_cs
                            ht_b[hsl].frees.append(ev_cs)
                            if n % 2 == 1:
                                k = n // 2
                                B1, B2, B3, T, T2 = TREE

                                def padd(o_, buf_i, extra=()):
                                    ev_ = p.add("gp", lambda e_, o=o_, a=HT[0], b=HT[1]:
                                                e_.tensor_add(o, a, b),
                                                waits=[ev_h, ev_h_even] + tree_b[buf_i].wr()
                                                      + list(extra),
                                                inc=c_gp)
                                    ht_b[0].frees.append(ev_)
                                    ht_b[1].frees.append(ev_)
                                    return ev_

                                def tadd(o_, a_, b_, buf_o, frees):
                                    ev_ = p.add("gp", lambda e_, o=o_, a=a_, b=b_:
                                                e_.tensor_add(o, a, b),
                                                waits=tree_b[buf_o].wr() if buf_o is not None else
                                                      yacc_b.wr(),
                                                inc=c_gp)
                                    for fi in frees:
                                        tree_b[fi].frees.append(ev_)
                                    return ev_

                                if k % 2 == 0:
                                    padd(B1, 0)
                                else:
                                    padd(T, 3)
                                    if k == 1 or k == 5:
                                        tadd(B2, B1, T, 1, (0, 3))
                                    elif k == 3:
                                        tadd(T2, B1, T, 4, (0, 3))
                                        tadd(B3, B2, T2, 2, (1, 4))
                                    elif k == 7:
                                        tadd(T2, B1, T, 4, (0, 3))
                                        tadd(T, B2, T2, 3, (1, 4))
                                        ev_ysum = tadd(YACC, B3, T, None, (2, 3))
                        xs_b.frees.append((c_sca, c_sca.count))
                        ev_t1 = p.add("vec", lambda e_, o=YACC, a=YACC, b=RSIL:
                                      e_.tensor_mul(o, a, b), waits=[ev_ysum, ev_rsl],
                                      inc=c_vec)
                        rsil_b.frees.append(ev_t1)
                        YFT = ABF[:, YF_OFF + ct * CH:YF_OFF + (ct + 1) * CH]
                        ev_yf = p.add("vec", lambda e_, o=YFT, i=YACC,
                                      s=STATS[:, O_D + ct:O_D + ct + 1], x=XCONV:
                                      e_.scalar_tensor_tensor(out=o, in0=i, scalar=s, in1=x,
                                                              op0=ALU.add, op1=ALU.mult),
                                      waits=[ev_xc] + yf_b[ct].wr(), inc=c_vec)
                        xconv_b.frees.append(ev_yf)
                        if seq == 0 and chk == 2:
                            dump(ct, YFT, ev_yf, yf_b[ct])
                        if seq == 0 and chk == 0 and ct == 0:
                            dump(16, YACC, ev_t1, yacc_b)
                            dump(17, YFT, ev_yf, yf_b[ct])
                        yacc_b.frees.append(ev_yf)
                        yf_b[ct].ready = ev_yf

                    ev_bank = [None] * NDT
                    for ct in range(LCT):
                        sl = ct % 2
                        for dt in range(NDT):
                            ev_m = p.add("pe", lambda e_, o=PS[dt][0:128, 0:512],
                                         l=WOT[sl][:, dt * 128:(dt + 1) * 128],
                                         r=ABF[:, YF_OFF + ct * CH:YF_OFF + (ct + 1) * CH],
                                         s=(ct == 0), st=(ct == LCT - 1):
                                         e_.matmul(o, l, r, start=s, stop=st),
                                         waits=([yf_b[ct].ready, wot_ld[ct]] +
                                                (psb[dt].wr() if ct == 0 else [])),
                                         inc=(c_pe if (dt == NDT - 1 or ct == LCT - 1) else None))
                            if ct == LCT - 1:
                                ev_bank[dt] = ev_m
                            if dt == NDT - 1:
                                wot_b[sl].frees.append(ev_m)
                        if ct + 2 < LCT:
                            sl2 = (ct + 2) % 2
                            wot_ld[ct + 2] = p.add("sy",
                                                   lambda e_, o=WOT[sl2],
                                                   i=woutt_h[(ct + 2) * 128:(ct + 3) * 128, :]:
                                                   e_.dma_start(out=o, in_=i),
                                                   waits=wot_b[sl2].wr(), inc=c_wot[sl2])
                    for dt in range(NDT):
                        ev_ra = p.add("vec", lambda e_, o=RES[:, dt * TCORE + t0:][:, :CH],
                                      a=PS[dt][0:128, 0:512], b=RES[:, dt * TCORE + t0:][:, :CH]:
                                      e_.tensor_add(o, a, b), waits=[ev_bank[dt]], inc=c_vec,
                                      bar=True)

                        psb[dt].frees.append(ev_ra)
                        if dt == NDT - 1:
                            for ct in range(NCT):
                                yf_b[ct].frees.append(ev_ra)
                            xln_b.frees.append(ev_ra)

        def barrier():
            return [(c_pe, c_pe.count), (c_vec, c_vec.count),
                    (c_sca, c_sca.count), (c_gp, c_gp.count)]

        if n_blocks >= 2 and not _SKIP_P2:
            bar1 = barrier()
            for dt in range(NDT):
                for e2 in range(EXP_N):
                    p.add("gp", lambda e_, o=RWTB[:, dt * 8 + e2:dt * 8 + e2 + 1],
                          i=rw[e2, dt * 128:(dt + 1) * 128].rearrange("(p one) -> p one", one=1):
                          e_.dma_start(out=o, in_=i), inc=c_misc2)
            p.add("gp", lambda e_, o=RBROW[:, :],
                  i=rb[:].rearrange("(one f) -> one f", one=1):
                  e_.dma_start(out=o, in_=i), inc=c_misc2)
            p.add("gp", lambda e_, o=DBB[:, :], i=dwb[:, :]:
                  e_.dma_start(out=o, in_=i), inc=c_misc2)
            ev_misc2 = (c_misc2, c_misc2.count)
            ev_rwt = ev_rbb = ev_dbb = ev_misc2

            moe = dict(
                xln2_b=Buf(), glu_b=[Buf() for _ in range(8)], actw_b=[Buf() for _ in range(8)],
                gt_b=Buf(), up_b=Buf(), sig_b=Buf(), wbt_b=Buf(), wtb_b=Buf(), wrow_b=Buf(),
                wsc_b=Buf(),
                dacc_b=[Buf() for _ in range(NDT)], gub_b=Buf(),
                ring_b=[Buf() for _ in range(4)],
            )
            GLUA = [ABF[:, YF_OFF + i * CH:YF_OFF + (i + 1) * CH] for i in range(8)]
            ACTW = [ABF[:, YF_OFF + (8 + i) * CH:YF_OFF + (9 + i) * CH] for i in range(8)]
            WBT = ABF[:, HT_OFF:HT_OFF + CH]
            GTT = TREE[3]
            UP1 = TREE[4]
            SIGT = ABF[:, YACC_OFF:YACC_OFF + CH]
            DACC = [AF32[:, ht * 512:(ht + 1) * 512] for ht in range(NDT)]
            RING = [AW[:, i * 8192:(i + 1) * 8192] for i in range(4)]
            O_GUB = 864
            O_RT = 64

            def ring_load(sloti, e2, kind, extra_waits=()):
                if kind == "d":
                    src = dwnb_h[e2].rearrange("(a p) f -> p a f", p=128)
                else:
                    base = {"g": 0, "u": IMOE}[kind]
                    src = gupb_h[e2][:, base:base + IMOE].rearrange("(a p) f -> p a f", p=128)
                return p.add("sy", lambda e_,
                             o=RING[sloti].rearrange("p (a f) -> p a f", a=NDT), i=src:
                             e_.dma_start(out=o, in_=i),
                             waits=moe["ring_b"][sloti].wr() + list(extra_waits) +
                                   [ev_pre_all],
                             inc=c_ring[sloti])

            ring_plan = []
            for e2 in range(EXP_N):
                ring_plan += [(e2, "g"), (e2, "u"), (e2, "d")]

            LSC, LEXP = _LIMIT2 if _LIMIT2 is not None else (NDT, EXP_N)
            for sc in range(LSC):
                t0 = sc * CH
                ev_xln2 = emit_ln(t0, (bar1 if sc == 0 else []) + moe["xln2_b"].wr())
                if sc == 0:
                    dump(9, XLNT[0], ev_xln2, moe["xln2_b"])
                ev_wsc = []
                for mi in range(4):
                    bk = 4 + mi % 2
                    wv_ = psb[bk].wr()
                    ev_lg = None
                    for dt in range(NDT):
                        ev_lg = p.add("pe", lambda e_, o=PS[bk][0:128, 0:8],
                                      l=XLNT[dt][:, mi * 128:(mi + 1) * 128],
                                      r=RWTB[:, dt * 8:(dt + 1) * 8], s=(dt == 0):
                                      e_.matmul(o, l, r, start=s, stop=False),
                                      waits=([ev_xln2, ev_rwt] + wv_) if dt == 0 else [])
                    ev_lg = p.add("pe", lambda e_, o=PS[bk][0:128, 0:8],
                                  l=ONESB[0:1, 0:128], r=RBROW[:, :]:
                                  e_.matmul(o, l, r, start=False, stop=True),
                                  waits=[ev_rbb], inc=c_pe)
                    NM1 = SMALL[:, O_RT:O_RT + 1]
                    M2 = SMALL[:, O_RT + 1:O_RT + 2]
                    RZ = SMALL[:, O_RT + 2:O_RT + 3]
                    ZZ = SMALL[:, O_RT + 3:O_RT + 4]
                    PEXP = SMALL[:, O_RT + 4:O_RT + 12]
                    DD = SMALL[:, O_RT + 12:O_RT + 20]
                    EQ = SMALL[:, O_RT + 20:O_RT + 28]
                    WSC = SMALL[:, 96 + mi * 8:96 + (mi + 1) * 8]
                    ev_nm1 = p.add("vec", lambda e_, o=NM1, i=PS[bk][0:128, 0:8]:
                          e_.tensor_reduce(o, i, mybir.AxisListType.X, ALU.max, negate=True),
                          waits=[ev_lg], inc=c_vec, bar=True)
                    ev_px = p.add("sca", lambda e_, o=PEXP, i=PS[bk][0:128, 0:8], b=NM1:
                                  e_.activation(out=o, in_=i, func=AF.Exp, bias=b),
                                  waits=[ev_lg, ev_nm1], inc=c_sca, bar=True)
                    p.add("vec", lambda e_, o=DD, i=PS[bk][0:128, 0:8], s=NM1:
                          e_.tensor_scalar_add(o, i, s), waits=[ev_lg], bar=True)
                    ev_fr = p.add("vec", lambda e_, o=EQ, i=DD:
                                  e_.tensor_single_scalar(o, i, 0.0, ALU.is_ge), inc=c_vec)
                    psb[bk].frees.append(ev_fr)
                    psb[bk].frees.append(ev_px)
                    p.add("vec", lambda e_, o=EQ, i=EQ, s=DD:
                          e_.scalar_tensor_tensor(out=o, in0=i, scalar=-1e30, in1=s,
                                                  op0=ALU.mult, op1=ALU.add))
                    p.add("vec", lambda e_, o=M2, i=EQ:
                          e_.tensor_reduce(o, i, mybir.AxisListType.X, ALU.max), bar=True)
                    p.add("vec", lambda e_, o=EQ, i=DD, s=M2:
                          e_.tensor_single_scalar(o, i, s, ALU.is_ge))
                    p.add("vec", lambda e_, o=ZZ, i=PEXP:
                          e_.tensor_reduce(o, i, mybir.AxisListType.X, ALU.add), waits=[ev_px])
                    p.add("vec", lambda e_, o=RZ, i=ZZ: e_.reciprocal(o, i), bar=True)
                    ev_w = p.add("vec", lambda e_, o=WSC, i=PEXP, s=RZ, m=EQ:
                                 e_.scalar_tensor_tensor(out=o, in0=i, scalar=s, in1=m,
                                                         op0=ALU.mult, op1=ALU.mult),
                                 waits=moe["wsc_b"].wr(), inc=c_vec)
                    ev_wsc.append(ev_w)
                    if sc == 0 and mi == 0:
                        dump(10, SMALL[:, 64:92], ev_w)
                if sc == 0:
                    dump(0, SMALL[:, 96:128], ev_wsc[3])
                ev_wtb = None
                for mi in range(4):
                    pst = PS[6]
                    evt = p.add("pe", lambda e_, o=pst[0:8, 0:128],
                                i=SMALL[:, 96 + mi * 8:96 + (mi + 1) * 8]:
                                e_.transpose(o, i, EYE[:, :]),
                                waits=[ev_wsc[mi]] + psb[6].wr(), inc=c_pe)
                    ev_wtb = p.add("vec", lambda e_, o=WTB[0:8, mi * 128:(mi + 1) * 128],
                                   i=pst[0:8, 0:128]:
                                   e_.tensor_copy(out=o, in_=i),
                                   waits=[evt] + (moe["wtb_b"].wr() if mi == 0 else []),
                                   inc=c_vec, bar=True)
                    psb[6].frees.append(ev_wtb)
                    if mi == 3:
                        moe["wsc_b"].frees.append((c_pe, c_pe.count))
                if sc == 0:
                    ring_ld = {}
                    ring_pos = 0
                    for k in range(4):
                        e2, kind = ring_plan[ring_pos]
                        ring_ld[(e2, kind)] = ring_load(k, e2, kind, extra_waits=bar1)
                        ring_pos += 1

                for e2 in range(LEXP):
                    sg = (3 * e2) % 4
                    su = (3 * e2 + 1) % 4
                    sd = (3 * e2 + 2) % 4
                    ev_g1 = p.add("sy", lambda e_, o=GUBSTG[:, :],
                                  i=gub[e2, :].rearrange("(a p) -> a p", p=128):
                                  e_.dma_start(out=o, in_=i),
                                  waits=moe["gub_b"].wr(), inc=c_gub)
                    pst = PS[7]
                    ev_g2 = p.add("pe", lambda e_, o=pst[0:128, 0:16], i=GUBSTG[:, :]:
                                  e_.transpose(o, i, EYE[0:16, 0:16]),
                                  waits=[ev_g1] + psb[7].wr(), inc=c_pe)
                    moe["gub_b"].frees.append(ev_g2)
                    ev_g3 = p.add("vec", lambda e_, o=STATS[:, O_GUB:O_GUB + 16], i=pst[0:128, 0:16]:
                                  e_.tensor_copy(out=o, in_=i), waits=[ev_g2], inc=c_vec, bar=True)
                    psb[7].frees.append(ev_g3)
                    ev_g4 = p.add("vec", lambda e_, o=STATS[:, O_GUB + 16:O_GUB + 24],
                                  i=STATS[:, O_GUB + 8:O_GUB + 16]:
                                  e_.tensor_scalar_add(o, i, 1.0), inc=c_vec)
                    if sc == 0 and e2 == 0:
                        dump(6, STATS[:, O_GUB:O_GUB + 24], ev_g4)
                    bk = 4 + e2 % 2
                    ev_wr = p.add("sy", lambda e_, o=WROW[:, :], i=WTB[e2:e2 + 1, 0:512]:
                                  e_.dma_start(out=o, in_=i),
                                  waits=[ev_wtb] + moe["wrow_b"].wr(), inc=c_wrow)
                    ev_wb = p.add("pe", lambda e_, o=PS[bk][0:128, 0:512],
                                  l=ONESB[0:1, 0:128], r=WROW[:, :]:
                                  e_.matmul(o, l, r, start=True, stop=True),
                                  waits=[ev_wr] + psb[bk].wr(), inc=c_pe)
                    moe["wrow_b"].frees.append(ev_wb)
                    ev_wbc = p.add("vec", lambda e_, o=WBT, i=PS[bk][0:128, 0:512]:
                                   e_.tensor_copy(out=o, in_=i),
                                   waits=[ev_wb] + moe["wbt_b"].wr(), inc=c_vec, bar=True)
                    if sc == 0 and e2 == 0:
                        dump(7, WBT, ev_wbc, moe["wbt_b"])
                    psb[bk].frees.append(ev_wbc)
                    if e2 == LEXP - 1:
                        moe["wtb_b"].frees.append(ev_wb)

                    evg = ring_ld[(e2, "g")]
                    for ft in range(8):
                        bk2 = ft % 2
                        ev_mg = None
                        for dt in range(NDT):
                            ev_mg = p.add("pe", lambda e_, o=PS[bk2][0:128, 0:512],
                                          l=RING[sg][:, dt * 1024 + ft * 128:][:, :128],
                                          r=XLNT[dt], s=(dt == 0), st=(dt == NDT - 1):
                                          e_.matmul(o, l, r, start=s, stop=st),
                                          waits=([evg, ev_xln2] + psb[bk2].wr()) if dt == 0 else [],
                                          inc=(c_pe if dt == NDT - 1 else None))
                            if ft == 7 and dt == NDT - 1:
                                moe["ring_b"][sg].frees.append(ev_mg)
                        ev_gt = p.add("vec", lambda e_, o=GTT, i=PS[bk2][0:128, 0:512],
                                      s=STATS[:, O_GUB + ft:O_GUB + ft + 1]:
                                      e_.tensor_scalar(o, i, s, 7.0, ALU.add, ALU.min),
                                      waits=[ev_mg, ev_g3] + moe["gt_b"].wr(), inc=c_vec, bar=True)
                        psb[bk2].frees.append(ev_gt)
                        ev_sig = p.add("sca", lambda e_, o=SIGT, i=GTT:
                                       e_.activation(out=o, in_=i, func=AF.Sigmoid, scale=1.702),
                                       waits=[ev_gt] + moe["sig_b"].wr(), inc=c_sca)
                        ev_glu = p.add("vec", lambda e_, o=GLUA[ft], a=GTT, b=SIGT:
                                       e_.tensor_mul(o, a, b),
                                       waits=[ev_sig] + moe["glu_b"][ft].wr(), inc=c_vec)
                        if sc == 0 and e2 == 0 and ft == 0:
                            dump(1, GTT, ev_gt, moe["gt_b"])
                            dump(2, GLUA[ft], ev_glu, moe["glu_b"][ft])
                        moe["gt_b"].frees.append(ev_glu)
                        moe["sig_b"].frees.append(ev_glu)
                        moe["glu_b"][ft].ready = ev_glu
                    evu = ring_ld[(e2, "u")]
                    for ft in range(8):
                        bk2 = 2 + ft % 2
                        ev_mu = None
                        for dt in range(NDT):
                            ev_mu = p.add("pe", lambda e_, o=PS[bk2][0:128, 0:512],
                                          l=RING[su][:, dt * 1024 + ft * 128:][:, :128],
                                          r=XLNT[dt], s=(dt == 0), st=(dt == NDT - 1):
                                          e_.matmul(o, l, r, start=s, stop=st),
                                          waits=([evu] + psb[bk2].wr()) if dt == 0 else [],
                                          inc=(c_pe if dt == NDT - 1 else None))
                            if ft == 7 and dt == NDT - 1:
                                moe["ring_b"][su].frees.append(ev_mu)
                            if ft == 7 and dt == NDT - 1 and e2 == LEXP - 1:
                                moe["xln2_b"].frees.append(ev_mu)
                        ev_u1 = p.add("vec", lambda e_, o=UP1, i=PS[bk2][0:128, 0:512],
                                      s=STATS[:, O_GUB + 16 + ft:O_GUB + 17 + ft]:
                                      e_.tensor_scalar(o, i, s, 8.0, ALU.add, ALU.min),
                                      waits=[ev_mu, ev_g4] + moe["up_b"].wr(), inc=c_vec, bar=True)
                        psb[bk2].frees.append(ev_u1)
                        p.add("vec", lambda e_, o=UP1, i=UP1:
                              e_.tensor_scalar_max(o, i, -6.0))
                        p.add("vec", lambda e_, o=UP1, a=UP1, b=GLUA[ft]:
                              e_.tensor_mul(o, a, b), waits=[moe["glu_b"][ft].ready])
                        ev_aw = p.add("vec", lambda e_, o=ACTW[ft], a=UP1, b=WBT:
                                      e_.tensor_mul(o, a, b),
                                      waits=[ev_wbc] + moe["actw_b"][ft].wr(), inc=c_vec)
                        moe["up_b"].frees.append(ev_aw)
                        moe["glu_b"][ft].frees.append(ev_aw)
                        moe["actw_b"][ft].ready = ev_aw
                        if sc == 0 and e2 == 0 and ft == 0:
                            dump(3, ACTW[ft], ev_aw, moe["actw_b"][ft])
                        if e2 == LEXP - 1:
                            moe["wbt_b"].frees.append(ev_aw)
                    moe["wbt_b"].frees.append((c_vec, c_vec.count))
                    evd = ring_ld[(e2, "d")]
                    for ht in range(NDT):
                        bk2 = ht % 2
                        wv_ = psb[bk2].wr()
                        ev_md = None
                        first = True
                        if e2 == 0:
                            ev_md = p.add("pe", lambda e_, o=PS[bk2][0:128, 0:512],
                                          l=DBB[:, ht * 128:(ht + 1) * 128], r=WTB[0:8, 0:512]:
                                          e_.matmul(o, l, r, start=True, stop=False),
                                          waits=[ev_dbb, ev_wtb] + wv_, inc=None)
                            first = False
                        for it in range(8):
                            ev_md = p.add("pe", lambda e_, o=PS[bk2][0:128, 0:512],
                                          l=RING[sd][:, it * 1024 + ht * 128:][:, :128],
                                          r=ACTW[it], s=first and it == 0, st=(it == 7):
                                          e_.matmul(o, l, r, start=s, stop=st),
                                          waits=([evd] + (wv_ if first else []) +
                                                 [moe["actw_b"][i2].ready for i2 in range(8)])
                                                if it == 0 else [],
                                          inc=(c_pe if it == 7 else None))
                            if ht == NDT - 1 and it == 7:
                                moe["ring_b"][sd].frees.append(ev_md)
                        if e2 == 0:
                            ev_da = p.add("vec", lambda e_, o=DACC[ht], i=PS[bk2][0:128, 0:512]:
                                          e_.tensor_copy(out=o, in_=i),
                                          waits=[ev_md] + moe["dacc_b"][ht].wr(), inc=c_vec,
                                          bar=True)
                        else:
                            ev_da = p.add("vec", lambda e_, o=DACC[ht], a=PS[bk2][0:128, 0:512],
                                          b=DACC[ht]:
                                          e_.tensor_add(o, a, b), waits=[ev_md], inc=c_vec,
                                          bar=True)
                        psb[bk2].frees.append(ev_da)
                        if sc == 0 and ht == 0 and e2 == 0:
                            dump(4, DACC[ht], ev_da, moe["dacc_b"][ht])
                        if e2 == LEXP - 1:
                            moe["dacc_b"][ht].ready = ev_da
                            if sc == 0 and ht == 0:
                                dump(5, DACC[ht], ev_da, moe["dacc_b"][ht])
                    for i2 in range(8):
                        moe["actw_b"][i2].frees.append((c_pe, c_pe.count))
                    for _ in range(3):
                        if ring_pos < len(ring_plan):
                            e3, kind3 = ring_plan[ring_pos]
                            slot3 = (3 * e3 + {"g": 0, "u": 1, "d": 2}[kind3]) % 4
                            ring_ld[(e3, kind3)] = ring_load(slot3, e3, kind3)
                            ring_pos += 1
                for ht in range(NDT):
                    ev_r2 = p.add("vec", lambda e_, o=RES[:, ht * TCORE + t0:][:, :CH],
                                  a=DACC[ht], b=RES[:, ht * TCORE + t0:][:, :CH]:
                                  e_.tensor_add(o, a, b), inc=c_vec)
                    moe["dacc_b"][ht].frees.append(ev_r2)
                    if ht == NDT - 1:
                        moe["xln2_b"].frees.append(ev_r2)
                    if sc == 0 and ht == 0:
                        dump(8, RES[:, 0:512], ev_r2)
                ring_plan = []
                for e2 in range(EXP_N):
                    ring_plan += [(e2, "g"), (e2, "u"), (e2, "d")]
                ring_pos_next = 0
                if sc < LSC - 1:
                    ring_ld = {}
                    ring_pos = 0
                    for k in range(4):
                        e3, kind3 = ring_plan[ring_pos]
                        slot3 = (3 * e3 + {"g": 0, "u": 1, "d": 2}[kind3]) % 4
                        ring_ld[(e3, kind3)] = ring_load(slot3, e3, kind3)
                        ring_pos += 1

        if n_blocks >= 3 and not _SKIP_P3:
            bar3 = barrier()
            QB = 512
            NQB = S // QB
            QTS_OFF = YF_OFF
            CTXT_OFF = YF_OFF + 4096
            ATT_OFF = HT_OFF
            RNB_OFF = TREE_OFF
            WQK_OFF = TREE_OFF + 3 * 512
            WVS_OFF = WQK_OFF + 3 * 1024
            WOS_OFF = TREE_OFF
            QTS = ABF[:, QTS_OFF:QTS_OFF + 4096]
            CTXT = ABF[:, CTXT_OFF:CTXT_OFF + 4096]
            ATT = [ABF[:, ATT_OFF + i * 512:ATT_OFF + (i + 1) * 512] for i in range(2)]
            RNB = ABF[:, RNB_OFF:RNB_OFF + 512]
            WQKS = [ABF[:, WQK_OFF + i * 1024:WQK_OFF + (i + 1) * 1024] for i in range(3)]
            WVS = [ABF[:, WVS_OFF + i * 512:WVS_OFF + (i + 1) * 512] for i in range(2)]
            WOS = [ABF[:, WOS_OFF + i * 128:WOS_OFF + (i + 1) * 128] for i in range(4)]
            KTC = AW[:, 16384:32768]
            VC = AW[:, 0:16384]
            O_BQK = 864
            O_BO = 880

            att = dict(qts_b=Buf(), ctxt_b=[Buf() for _ in range(16)], att_b=[Buf(), Buf()],
                       rnb_b=Buf(), wqk_b=[Buf() for _ in range(3)],
                       wv_b=[Buf() for _ in range(2)], wo_b=[Buf() for _ in range(4)],
                       xln3_b=Buf(), rnt_b=Buf(), bstg_b=Buf(), kv_free_ev=None)
            c_wqk = [p.chan(f"wqk{i}", 16, True) for i in range(3)]
            c_wv = [p.chan(f"wv{i}", 16, True) for i in range(2)]
            c_wo = [p.chan(f"wo{i}", 16, True) for i in range(4)]
            c_misc3 = p.chan("m3", 16, True)
            c_bstg = p.chan("bstg", 16, True)
            for c in (c_wqk + c_wv + c_wo + [c_misc3, c_bstg]):
                c.h = st.enter_context(nc.semaphore(c.name))

            for dt in range(NDT):
                p.add("gp", lambda e_, o=WNB[:, dt * 16:(dt + 1) * 16],
                      i=wn[dt * 128:(dt + 1) * 128, :]:
                      e_.dma_start(out=o, in_=i), inc=c_misc3)
            p.add("gp", lambda e_, o=BNROW[:, :], i=bn[:].rearrange("(one f) -> one f", one=1):
                  e_.dma_start(out=o, in_=i), inc=c_misc3)
            p.add("gp", lambda e_, o=BVROW[:, :], i=bv[:].rearrange("(one f) -> one f", one=1):
                  e_.dma_start(out=o, in_=i), inc=c_misc3)
            ev_misc3 = (c_misc3, c_misc3.count)
            def col_load(vec_ap, n16, dst_off, wt_extra):
                ev1 = p.add("sy", lambda e_, o=GUBSTG[0:n16, :], i=vec_ap:
                            e_.dma_start(out=o, in_=i),
                            waits=att["bstg_b"].wr() + wt_extra, inc=c_bstg)
                ev2 = p.add("pe", lambda e_, o=PS[7][0:128, 0:n16], i=GUBSTG[0:n16, :]:
                            e_.transpose(o, i, EYE[0:n16, 0:n16]),
                            waits=[ev1] + psb[7].wr(), inc=c_pe)
                att["bstg_b"].frees.append(ev2)
                ev3 = p.add("vec", lambda e_, o=STATS[:, dst_off:dst_off + n16],
                            i=PS[7][0:128, 0:n16]:
                            e_.tensor_copy(out=o, in_=i), waits=[ev2], inc=c_vec)
                psb[7].frees.append(ev3)
                return ev3
            ev_bqk = col_load(bqk[:].rearrange("(a p) -> a p", p=128), 16, O_BQK, bar3)
            ev_bo = col_load(bo[:].rearrange("(a p) -> a p", p=128), 8, O_BO, [])

            def wqk_load(sloti, ft, extra=()):
                return p.add("sy", lambda e_,
                             o=WQKS[sloti].rearrange("p (a f) -> p a f", a=NDT),
                             i=wqkb_h[:, ft * 128:(ft + 1) * 128].rearrange(
                                 "(a p) f -> p a f", p=128):
                             e_.dma_start(out=o, in_=i),
                             waits=att["wqk_b"][sloti].wr() + list(extra) + [ev_pre_all],
                             inc=c_wqk[sloti])

            def wv_load(sloti, dt, g, extra=()):
                return p.add("sy", lambda e_, o=WVS[sloti],
                             i=wvb_h[dt * 128:(dt + 1) * 128, g * 512:(g + 1) * 512]:
                             e_.dma_start(out=o, in_=i),
                             waits=att["wv_b"][sloti].wr() + list(extra) + [ev_pre_all],
                             inc=c_wv[sloti])

            def wo_load(sloti, di, do, extra=()):
                return p.add("sy", lambda e_, o=WOS[sloti],
                             i=wob_h[di * 128:(di + 1) * 128, do * 128:(do + 1) * 128]:
                             e_.dma_start(out=o, in_=i),
                             waits=att["wo_b"][sloti].wr() + list(extra) + [ev_pre_all],
                             inc=c_wo[sloti])

            LSEQ3, LQB3 = _LIMIT3 if _LIMIT3 is not None else (SEQN, NQB)
            for seq in range(LSEQ3):
                for qb in range(LQB3):
                    t0 = seq * S + qb * QB
                    ev_xln3 = emit_ln(t0, (bar3 if (seq == 0 and qb == 0) else []) +
                                      att["xln3_b"].wr())
                    ev_rn4 = []
                    for mi in range(4):
                        bk = 4 + mi % 2
                        wv_ = psb[bk].wr()
                        ev_n = None
                        for dt in range(NDT):
                            ev_n = p.add("pe", lambda e_, o=PS[bk][0:128, 0:16],
                                         l=XLNT[dt][:, mi * 128:(mi + 1) * 128],
                                         r=WNB[:, dt * 16:(dt + 1) * 16], s=(dt == 0):
                                         e_.matmul(o, l, r, start=s, stop=False),
                                         waits=([ev_xln3, ev_misc3] + wv_) if dt == 0 else [])
                        ev_n = p.add("pe", lambda e_, o=PS[bk][0:128, 0:16],
                                     l=ONESB[0:1, 0:128], r=BNROW[:, :]:
                                     e_.matmul(o, l, r, start=False, stop=True), inc=c_pe)
                        EXPN = SMALL[:, 80:96]
                        ev_e = p.add("sca", lambda e_, o=EXPN, i=PS[bk][0:128, 0:16]:
                                     e_.activation(out=o, in_=i, func=AF.Exp), waits=[ev_n])
                        psb[bk].frees.append(ev_e)
                        RNm = SMALL[:, 64 + 0:80] if False else SMALL[:, 64:80]
                        ev_r = p.add("vec", lambda e_, o=RNm, i=EXPN:
                                     e_.reciprocal(o, i), waits=[ev_e, (c_sca, c_sca.count)])
                        ev_t = p.add("pe", lambda e_, o=PS[6][0:16, 0:128], i=RNm:
                                     e_.transpose(o, i, EYE[:, :]),
                                     waits=[ev_r] + psb[6].wr(), inc=c_pe)
                        ev_c = p.add("vec", lambda e_, o=RNT[:, mi * 128:(mi + 1) * 128],
                                     i=PS[6][0:16, 0:128]:
                                     e_.tensor_copy(out=o, in_=i),
                                     waits=[ev_t] + (att["rnt_b"].wr() if mi == 0 else []),
                                     inc=c_vec)
                        psb[6].frees.append(ev_c)
                        ev_rn4.append(ev_c)
                    ev_rnt = ev_rn4[-1]
                    if seq == 0 and qb == 1:
                        dump(11, RNT[0:16, 0:512], ev_rnt, att["rnt_b"])

                    if seq == 0 and qb == 0:
                        wqk_ld = {}
                        for k3 in range(3):
                            wqk_ld[k3] = wqk_load(k3, k3, extra=bar3)
                        p3_ftg = [0]
                    ev_qts_last = None
                    ev_ktev = None
                    for ft in range(16):
                        sl = p3_ftg[0] % 3
                        bk = ft % 2
                        evw = wqk_ld[sl]
                        ev_m = None
                        for dt in range(NDT):
                            ev_m = p.add("pe", lambda e_, o=PS[bk][0:128, 0:512],
                                         l=WQKS[sl][:, dt * 128:(dt + 1) * 128], r=XLNT[dt],
                                         s=(dt == 0), st=(dt == NDT - 1):
                                         e_.matmul(o, l, r, start=s, stop=st),
                                         waits=([evw, ev_xln3] + psb[bk].wr()) if dt == 0 else [],
                                         inc=(c_pe if dt == NDT - 1 else None))
                        att["wqk_b"][sl].frees.append(ev_m)
                        if ft < 8:
                            bk2 = 4 + ft % 2
                            ev_b = p.add("pe", lambda e_, o=PS[bk2][0:128, 0:512],
                                         l=EHB[0:16, ft * 128:(ft + 1) * 128], r=RNT[:, 0:512]:
                                         e_.matmul(o, l, r, start=True, stop=True),
                                         waits=[ev_rnt] + psb[bk2].wr(), inc=c_pe)
                            ev_rb = p.add("vec", lambda e_, o=RNB, i=PS[bk2][0:128, 0:512]:
                                          e_.tensor_copy(out=o, in_=i),
                                          waits=[ev_b] + att["rnb_b"].wr(), inc=c_vec)
                            psb[bk2].frees.append(ev_rb)
                            ev_q = p.add("vec", lambda e_, o=ABF[:, QTS_OFF + ft * 512:][:, :512],
                                         i=PS[bk][0:128, 0:512],
                                         s=STATS[:, O_BQK + ft:O_BQK + ft + 1], m=RNB:
                                         e_.scalar_tensor_tensor(out=o, in0=i, scalar=s, in1=m,
                                                                 op0=ALU.add, op1=ALU.mult),
                                         waits=[ev_m, ev_bqk, ev_rb] + (att["qts_b"].wr()
                                                                        if ft == 0 else []),
                                         inc=c_vec)
                            att["rnb_b"].frees.append(ev_q)
                            psb[bk].frees.append(ev_q)
                            ev_qts_last = ev_q
                            if seq == 0 and qb == 1 and ft == 0:
                                dump(12, ABF[:, QTS_OFF:QTS_OFF + 512], ev_q, att["qts_b"])
                                dump(13, RNB, ev_rb, att["rnb_b"])
                        else:
                            ftk = ft - 8
                            ev_k = p.add("vec", lambda e_,
                                         o=KTC[:, ftk * 2048 + qb * 512:][:, :512],
                                         i=PS[bk][0:128, 0:512],
                                         s=STATS[:, O_BQK + ft:O_BQK + ft + 1]:
                                         e_.tensor_scalar_add(o, i, s),
                                         waits=[ev_m, ev_bqk] + ([att["kv_free_ev"]]
                                                                  if (ft == 8 and qb == 0
                                                                      and seq > 0) else []),
                                         inc=c_vec)
                            psb[bk].frees.append(ev_k)
                            ev_ktev = ev_k
                            if seq == 0 and qb == 1 and ft == 8:
                                dump(14, KTC[:, 0:512], ev_k)
                                dump(19, KTC[:, 512:1024], ev_k)
                        tgt = p3_ftg[0] + 3
                        p3_ftg[0] += 1
                        if tgt < 16 * LQB3 * LSEQ3:
                            wqk_ld[tgt % 3] = wqk_load(tgt % 3, tgt % 16)

                    ev_vev = None
                    for g in range(2):
                        for dt in range(NDT):
                            sl = dt % 2
                            evl = wv_load(sl, dt, g,
                                          extra=(bar3 if (seq == 0 and qb == 0 and g == 0
                                                          and dt == 0) else []))
                            for mi in range(4):
                                bk = 2 + mi if False else (2 + mi)
                                ev_m = p.add("pe", lambda e_, o=PS[2 + mi][0:128, 0:512],
                                             l=XLNT[dt][:, mi * 128:(mi + 1) * 128],
                                             r=WVS[sl], s=(dt == 0):
                                             e_.matmul(o, l, r, start=s, stop=False),
                                             waits=([evl] + (psb[2 + mi].wr() if dt == 0 else [])),
                                             inc=(c_pe if mi == 3 else None))
                            att["wv_b"][sl].frees.append(ev_m)
                        for mi in range(4):
                            ev_m = p.add("pe", lambda e_, o=PS[2 + mi][0:128, 0:512],
                                         l=ONESB[0:1, 0:128], r=BVROW[:, g * 512:(g + 1) * 512]:
                                         e_.matmul(o, l, r, start=False, stop=True),
                                         waits=[ev_misc3], inc=c_pe)
                            ev_v = p.add("vec", lambda e_,
                                         o=VC[:, (qb * 4 + mi) * 1024 + g * 512:][:, :512],
                                         i=PS[2 + mi][0:128, 0:512]:
                                         e_.tensor_copy(out=o, in_=i),
                                         waits=[ev_m] + ([att["kv_free_ev"]]
                                                         if (qb == 0 and g == 0 and mi == 0
                                                             and seq > 0) else []),
                                         inc=c_vec)
                            psb[2 + mi].frees.append(ev_v)
                            ev_vev = ev_v
                            if seq == 0 and qb == 1 and g == 0 and mi == 0:
                                dump(15, VC[:, 0:512], ev_v)

                    nkt = 4 * (qb + 1)
                    ev_ctx_done = []
                    for h in range(16):
                        po = (h % 2) * 64
                        ftk = h // 2
                        bkc = 2 + h % 2
                        wvc = psb[bkc].wr()
                        ev_cm = None
                        for kt in range(nkt):
                            bks = kt % 2
                            ev_s = None
                            wv_ = psb[bks].wr()
                            ev_s = p.add("pe", lambda e_, o=PS[bks][0:128, 0:512],
                                         l=KTC[po:po + 64, ftk * 2048 + kt * 128:][:, :128],
                                         r=QTS[po:po + 64, :][:, (h // 2) * 512:(h // 2) * 512 + 512]
                                         if False else
                                         ABF[po:po + 64, QTS_OFF + ftk * 512:QTS_OFF + ftk * 512 + 512],
                                         s=True:
                                         e_.matmul(o, l, r, start=s, stop=True),
                                         waits=[ev_qts_last, ev_ktev] + wv_, inc=c_pe)
                            asl = kt % 2
                            if kt >= 4 * qb:
                                j = kt - 4 * qb
                                ev_a = p.add("vec", lambda e_, o=ATT[asl], a=PS[bks][0:128, 0:512],
                                             b=PTRIU[:, 384 - 128 * j:][:, :512]:
                                             e_.tensor_mul(o, a, b),
                                             waits=[ev_s] + att["att_b"][asl].wr(), inc=c_vec)
                                if seq == 0 and qb == 1 and h == 0 and kt == 4:
                                    dump(17, ATT[asl], ev_a, att["att_b"][asl])
                            else:
                                ev_a = p.add("vec", lambda e_, o=ATT[asl], i=PS[bks][0:128, 0:512]:
                                             e_.tensor_copy(out=o, in_=i),
                                             waits=[ev_s] + att["att_b"][asl].wr(), inc=c_vec)
                                if seq == 0 and qb == 1 and h == 0 and kt == 0:
                                    dump(18, ATT[asl], ev_a, att["att_b"][asl])
                            psb[bks].frees.append(ev_a)
                            ev_cm = p.add("pe", lambda e_, o=PS[bkc][po:po + 64, 0:512],
                                          l=VC[:, kt * 1024 + h * 64:][:, :64], r=ATT[asl],
                                          s=(kt == 0), st=(kt == nkt - 1):
                                          e_.matmul(o, l, r, start=s, stop=st),
                                          waits=[ev_a, ev_vev] + (wvc if kt == 0 else []),
                                          inc=c_pe)
                            att["att_b"][asl].frees.append(ev_cm)
                        ev_cc = p.add("vec", lambda e_,
                                      o=ABF[po:po + 64, CTXT_OFF + ftk * 512:CTXT_OFF + ftk * 512 + 512],
                                      i=PS[bkc][po:po + 64, 0:512]:
                                      e_.tensor_copy(out=o, in_=i),
                                      waits=[ev_cm] + (att["ctxt_b"][h].wr()), inc=c_vec)
                        psb[bkc].frees.append(ev_cc)
                        att["ctxt_b"][h].ready = ev_cc
                        if seq == 0 and qb == 1 and h == 1:
                            dump(16, ABF[:, CTXT_OFF:CTXT_OFF + 512], ev_cc, att["ctxt_b"][0])
                        if seq == 0 and qb == 1 and h == 15:
                            dump(20, ABF[:, CTXT_OFF + 7 * 512:CTXT_OFF + 8 * 512], ev_cc,
                                 att["ctxt_b"][15])

                    wo_ld = {}
                    for k3 in range(4):
                        wo_ld[k3] = wo_load(k3, k3, 0, extra=[ev_vev, ev_qts_last])
                    for do in range(NDT):
                        bk = 4 + do % 2
                        wv_ = psb[bk].wr()
                        ev_m = None
                        if seq == 0 and qb == 1 and do == 0:
                            dump(21, WOS[0], wo_ld[0], att["wo_b"][0])
                        for di in range(NDT):
                            idx = do * 8 + di
                            sl = idx % 4
                            evw = wo_ld[sl]
                            ev_m = p.add("pe", lambda e_, o=PS[bk][0:128, 0:512],
                                         l=WOS[sl],
                                         r=ABF[:, CTXT_OFF + di * 512:CTXT_OFF + (di + 1) * 512],
                                         s=(di == 0), st=(di == NDT - 1):
                                         e_.matmul(o, l, r, start=s, stop=st),
                                         waits=([evw] +
                                                [att["ctxt_b"][2 * di].ready,
                                                 att["ctxt_b"][2 * di + 1].ready] +
                                                (wv_ if di == 0 else [])),
                                         inc=c_pe)
                            att["wo_b"][sl].frees.append(ev_m)
                            nidx = idx + 4
                            if nidx < 64:
                                wo_ld[nidx % 4] = wo_load(nidx % 4, nidx % 8, nidx // 8)
                        ev_ra = p.add("vec", lambda e_, o=RES[:, do * TCORE + t0:][:, :512],
                                      a=PS[bk][0:128, 0:512],
                                      s=STATS[:, O_BO + do:O_BO + do + 1],
                                      b=RES[:, do * TCORE + t0:][:, :512]:
                                      e_.scalar_tensor_tensor(out=o, in0=a, scalar=s, in1=b,
                                                              op0=ALU.add, op1=ALU.add),
                                      waits=[ev_m, ev_bo], inc=c_vec)
                        if seq == 0 and qb == 1 and do == 0:
                            dump(23, RES[:, 0 * TCORE + t0:][:, :512], ev_m)
                        psb[bk].frees.append(ev_ra)
                        if seq == 0 and qb == 1 and do == 0:
                            dump(22, RES[:, 0 * TCORE + t0:][:, :512], ev_ra)
                        if do == NDT - 1:
                            ev_wo_done = (c_pe, c_pe.count)
                            tree_b[0].frees.append(ev_wo_done)
                            att["rnb_b"].frees.append(ev_wo_done)
                            att["xln3_b"].frees.append(ev_ra)
                            att["qts_b"].frees.append(ev_ra)
                            for h in range(16):
                                att["ctxt_b"][h].frees.append(ev_ra)
                    att["rnt_b"].frees.append((c_pe, c_pe.count))
                att["kv_free_ev"] = (c_pe, c_pe.count)

        if n_blocks >= 4:
            bar4 = barrier()
            c_wq = p.chan("wq", 16, True)
            c_wqs = p.chan("wqs", 16, True)
            c_hsp = p.chan("hsp", 16, True)
            c_hld = [p.chan(f"hld{i}", 16, True) for i in range(2)]
            c_wg = [p.chan(f"wg{i}", 16, True) for i in range(3)]
            c_wu = [p.chan(f"wu{i}", 16, True) for i in range(3)]
            c_wd = [p.chan(f"wd{i}", 16, True) for i in range(2)]
            c_cc = p.chan("ccio", 16, True)
            for c in ([c_wq, c_wqs, c_hsp, c_cc] + c_hld + c_wg + c_wu + c_wd):
                c.h = st.enter_context(nc.semaphore(c.name))

            SG_OFF, UP_OFF = 0, 512
            H_OFF, HQ_OFF = 1024, 2048
            WG_OFF, WU_OFF = 5120, 8192
            WD_OFF = 11264
            XQ_OFF = 13312
            WGS = [ABF[:, WG_OFF + i * 1024:WG_OFF + (i + 1) * 1024] for i in range(3)]
            WUS = [ABF[:, WU_OFF + i * 1024:WU_OFF + (i + 1) * 1024] for i in range(3)]
            SGT = ABF[:, SG_OFF:SG_OFF + 512]
            UPT = ABF[:, UP_OFF:UP_OFF + 512]
            HT4 = [ABF[:, H_OFF + i * 512:H_OFF + (i + 1) * 512] for i in range(2)]
            HQ4 = [ABF[:, HQ_OFF + i * 512:HQ_OFF + (i + 1) * 512] for i in range(2)]
            WDS = [ABF[:, WD_OFF + i * 1024:WD_OFF + (i + 1) * 1024] for i in range(2)]
            XQT = [ABF[:, XQ_OFF + i * 512:XQ_OFF + (i + 1) * 512] for i in range(NDT)]
            XLN4 = [AF32[:, i * 512:(i + 1) * 512] for i in range(NDT)]
            CCROW = SMALL[0:1, 80:96]

            ffn = dict(wq_b=Buf(), wg_b=[Buf() for _ in range(3)], wu_b=[Buf() for _ in range(3)],
                       wd_b=[Buf() for _ in range(2)], sg_b=Buf(), up_b=Buf(),
                       h_b=[Buf(), Buf()], hq_b=[Buf(), Buf()], xq_b=Buf(), ccrow_b=Buf())

            WQSTG = AF32[:, 0:2816]
            WQBF = ABF[:, 0:128]
            ev_wqs_done = None
            for widx, (wsrc, wdst, nrow_t, ncol) in enumerate(
                    [(w_gate, wqg_h, NIT, D), (w_up, wqu_h, NIT, D), (w_down, wqd_h, NDT, IFFN)]):
                for rt in range(nrow_t):
                    evl = p.add("sy", lambda e_, o=AF32[:, 0:ncol][:, 0:ncol],
                                i=wsrc[rt * 128:(rt + 1) * 128, :]:
                                e_.dma_start(out=o, in_=i),
                                waits=ffn["wq_b"].wr() + (bar4 if widx == 0 and rt == 0 else []),
                                inc=c_wq)
                    ev_mx = p.add("vec", lambda e_, o=SMALL[:, 66:67], i=AF32[:, 0:ncol]:
                                  e_.tensor_reduce(o, i, mybir.AxisListType.X, ALU.max,
                                                   apply_absolute_value=True),
                                  waits=[evl], inc=c_vec)
                    p.add("vec", lambda e_, o=SMALL[:, 71:72], i=SMALL[:, 66:67]:
                          e_.tensor_scalar_mul(o, i, 1.0 / QMAX))
                    p.add("vec", lambda e_, o=SMALL[:, 72:73], i=SMALL[:, 71:72]:
                          e_.reciprocal(o, i))
                    p.add("vec", lambda e_, o=AF32[:, 0:ncol], i=AF32[:, 0:ncol],
                          s=SMALL[:, 72:73]:
                          e_.tensor_scalar_mul(o, i, s))
                    p.add("vec", lambda e_, o=AF32[:, 0:ncol], i=AF32[:, 0:ncol]:
                          e_.tensor_scalar(o, i, ROUND_C, -ROUND_C, ALU.add, ALU.add))
                    ev_q = p.add("vec", lambda e_, o=AF32[:, 0:ncol], i=AF32[:, 0:ncol],
                                 s=SMALL[:, 71:72]:
                                 e_.tensor_scalar_mul(o, i, s), inc=c_vec)
                    for ct4 in range(ncol // 128):
                        pslot = 6 + ct4 % 2
                        evt = p.add("pe", lambda e_, o=PS[pslot][0:128, 0:128],
                                    i=AF32[:, ct4 * 128:(ct4 + 1) * 128]:
                                    e_.transpose(o, i, EYE[:, :]),
                                    waits=[ev_q] + psb[pslot].wr(), inc=c_pe)
                        if ct4 == ncol // 128 - 1:
                            ffn["wq_b"].frees.append(evt)
                        evc = p.add("vec", lambda e_, o=WQBF, i=PS[pslot][0:128, 0:128]:
                                    e_.tensor_copy(out=o, in_=i),
                                    waits=[evt] + wotmp_b.wr(), inc=c_vec)
                        psb[pslot].frees.append(evc)
                        if widx < 2:
                            odst = wdst[rt][:, ct4 * 128:(ct4 + 1) * 128]
                        else:
                            odst = wdst[ct4 * 128:(ct4 + 1) * 128, rt * 128:(rt + 1) * 128]
                        ev_s = p.add("sy", lambda e_, o=odst, i=WQBF:
                                     e_.dma_start(out=o, in_=i), waits=[evc], inc=c_wqs)
                        wotmp_b.frees.append(ev_s)
            ev_wqs_done = (c_wqs, c_wqs.count)

            ev_m0 = p.add("vec", lambda e_: e_.memset(SMALL[:, 64:128], 0.0),
                          waits=bar4, inc=c_vec)
            p.add("vec", lambda e_: e_.memset(CCROW, 0.0), inc=c_vec)
            for sc in range(NDT):
                t0 = sc * CH
                ev_x4 = emit_ln(t0, ([ev_m0] + bar4) if sc == 0 else [], outs=XLN4)
                for dt in range(NDT):
                    ev_mx = p.add("vec", lambda e_, o=SMALL[:, 66:67], i=XLN4[dt]:
                                  e_.tensor_reduce(o, i, mybir.AxisListType.X, ALU.max,
                                                   apply_absolute_value=True),
                                  waits=[ev_x4], inc=c_vec)
                    p.add("vec", lambda e_, o=SMALL[:, 64:65], a=SMALL[:, 64:65],
                          b=SMALL[:, 66:67]:
                          e_.tensor_max(o, a, b))
            ev_t = p.add("pe", lambda e_, o=PS[6][0:1, 0:128], i=SMALL[:, 64:65]:
                         e_.transpose(o, i, EYE[:, :]),
                         waits=[(c_vec, c_vec.count)] + psb[6].wr(), inc=c_pe)
            ev_g = p.add("vec", lambda e_, o=CCROW, i=PS[6][0:1, 0:128].rearrange("a (b c) -> a (b c)")
                         if False else PS[6][0:1, 0:128]:
                         e_.tensor_reduce(o[:, 0:1] if False else CCROW[:, 0:1], i,
                                          mybir.AxisListType.X, ALU.max),
                         waits=[ev_t], inc=c_vec)
            psb[6].frees.append(ev_g)
            ev_ccw = p.add("sy", lambda e_, o=cc1i[:, :], i=CCROW:
                           e_.dma_start(out=o, in_=i), waits=[ev_g], inc=c_cc)
            ev_cc1 = p.add("gp", lambda e_: e_.collective_compute(
                "AllReduce", ALU.max, replica_groups=[[0, 1, 2, 3, 4, 5, 6, 7]],
                ins=[cc1i[:, :]], outs=[cc1o[:, :]]),
                waits=[ev_ccw], inc=c_gp)
            ev_ccr = p.add("sy", lambda e_, o=CCROW, i=cc1o[:, :]:
                           e_.dma_start(out=o, in_=i),
                           waits=[ev_cc1] + ffn["ccrow_b"].wr(), inc=c_cc)
            ev_cl = p.add("sy", lambda e_, o=SMALL[0:1, 96:97],
                          i=clip_x[:].rearrange("(a b) -> a b", a=1):
                          e_.dma_start(out=o, in_=i), inc=c_cc)
            ev_sg = p.add("sca", lambda e_, o=SMALL[0:1, 97:98], i=SMALL[0:1, 96:97]:
                          e_.activation(out=o, in_=i, func=AF.Sigmoid),
                          waits=[ev_cl], inc=c_sca)
            ev_s1 = p.add("vec", lambda e_, o=SMALL[0:1, 98:99], a=SMALL[0:1, 97:98], b=CCROW[:, 0:1]:
                          e_.tensor_mul(o, a, b), waits=[ev_sg, ev_ccr], inc=c_vec)
            p.add("vec", lambda e_, o=SMALL[0:1, 98:99], i=SMALL[0:1, 98:99]:
                  e_.tensor_scalar_mul(o, i, 1.0 / QMAX))
            ev_rs1p = p.add("vec", lambda e_, o=SMALL[0:1, 99:100], i=SMALL[0:1, 98:99]:
                            e_.reciprocal(o, i), inc=c_vec)
            ev_onesf = p.add("vec", lambda e_: e_.memset(GUBSTG[0:1, :], 1.0),
                             waits=bar4, inc=c_vec)

            def bcast_col(src_row, dst_col, wt):
                evb = p.add("pe", lambda e_, o=PS[6][0:128, 0:1], l=GUBSTG[0:1, 0:128], r=src_row:
                            e_.matmul(o, l, r, start=True, stop=True),
                            waits=[wt, ev_onesf] + psb[6].wr(), inc=c_pe)
                evc2 = p.add("vec", lambda e_, o=dst_col, i=PS[6][0:128, 0:1]:
                             e_.tensor_copy(out=o, in_=i), waits=[evb], inc=c_vec)
                psb[6].frees.append(evc2)
                return evc2
            ev_s1c = bcast_col(SMALL[0:1, 98:99], SMALL[:, 67:68], ev_rs1p)
            ev_rs1c = bcast_col(SMALL[0:1, 99:100], SMALL[:, 68:69], ev_rs1p)
            dump(0, SMALL[:, 64:100], ev_rs1c)

            p.add("vec", lambda e_: e_.memset(SMALL[:, 65:66], 0.0), inc=c_vec)
            wg_ld, wu_ld = {}, {}

            def wg_load(sloti, it, extra=()):
                return p.add("sy", lambda e_, o=WGS[sloti], i=wqg_h[it]:
                             e_.dma_start(out=o, in_=i),
                             waits=ffn["wg_b"][sloti].wr() + list(extra), inc=c_wg[sloti])

            def wu_load(sloti, it, extra=()):
                return p.add("sy", lambda e_, o=WUS[sloti], i=wqu_h[it]:
                             e_.dma_start(out=o, in_=i),
                             waits=ffn["wu_b"][sloti].wr() + list(extra), inc=c_wu[sloti])

            for k3 in range(3):
                wg_ld[k3] = wg_load(k3, k3, extra=[ev_wqs_done])
                wu_ld[k3] = wu_load(k3, k3, extra=[ev_wqs_done])
            dump(7, WGS[0][:, 0:512], wg_ld[0], ffn["wg_b"][0])
            gq_pos = [0]
            n_hsp = 0
            for sc in range(NDT):
                t0 = sc * CH
                ev_x4 = emit_ln(t0, [], outs=XLN4, rstd_mul_col=SMALL[0:1, 68:69],
                                rstd_mul_ev=ev_rs1c)
                ev_xq = None
                for dt in range(NDT):
                    p.add("vec", lambda e_, o=XLN4[dt], i=XLN4[dt]:
                          e_.tensor_scalar(o, i, ROUND_C, -ROUND_C, ALU.add, ALU.add),
                          waits=([ev_x4] + ffn["xq_b"].wr()) if dt == 0 else [])
                    p.add("vec", lambda e_, o=XLN4[dt], i=XLN4[dt]:
                          e_.tensor_scalar(o, i, QMAX, -QMAX - 1.0, ALU.min, ALU.max))
                    ev_xq = p.add("vec", lambda e_, o=XQT[dt], i=XLN4[dt], s=SMALL[:, 67:68]:
                                  e_.tensor_scalar_mul(o, i, s),
                                  waits=[ev_s1c], inc=c_vec)
                    if sc == 0 and dt == 0:
                        dump(1, XQT[0], ev_xq, ffn["xq_b"])
                for it in range(NIT):
                    slg = gq_pos[0] % 3
                    evwg, evwu = wg_ld[slg], wu_ld[slg]
                    bkg = it % 2
                    bku = 2 + it % 2
                    wvg = psb[bkg].wr()
                    wvu = psb[bku].wr()
                    ev_mg = None
                    ev_mu = None
                    for dt in range(NDT):
                        ev_mg = p.add("pe", lambda e_, o=PS[bkg][0:128, 0:512],
                                      l=WGS[slg][:, dt * 128:(dt + 1) * 128], r=XQT[dt],
                                      s=(dt == 0), st=(dt == NDT - 1):
                                      e_.matmul(o, l, r, start=s, stop=st),
                                      waits=([evwg, ev_xq] + wvg) if dt == 0 else [],
                                      inc=(c_pe if dt == NDT - 1 else None))
                    ffn["wg_b"][slg].frees.append(ev_mg)
                    for dt in range(NDT):
                        ev_mu = p.add("pe", lambda e_, o=PS[bku][0:128, 0:512],
                                      l=WUS[slg][:, dt * 128:(dt + 1) * 128], r=XQT[dt],
                                      s=(dt == 0), st=(dt == NDT - 1):
                                      e_.matmul(o, l, r, start=s, stop=st),
                                      waits=([evwu, ev_xq] + wvu) if dt == 0 else [],
                                      inc=(c_pe if dt == NDT - 1 else None))
                    ffn["wu_b"][slg].frees.append(ev_mu)
                    tgt = gq_pos[0] + 3
                    gq_pos[0] += 1
                    if tgt < NIT * NDT:
                        wg_ld[tgt % 3] = wg_load(tgt % 3, tgt % NIT)
                        wu_ld[tgt % 3] = wu_load(tgt % 3, tgt % NIT)
                    ev_sg2 = p.add("sca", lambda e_, o=SGT, i=PS[bkg][0:128, 0:512]:
                                   e_.activation(out=o, in_=i, func=AF.Silu),
                                   waits=[ev_mg] + ffn["sg_b"].wr(), inc=c_sca)
                    psb[bkg].frees.append(ev_sg2)
                    ev_up = p.add("vec", lambda e_, o=UPT, i=PS[bku][0:128, 0:512]:
                                  e_.tensor_copy(out=o, in_=i),
                                  waits=[ev_mu] + ffn["up_b"].wr(), inc=c_vec)
                    psb[bku].frees.append(ev_up)
                    hsl = it % 2
                    ev_h = p.add("vec", lambda e_, o=HT4[hsl], a=SGT, b=UPT:
                                 e_.tensor_mul(o, a, b),
                                 waits=[ev_sg2, ev_up] + ffn["h_b"][hsl].wr(), inc=c_vec)
                    if sc == 0 and it == 0:
                        dump(2, SGT, ev_sg2, ffn["sg_b"])
                        dump(3, UPT, ev_up, ffn["up_b"])
                        dump(4, HT4[hsl], ev_h, ffn["h_b"][hsl])
                    ffn["sg_b"].frees.append(ev_h)
                    ffn["up_b"].frees.append(ev_h)
                    ev_hs = p.add("sy", lambda e_,
                                  o=hsp_h[it * 128:(it + 1) * 128, sc * 512:(sc + 1) * 512],
                                  i=HT4[hsl]:
                                  e_.dma_start(out=o, in_=i), waits=[ev_h], inc=c_hsp)
                    n_hsp += 1
                    ffn["h_b"][hsl].frees.append(ev_hs)
                    ev_mx = p.add("vec", lambda e_, o=SMALL[:, 66:67], i=HT4[hsl]:
                                  e_.tensor_reduce(o, i, mybir.AxisListType.X, ALU.max,
                                                   apply_absolute_value=True),
                                  waits=[ev_h], inc=c_vec)
                    ev_mh = p.add("vec", lambda e_, o=SMALL[:, 65:66], a=SMALL[:, 65:66],
                                  b=SMALL[:, 66:67]:
                                  e_.tensor_max(o, a, b), inc=c_vec)
                    ffn["h_b"][hsl].frees.append(ev_mh)
                if sc == NDT - 1:
                    ffn["xq_b"].frees.append((c_pe, c_pe.count))

            dump(8, SMALL[:, 64:72], (c_vec, c_vec.count))
            ev_t2 = p.add("pe", lambda e_, o=PS[6][0:1, 0:128], i=SMALL[:, 65:66]:
                          e_.transpose(o, i, EYE[:, :]),
                          waits=[(c_vec, c_vec.count)] + psb[6].wr(), inc=c_pe)
            ev_g2 = p.add("vec", lambda e_, o=CCROW[:, 0:1], i=PS[6][0:1, 0:128]:
                          e_.tensor_reduce(o, i, mybir.AxisListType.X, ALU.max),
                          waits=[ev_t2] + ffn["ccrow_b"].wr(), inc=c_vec)
            psb[6].frees.append(ev_g2)
            ev_ccw2 = p.add("sy", lambda e_, o=cc2i[:, :], i=CCROW:
                            e_.dma_start(out=o, in_=i), waits=[ev_g2], inc=c_cc)
            ev_cc2 = p.add("gp", lambda e_: e_.collective_compute(
                "AllReduce", ALU.max, replica_groups=[[0, 1, 2, 3, 4, 5, 6, 7]],
                ins=[cc2i[:, :]], outs=[cc2o[:, :]]),
                waits=[ev_ccw2], inc=c_gp)
            ev_ccr2 = p.add("sy", lambda e_, o=CCROW, i=cc2o[:, :]:
                            e_.dma_start(out=o, in_=i), waits=[ev_cc2], inc=c_cc)
            ev_cl2 = p.add("sy", lambda e_, o=SMALL[0:1, 96:97],
                           i=clip_down[:].rearrange("(a b) -> a b", a=1):
                           e_.dma_start(out=o, in_=i), waits=[ev_ccr2], inc=c_cc)
            ev_sg3 = p.add("sca", lambda e_, o=SMALL[0:1, 97:98], i=SMALL[0:1, 96:97]:
                           e_.activation(out=o, in_=i, func=AF.Sigmoid),
                           waits=[ev_cl2], inc=c_sca)
            ev_s2 = p.add("vec", lambda e_, o=SMALL[0:1, 98:99], a=SMALL[0:1, 97:98],
                          b=CCROW[:, 0:1]:
                          e_.tensor_mul(o, a, b), waits=[ev_sg3, ev_ccr2], inc=c_vec)
            p.add("vec", lambda e_, o=SMALL[0:1, 98:99], i=SMALL[0:1, 98:99]:
                  e_.tensor_scalar_mul(o, i, 1.0 / QMAX))
            ev_rs2p = p.add("vec", lambda e_, o=SMALL[0:1, 99:100], i=SMALL[0:1, 98:99]:
                            e_.reciprocal(o, i), inc=c_vec)
            ev_s2c = bcast_col(SMALL[0:1, 98:99], SMALL[:, 69:70], ev_rs2p)
            ev_rs2c = bcast_col(SMALL[0:1, 99:100], SMALL[:, 70:71], ev_rs2p)
            dump(5, SMALL[:, 64:100], ev_rs2c)
            ev_hsp_all = (c_hsp, c_hsp.count)

            wd_ld = {}

            def wd_load2(sloti, it, extra=()):
                return p.add("sy", lambda e_, o=WDS[sloti][:, 0:1024],
                             i=wqd_h[it * 128:(it + 1) * 128, :]:
                             e_.dma_start(out=o, in_=i),
                             waits=ffn["wd_b"][sloti].wr() + list(extra), inc=c_wd[sloti])

            for k3 in range(2):
                wd_ld[k3] = wd_load2(k3, k3, extra=[ev_wqs_done])
            hl_ld = {}
            for sc in range(NDT):
                ev_bank4 = [None] * NDT
                for it in range(NIT):
                    hsl = it % 2
                    ev_hl = p.add("sy", lambda e_, o=HT4[hsl],
                                  i=hsp_h[it * 128:(it + 1) * 128, sc * 512:(sc + 1) * 512]:
                                  e_.dma_start(out=o, in_=i),
                                  waits=[ev_hsp_all] + ffn["h_b"][hsl].wr(), inc=c_hld[hsl])
                    p.add("vec", lambda e_, o=AF32[:, 0:512], i=HT4[hsl], s=SMALL[:, 70:71]:
                          e_.tensor_scalar_mul(o, i, s),
                          waits=[ev_hl, ev_rs2c] + ffn["hq_b"][hsl].wr())
                    ffn["h_b"][hsl].frees.append((c_vec, c_vec.count))
                    p.add("vec", lambda e_, o=AF32[:, 0:512], i=AF32[:, 0:512]:
                          e_.tensor_scalar(o, i, ROUND_C, -ROUND_C, ALU.add, ALU.add))
                    ev_hq = p.add("vec", lambda e_, o=HQ4[hsl], i=AF32[:, 0:512]:
                                  e_.tensor_scalar(o, i, QMAX, -QMAX - 1.0, ALU.min, ALU.max),
                                  inc=c_vec)
                    if sc == 0 and it == 0:
                        dump(6, HQ4[hsl], ev_hq, ffn["hq_b"][hsl])
                    sld = it % 2
                    evwd = wd_ld[sld]
                    for ht in range(NDT):
                        ev_md = p.add("pe", lambda e_, o=PS[ht][0:128, 0:512],
                                      l=WDS[sld][:, ht * 128:(ht + 1) * 128], r=HQ4[hsl],
                                      s=(it == 0), st=(it == NIT - 1):
                                      e_.matmul(o, l, r, start=s, stop=st),
                                      waits=([evwd, ev_hq] + (psb[ht].wr() if it == 0 else [])),
                                      inc=c_pe)
                        if it == NIT - 1:
                            ev_bank4[ht] = ev_md
                        if ht == NDT - 1:
                            ffn["wd_b"][sld].frees.append(ev_md)
                            ffn["hq_b"][hsl].frees.append(ev_md)
                    nit2 = it + 2
                    if nit2 < NIT:
                        wd_ld[nit2 % 2] = wd_load2(nit2 % 2, nit2)
                    elif sc < NDT - 1:
                        wd_ld[(nit2 - NIT) % 2] = wd_load2((nit2 - NIT) % 2, nit2 - NIT)
                t0 = sc * CH
                for ht in range(NDT):
                    ev_ra = p.add("vec", lambda e_, o=RES[:, ht * TCORE + t0:][:, :512],
                                  a=PS[ht][0:128, 0:512], s=SMALL[:, 69:70],
                                  b=RES[:, ht * TCORE + t0:][:, :512]:
                                  e_.scalar_tensor_tensor(out=o, in0=a, scalar=s, in1=b,
                                                          op0=ALU.mult, op1=ALU.add),
                                  waits=[ev_bank4[ht], ev_s2c], inc=c_vec)
                    psb[ht].frees.append(ev_ra)

        OSTG = [AF32[:, 0:1024], AF32[:, 1024:2048]]
        ostg_b = [Buf(), Buf()]
        n_out = 0
        last_vec_ev = (c_vec, c_vec.count)
        for seq in range(SEQN):
            for tt in range(S // 128):
                sl = (seq * (S // 128) + tt) % 2
                evlast = None
                for dt in range(NDT):
                    evc0 = p.add("vec", lambda e_, o=OSTG[sl][:, dt * 128:(dt + 1) * 128],
                                 i=RES[:, dt * TCORE + seq * S + tt * 128:][:, :128]:
                                 e_.tensor_copy(out=o, in_=i),
                                 waits=[last_vec_ev] + (ostg_b[sl].wr() if dt == 0 else []),
                                 inc=c_vec)
                    evt, evc = transpose_128(OSTG[sl][:, dt * 128:(dt + 1) * 128],
                                             OSTG[sl][:, dt * 128:(dt + 1) * 128],
                                             tcount % 2, [evc0])
                    tcount += 1
                    evlast = evc
                evd = p.add("sca", lambda e_, o=y_out[seq, tt * 128:(tt + 1) * 128, :], i=OSTG[sl]:
                            e_.dma_start(out=o, in_=i), waits=[evlast], inc=c_out[sl])
                n_out += 1
                ostg_b[sl].frees.append(evd)
        p.add("sy", lambda e_: e_.nop(),
              waits=[(c_out[0], c_out[0].count), (c_out[1], c_out[1].count)])

        with nc.Block() as blk:
            blk.tensor(lambda e_: p.emit("pe", e_))
            blk.vector(lambda e_: p.emit("vec", e_))
            blk.scalar(lambda e_: p.emit("sca", e_))
            blk.gpsimd(lambda e_: p.emit("gp", e_))
            blk.sync(lambda e_: p.emit("sy", e_))
    return nc, p.tally()


def _constants():
    eye = np.eye(128, dtype=np.float32)
    ones = np.ones((128, 128), dtype=BF)
    r = np.arange(128)[:, None]
    x = np.arange(896)[None, :]
    ptriu = (x >= r + 384).astype(BF)
    eh = np.zeros((NHEAD, D), dtype=BF)
    for f in range(D):
        eh[f // HDIM, f] = 1
    return eye, ones, ptriu, eh


def kernel(**inputs):
    x = np.asarray(inputs["L0_attn_x"], dtype=np.float32)
    assert x.shape == (B, S, D)
    eye, ones, ptriu, eh = _constants()
    weight_names = [k for k in inputs.keys() if k != "L0_attn_x"]
    nc, tally = build_program(_BLOCKS)
    in_maps = []
    for core in range(NCORE):
        m = {"x": np.ascontiguousarray(x[core * SEQN:(core + 1) * SEQN]),
             "c_eye": eye, "c_ones": ones, "c_ptriu": ptriu, "c_eh": eh}
        for k in weight_names:
            m[k] = np.ascontiguousarray(np.asarray(inputs[k], dtype=np.float32))
        in_maps.append(m)
    res = run_bass_kernel_spmd(nc, in_maps, core_ids=list(range(NCORE)))
    out = np.concatenate([r["y"].reshape(SEQN, S, D) for r in res.results], axis=0)
    return out.astype(np.float32)


# version 04
# speedup vs baseline: 1.0958x; 1.0958x over previous
import contextlib
import numpy as np

try:
    import concourse.bass as bass
except ImportError:
    import sys
    sys.path.insert(0, "/opt/trn_rl_repo")
    import concourse.bass as bass
import concourse.mybir as mybir
import ml_dtypes
from concourse.bass_utils import run_bass_kernel_spmd
from concourse.alu_op_type import AluOpType as ALU

F32 = mybir.dt.float32
BF16 = mybir.dt.bfloat16
AF = mybir.ActivationFunctionType

BF = ml_dtypes.bfloat16

NCORE = 8
B, S, D = 16, 2048, 1024
SEQN = 2
TCORE = SEQN * S
NDT = D // 128
CH = 512
NCH_SEQ = S // CH
INNER = 2048
NCT = INNER // 128
NST = 16
KCONV = 4
EXP_N = 8
IMOE = 1024
NHEAD = 16
HDIM = 64
IFFN = 2816
NIT = IFFN // 128
ROUND_C = float(3 * (1 << 22))
QMAX = 127.0

ENGS = ("pe", "vec", "sca", "gp", "sy")

_BLOCKS = 4
_LIMIT = None
_LIMIT2 = None
_SKIP_P1 = False
_SKIP_P2 = False
_LIMIT3 = None
_SKIP_P3 = False
_DEBUG = False


class Chan:
    __slots__ = ("name", "step", "count", "h", "engs", "is_dma", "unbarred")
    def __init__(self, name, step=1, is_dma=False):
        self.name, self.step, self.count = name, step, 0
        self.h = None
        self.engs = set()
        self.is_dma = is_dma
        self.unbarred = set()


class Buf:
    __slots__ = ("ready", "frees")
    def __init__(self):
        self.ready = None
        self.frees = []
    def wr(self):
        w = list(self.frees)
        self.frees = []
        return w


class Prog:
    def __init__(self):
        self.ops = {e: [] for e in ENGS}
        self.chans = []

    def chan(self, name, step=1, is_dma=False):
        c = Chan(name, step, is_dma)
        self.chans.append(c)
        return c

    def add(self, eng, fn, waits=(), inc=None, bar=False, nobar=False):
        ws = [w for w in waits if w is not None]
        if eng in ("vec", "sca") and not nobar:
            bar = True
        if bar and inc is None:
            inc = self.engine_chan[eng]
        ev = None
        if inc is not None:
            inc.count += inc.step
            inc.engs.add(eng)
            ev = (inc, inc.count)
            if nobar:
                inc.unbarred.add(inc.count)
        self.ops[eng].append((ws, fn, inc, bar, None if ev is None else ev[1]))
        return ev

    def emit(self, eng_name, eng):
        satisfied = {}
        pending_bar = None
        for ws, fn, inc, bar, val in self.ops[eng_name]:
            if pending_bar is not None:
                c, v = pending_bar
                if satisfied.get(c, -1) < v:
                    eng.wait_ge(c.h, v)
                    satisfied[c] = v
                pending_bar = None
            for (c, v) in ws:
                if not c.is_dma and c.engs == {eng_name} and v not in c.unbarred:
                    continue
                if satisfied.get(c, -1) >= v:
                    continue
                eng.wait_ge(c.h, v)
                satisfied[c] = v
            inst = fn(eng)
            if inc is not None:
                assert inst is not None
                inst.then_inc(inc.h, inc.step)
            if bar:
                assert inc is not None
                pending_bar = (inc, val)

    def tally(self):
        return {e: len(self.ops[e]) for e in ENGS}


class Carver:
    def __init__(self, total):
        self.off, self.total = 0, total
    def take(self, n, align=8):
        a = (self.off + align - 1) // align * align
        self.off = a + n
        assert self.off <= self.total, f"carve overflow {self.off}>{self.total}"
        return a


def build_program(n_blocks):
    nc = bass.Bass("TRN2", target_bir_lowering=False, debug=False,
                   detect_race_conditions=False)
    p = Prog()

    c_pe = p.chan("pe")
    c_vec = p.chan("vec")
    c_sca = p.chan("sca")
    c_gp = p.chan("gp")
    p.engine_chan = {"pe": c_pe, "vec": c_vec, "sca": c_sca, "gp": c_gp, "sy": None}
    c_dg = p.chan("dg", 16, True)
    c_stg = [p.chan(f"stg{i}", 16, True) for i in range(2)]
    c_wost = p.chan("wost", 16, True)
    c_misc0 = p.chan("m0", 16, True)
    c_wot = [p.chan(f"wot{i}", 16, True) for i in range(2)]
    c_ring = [p.chan(f"rng{i}", 16, True) for i in range(4)]
    c_gub = p.chan("gub", 16, True)
    c_wrow = p.chan("wrow", 16, True)
    c_misc2 = p.chan("m2", 16, True)
    c_out = [p.chan(f"out{i}", 16, True) for i in range(2)]
    c_pre = p.chan("pre", 16, True)

    x_in = nc.dram_tensor("x", [SEQN, S, D], F32, kind="ExternalInput").ap()
    y_out = nc.dram_tensor("y", [SEQN, S, D], F32, kind="ExternalOutput").ap()

    def din(name, shape):
        return nc.dram_tensor(name, shape, F32, kind="ExternalInput").ap()

    w_in = din("L0_attn_W_in", [2 * INNER, D])
    conv_w = din("L0_attn_conv_w", [INNER, 1, KCONV])
    conv_b = din("L0_attn_conv_b", [INNER])
    a_log = din("L0_attn_A_log", [INNER, NST])
    d_vec = din("L0_attn_D", [INNER])
    w_out = din("L0_attn_W_out", [D, INNER])
    rw = din("L0_ffn_router_weight", [EXP_N, D])
    rb = din("L0_ffn_router_bias", [EXP_N])
    gup = din("L0_ffn_gate_up_proj", [EXP_N, D, 2 * IMOE])
    gub = din("L0_ffn_gate_up_bias", [EXP_N, 2 * IMOE])
    dwn = din("L0_ffn_down_proj", [EXP_N, IMOE, D])
    dwb = din("L0_ffn_down_bias", [EXP_N, D])
    wqk = din("L1_attn_Wqk", [D, 2 * D])
    bqk = din("L1_attn_bqk", [2 * D])
    wv = din("L1_attn_Wv", [D, D])
    bv = din("L1_attn_bv", [D])
    wn = din("L1_attn_Wn", [D, NHEAD])
    bn = din("L1_attn_bn", [NHEAD])
    wo = din("L1_attn_Wo", [D, D])
    bo = din("L1_attn_bo", [D])
    clip_x = din("L1_ffn_clip_x", [1])
    clip_down = din("L1_ffn_clip_down", [1])
    w_gate = din("L1_ffn_w_gate", [IFFN, D])
    w_up = din("L1_ffn_w_up", [IFFN, D])
    w_down = din("L1_ffn_w_down", [D, IFFN])

    c_eye = nc.dram_tensor("c_eye", [128, 128], F32, kind="ExternalInput").ap()
    c_ptriu = nc.dram_tensor("c_ptriu", [128, 896], BF16, kind="ExternalInput").ap()
    c_eh = nc.dram_tensor("c_eh", [NHEAD, D], BF16, kind="ExternalInput").ap()
    c_ones = nc.dram_tensor("c_ones", [128, 128], BF16, kind="ExternalInput").ap()

    woutt_h = nc.dram_tensor("woutt_h", [INNER, D], BF16, kind="Internal").ap()
    wqg_h = nc.dram_tensor("wqg_h", [NIT, 128, NDT * 128], BF16, kind="Internal").ap()
    wqu_h = nc.dram_tensor("wqu_h", [NIT, 128, NDT * 128], BF16, kind="Internal").ap()
    wqkb_h = nc.dram_tensor("wqkb_h", [D, 2 * D], BF16, kind="Internal").ap()
    wvb_h = nc.dram_tensor("wvb_h", [D, D], BF16, kind="Internal").ap()
    wob_h = nc.dram_tensor("wob_h", [D, D], BF16, kind="Internal").ap()
    gupb_h = nc.dram_tensor("gupb_h", [EXP_N, D, 2 * IMOE], BF16, kind="Internal").ap()
    dwnb_h = nc.dram_tensor("dwnb_h", [EXP_N, IMOE, D], BF16, kind="Internal").ap()
    wqd_h = nc.dram_tensor("wqd_h", [IFFN, D], BF16, kind="Internal").ap()
    hsp_h = nc.dram_tensor("hsp_h", [IFFN, TCORE], BF16, kind="Internal").ap()
    cc1i = nc.dram_tensor("cc1i", [1, 16], F32, kind="Internal").ap()
    cc1o = nc.dram_tensor("cc1o", [1, 16], F32, kind="Internal", addr_space="Shared").ap()
    cc2i = nc.dram_tensor("cc2i", [1, 16], F32, kind="Internal").ap()
    cc2o = nc.dram_tensor("cc2o", [1, 16], F32, kind="Internal", addr_space="Shared").ap()
    dbg = (nc.dram_tensor("dbg", [24, 128, 512], F32, kind="ExternalOutput").ap()
           if _DEBUG else None)

    with contextlib.ExitStack() as st:
        e = st.enter_context
        RES = e(nc.sbuf_tensor("res", [128, NDT * TCORE], BF16))
        AW = e(nc.sbuf_tensor("aw", [128, 32768], BF16))
        ABF = e(nc.sbuf_tensor("abf", [128, 18944], BF16))
        AF32 = e(nc.sbuf_tensor("af32", [128, 4104], F32))
        STATS = e(nc.sbuf_tensor("stats", [128, 2936], F32))
        LNBF = e(nc.sbuf_tensor("lnbf", [128, 1024], BF16))
        EYE = e(nc.sbuf_tensor("eye", [128, 128], F32))
        ONESB = e(nc.sbuf_tensor("onesb", [128, 128], BF16))
        PTRIU = e(nc.sbuf_tensor("ptriu", [128, 896], BF16))
        EHB = e(nc.sbuf_tensor("ehb", [NHEAD, D], BF16))
        SMALL = e(nc.sbuf_tensor("small", [128, 128], F32))
        DBB = e(nc.sbuf_tensor("dbb", [EXP_N, IMOE], BF16))
        GUBSTG = e(nc.sbuf_tensor("gubstg", [16, 128], F32))
        WTB = e(nc.sbuf_tensor("wtb", [EXP_N, 512], BF16))
        RWTB = e(nc.sbuf_tensor("rwtb", [128, 64], BF16))
        RBROW = e(nc.sbuf_tensor("rbrow", [1, 8], BF16))
        WROW = e(nc.sbuf_tensor("wrow", [1, 512], BF16))
        RNT = e(nc.sbuf_tensor("rnt", [16, 512], BF16))
        WNB = e(nc.sbuf_tensor("wnb", [128, 128], BF16))
        BNROW = e(nc.sbuf_tensor("bnrow", [1, 16], BF16))
        BVROW = e(nc.sbuf_tensor("bvrow", [1, 1024], BF16))

        PS = [e(nc.psum_tensor(f"ps{i}", [128, 512], F32)) for i in range(8)]
        psb = [Buf() for _ in range(8)]

        for c in p.chans:
            c.h = e(nc.semaphore(c.name))

        def dump(slot, ap, ev, buf=None):
            if not _DEBUG:
                return
            pp, ff = ap.shape[0], ap.shape[1]
            evd = p.add("gp", lambda e_, o=dbg[slot, 0:pp, 0:ff], i=ap:
                        e_.dma_start(out=o, in_=i), waits=[ev], inc=c_dg)
            if buf is not None:
                buf.frees.append(evd)

        O_ANEG, O_D, O_CW, O_CB = 0, 256, 272, 336
        O_HST = 352
        LNR = [888 + i * 512 for i in range(4)]
        ln_mean_b = LNBF[0:1, 0:512]
        ln_rstd_b = LNBF[0:1, 512:1024]
        O_TAIL = 16

        p.add("sy", lambda e_: e_.dma_start(out=EYE[:, :], in_=c_eye[:, :]), inc=c_misc0)
        p.add("sy", lambda e_: e_.dma_start(out=ONESB[:, :], in_=c_ones[:, :]), inc=c_misc0)

        for e2 in range(EXP_N):
            p.add("gp", lambda e_, o=gupb_h[e2], i=gup[e2]:
                  e_.dma_start(out=o, in_=i), inc=c_pre)
            p.add("gp", lambda e_, o=dwnb_h[e2], i=dwn[e2]:
                  e_.dma_start(out=o, in_=i), inc=c_pre)
        p.add("gp", lambda e_: e_.dma_start(out=wqkb_h[:, :], in_=wqk[:, :]), inc=c_pre)
        p.add("gp", lambda e_: e_.dma_start(out=wvb_h[:, :], in_=wv[:, :]), inc=c_pre)
        p.add("gp", lambda e_: e_.dma_start(out=wob_h[:, :], in_=wo[:, :]), inc=c_pre)
        ev_pre_all = (c_pre, c_pre.count)
        p.add("sy", lambda e_: e_.dma_start(out=PTRIU[:, :], in_=c_ptriu[:, :]), inc=c_misc0)
        p.add("sy", lambda e_: e_.dma_start(out=EHB[:, :], in_=c_eh[:, :]), inc=c_misc0)

        for ct in range(NCT):
            p.add("sy", lambda e_, o=STATS[:, O_CW + ct * 4: O_CW + ct * 4 + 4],
                  i=conv_w[ct * 128:(ct + 1) * 128, 0, :]:
                  e_.dma_start(out=o, in_=i), inc=c_misc0)
            p.add("sy", lambda e_, o=STATS[:, O_CB + ct: O_CB + ct + 1],
                  i=conv_b[ct * 128:(ct + 1) * 128].rearrange("(p one) -> p one", one=1):
                  e_.dma_start(out=o, in_=i), inc=c_misc0)
            p.add("sy", lambda e_, o=STATS[:, O_D + ct: O_D + ct + 1],
                  i=d_vec[ct * 128:(ct + 1) * 128].rearrange("(p one) -> p one", one=1):
                  e_.dma_start(out=o, in_=i), inc=c_misc0)
            p.add("sy", lambda e_, o=STATS[:, O_ANEG + ct * NST: O_ANEG + (ct + 1) * NST],
                  i=a_log[ct * 128:(ct + 1) * 128, :]:
                  e_.dma_start(out=o, in_=i), inc=c_misc0)
        ev_misc0 = (c_misc0, c_misc0.count)
        ev_eye = ev_ones = ev_cw = ev_misc0
        ev_aneg = None
        for ct in range(NCT):
            evx = p.add("sca", lambda e_, o=SMALL[:, 0:NST],
                        i=STATS[:, O_ANEG + ct * NST: O_ANEG + (ct + 1) * NST]:
                        e_.activation(out=o, in_=i, func=AF.Exp),
                        waits=[ev_misc0, ev_aneg], inc=c_sca)
            ev_aneg = p.add("vec", lambda e_, o=STATS[:, O_ANEG + ct * NST: O_ANEG + (ct + 1) * NST],
                            i=SMALL[:, 0:NST]:
                            e_.tensor_scalar_mul(o, i, -1.0), waits=[evx], inc=c_vec)

        STG = [AF32[:, 0:1024], AF32[:, 1024:2048]]
        stg_b = [Buf(), Buf()]

        def transpose_128(src_ap, dst_ap, pslot, waits, cast_buf=None):
            pst = PS[6 + pslot]
            evt = p.add("pe", lambda e_, o=pst[0:128, 0:128], i=src_ap:
                        e_.transpose(o, i, EYE[:, :]),
                        waits=list(waits) + psb[6 + pslot].wr() + [ev_misc0], inc=c_pe)
            evc = p.add("vec", lambda e_, o=dst_ap, i=pst[0:128, 0:128]:
                        e_.tensor_copy(out=o, in_=i), waits=[evt], inc=c_vec, bar=True)
            psb[6 + pslot].frees.append(evc)
            return evt, evc

        tcount = 0
        for seq in range(SEQN):
            for tt in range(S // 128):
                sl = (seq * (S // 128) + tt) % 2
                evl = p.add("sy", lambda e_, o=STG[sl], i=x_in[seq, tt * 128:(tt + 1) * 128, :]:
                            e_.dma_start(out=o, in_=i),
                            waits=stg_b[sl].wr(), inc=c_stg[sl])
                for dt in range(NDT):
                    evt, evc = transpose_128(
                        STG[sl][:, dt * 128:(dt + 1) * 128],
                        RES[:, dt * TCORE + seq * S + tt * 128:][:, :128],
                        tcount % 2, [evl])
                    tcount += 1
                    if dt == NDT - 1:
                        stg_b[sl].frees.append(evt)
        ev_res_ready = (c_vec, c_vec.count)

        ev_winT = None
        for ct2 in range(2 * NCT):
            sl = ct2 % 2
            evl = p.add("sy", lambda e_, o=STG[sl], i=w_in[ct2 * 128:(ct2 + 1) * 128, :]:
                        e_.dma_start(out=o, in_=i), waits=stg_b[sl].wr(), inc=c_stg[sl])
            for dt in range(NDT):
                evt, evc = transpose_128(
                    STG[sl][:, dt * 128:(dt + 1) * 128],
                    AW[:, dt * 4096 + ct2 * 128:][:, :128],
                    tcount % 2, [evl])
                tcount += 1
                if dt == NDT - 1:
                    stg_b[sl].frees.append(evt)
                ev_winT = evc

        WOTMP = ABF[:, 0:128]
        wotmp_b = Buf()
        ev_woutt = None
        for dt in range(NDT):
            for half in range(2):
                sl = (dt * 2 + half) % 2
                evl = p.add("sy", lambda e_, o=STG[sl],
                            i=w_out[dt * 128:(dt + 1) * 128, half * 1024:(half + 1) * 1024]:
                            e_.dma_start(out=o, in_=i), waits=stg_b[sl].wr(), inc=c_stg[sl])
                for j in range(8):
                    ct2 = half * 8 + j
                    evt, evc = transpose_128(
                        STG[sl][:, j * 128:(j + 1) * 128], WOTMP,
                        tcount % 2, [evl] + wotmp_b.wr())
                    tcount += 1
                    if j == 7:
                        stg_b[sl].frees.append(evt)
                    ev_woutt = p.add("sy", lambda e_,
                                     o=woutt_h[ct2 * 128:(ct2 + 1) * 128, dt * 128:(dt + 1) * 128],
                                     i=WOTMP:
                                     e_.dma_start(out=o, in_=i), waits=[evc], inc=c_wost)
                    wotmp_b.frees.append(ev_woutt)

        cbf = Carver(18944)
        XLN_OFF = cbf.take(8 * CH)
        YF_OFF = cbf.take(NCT * CH)
        HT_OFF = cbf.take(2 * CH)
        TREE_OFF = cbf.take(5 * CH)
        YACC_OFF = cbf.take(CH)
        RSIL_OFF = cbf.take(CH)
        WOT_OFF = cbf.take(2 * 1024)

        XLNT = [ABF[:, XLN_OFF + dt * CH:XLN_OFF + (dt + 1) * CH] for dt in range(NDT)]
        TREE = [ABF[:, TREE_OFF + i * CH:TREE_OFF + (i + 1) * CH] for i in range(5)]
        MB, RBB, TMPB = TREE[0], TREE[1], TREE[2]
        SQ = [ABF[:, YF_OFF + i * CH:YF_OFF + (i + 1) * CH] for i in range(2)]
        HT = [ABF[:, HT_OFF + i * CH:HT_OFF + (i + 1) * CH] for i in range(2)]
        YACC = ABF[:, YACC_OFF:YACC_OFF + CH]
        RSIL = ABF[:, RSIL_OFF:RSIL_OFF + CH]
        WOT = [ABF[:, WOT_OFF + i * 1024:WOT_OFF + (i + 1) * 1024] for i in range(2)]

        caf = Carver(4104)
        caf.off = 2048
        XS_OFF = caf.take(520)
        CACC_OFF = caf.take(512)
        AT_OFF = caf.take(2 * 512)
        XS = AF32[:, XS_OFF:XS_OFF + 515]
        CACC = AF32[:, CACC_OFF:CACC_OFF + 512]
        ATIL = [AF32[:, AT_OFF + i * 512:AT_OFF + (i + 1) * 512] for i in range(2)]
        XCONV = AF32[:, 0:512]
        DELTA = AF32[:, 512:1024]
        BX = AF32[:, 1024:1536]

        sq_b = [Buf(), Buf()]
        atil_b = [Buf(), Buf()]
        ht_b = [Buf(), Buf()]
        tree_b = [Buf() for _ in range(5)]
        yacc_b = Buf()
        rsil_b = Buf()
        cacc_b = Buf()
        xconv_b = Buf()
        xs_b = Buf()
        yf_b = [Buf() for _ in range(NCT)]
        wot_b = [Buf(), Buf()]
        xln_b = Buf()

        def emit_ln(t0, res_ready_evs, rstd_mul_col=None, rstd_mul_ev=None, outs=None):
            bsum, bsq = 0, 1
            w0 = psb[bsum].wr() + res_ready_evs
            ev_s1 = None
            for dt in range(NDT):
                ev_s1 = p.add("pe", lambda e_, o=PS[bsum][0:1, 0:512],
                      l=ONESB[:, 0:1], r=RES[:, dt * TCORE + t0:][:, :CH], s=(dt == 0),
                      st=(dt == NDT - 1):
                      e_.matmul(o, l, r, start=s, stop=st),
                      waits=(w0 + [ev_ones]) if dt == 0 else [],
                      inc=(c_pe if dt == NDT - 1 else None))
            w1 = psb[bsq].wr()
            ev_s2 = None
            for dt in range(NDT):
                evq = p.add("sca", lambda e_, o=SQ[dt % 2], i=RES[:, dt * TCORE + t0:][:, :CH]:
                            e_.activation(out=o, in_=i, func=AF.Square),
                            waits=sq_b[dt % 2].wr() + res_ready_evs, inc=c_sca)
                ev_s2 = p.add("pe", lambda e_, o=PS[bsq][0:1, 0:512], l=ONESB[:, 0:1], r=SQ[dt % 2],
                              s=(dt == 0), st=(dt == NDT - 1):
                              e_.matmul(o, l, r, start=s, stop=st),
                              waits=[evq] + (w1 if dt == 0 else []), inc=c_pe)
                sq_b[dt % 2].frees.append(ev_s2)
            R0, R1, R2, R3 = (STATS[0:1, LNR[i]:LNR[i] + 512] for i in range(4))
            p.add("vec", lambda e_, o=R0, i=PS[bsum][0:1, 0:512]:
                  e_.tensor_scalar_mul(o, i, -1.0 / D), waits=[ev_s1], bar=True)
            p.add("vec", lambda e_, o=R1, a=R0: e_.tensor_mul(o, a, a))
            p.add("vec", lambda e_, o=R2, i=PS[bsq][0:1, 0:512], m=R1:
                  e_.scalar_tensor_tensor(out=o, in0=i, scalar=1.0 / D, in1=m,
                                          op0=ALU.mult, op1=ALU.subtract), waits=[ev_s2], bar=True)
            ev_fr = p.add("vec", lambda e_, o=R3, i=R2:
                          e_.tensor_scalar_add(o, i, 1e-5), inc=c_vec)
            psb[bsum].frees.append(ev_fr)
            psb[bsq].frees.append(ev_fr)
            ev_sq = p.add("sca", lambda e_, o=R1, i=R3:
                          e_.activation(out=o, in_=i, func=AF.Sqrt), waits=[ev_fr], inc=c_sca)
            p.add("vec", lambda e_, o=R2, i=R1: e_.reciprocal(o, i), waits=[ev_sq], bar=True)
            p.add("vec", lambda e_, o=R1, a=R2: e_.tensor_mul(o, a, a))
            p.add("vec", lambda e_, o=R1, a=R3, b=R1: e_.tensor_mul(o, a, b))
            p.add("vec", lambda e_, o=R1, i=R1:
                  e_.tensor_scalar(o, i, -0.5, 1.5, ALU.mult, ALU.add))
            p.add("vec", lambda e_, o=R1, a=R2, b=R1: e_.tensor_mul(o, a, b))
            if rstd_mul_col is not None:
                p.add("vec", lambda e_, o=R1, i=R1, s=rstd_mul_col:
                      e_.tensor_scalar_mul(o, i, s), waits=[rstd_mul_ev])
            ev_nm = p.add("vec", lambda e_, o=ln_mean_b, i=R0:
                          e_.tensor_copy(out=o, in_=i), inc=c_vec)
            ev_rs = p.add("vec", lambda e_, o=ln_rstd_b, i=R1:
                          e_.tensor_copy(out=o, in_=i), inc=c_vec)
            bm, br = 2, 3
            evbm = p.add("pe", lambda e_, o=PS[bm][0:128, 0:512], l=ONESB[0:1, :], r=ln_mean_b:
                         e_.matmul(o, l, r, start=True, stop=True),
                         waits=[ev_nm] + psb[bm].wr(), inc=c_pe)
            evbr = p.add("pe", lambda e_, o=PS[br][0:128, 0:512], l=ONESB[0:1, :], r=ln_rstd_b:
                         e_.matmul(o, l, r, start=True, stop=True),
                         waits=[ev_rs] + psb[br].wr(), inc=c_pe)
            evMB = p.add("vec", lambda e_, o=MB, i=PS[bm][0:128, 0:512]:
                         e_.tensor_copy(out=o, in_=i), waits=[evbm] + tree_b[0].wr(), inc=c_vec,
                         bar=True)
            evRB = p.add("vec", lambda e_, o=RBB, i=PS[br][0:128, 0:512]:
                         e_.tensor_copy(out=o, in_=i), waits=[evbr] + tree_b[1].wr(), inc=c_vec,
                         bar=True)
            psb[bm].frees.append(evMB)
            psb[br].frees.append(evRB)
            ev_last = None
            otiles = outs if outs is not None else XLNT
            for dt in range(NDT):
                p.add("vec", lambda e_, o=TMPB, a=RES[:, dt * TCORE + t0:][:, :CH], b=MB:
                      e_.tensor_add(o, a, b),
                      waits=([evMB, evRB] + tree_b[2].wr()) if dt == 0 else [])
                ev_last = p.add("vec", lambda e_, o=otiles[dt], a=TMPB, b=RBB:
                                e_.tensor_mul(o, a, b),
                                inc=(c_vec if dt == NDT - 1 else None))
            tree_b[0].frees.append(ev_last)
            tree_b[1].frees.append(ev_last)
            tree_b[2].frees.append(ev_last)
            return ev_last

        carry_ev = [[None] * NCT for _ in range(SEQN)]
        tail_ev = [None] * NCT

        ev_hst0 = p.add("vec", lambda e_: e_.memset(STATS[:, O_HST:O_HST + SEQN * NCT * NST], 0.0),
                        inc=c_vec)

        LSEQ, LCHK, LCT = _LIMIT if _LIMIT is not None else (SEQN, NCH_SEQ, NCT)
        if _SKIP_P1:
            LSEQ = 0
        if n_blocks >= 1:
            for seq in range(LSEQ):
                ev_tz = p.add("vec", lambda e_: e_.memset(SMALL[:, O_TAIL:O_TAIL + NCT * 3], 0.0),
                              inc=c_vec)
                for ct in range(NCT):
                    tail_ev[ct] = ev_tz
                for chk in range(LCHK):
                    t0 = seq * S + chk * CH
                    rdy = [ev_res_ready] if (seq == 0 and chk == 0) else []
                    ev_xln = emit_ln(t0, rdy + xln_b.wr())

                    wot_ld = [None] * LCT
                    for ct in range(min(2, LCT)):
                        sl = ct % 2
                        wot_ld[ct] = p.add("sy",
                                           lambda e_, o=WOT[sl], i=woutt_h[ct * 128:(ct + 1) * 128, :]:
                                           e_.dma_start(out=o, in_=i),
                                           waits=wot_b[sl].wr() + [ev_woutt], inc=c_wot[sl])

                    for ct in range(LCT):
                        bx_m = ct % 2
                        bx_r = 4 + ct % 2
                        ev_mmx = None
                        for dt in range(NDT):
                            ev_mmx = p.add("pe", lambda e_, o=PS[bx_m][0:128, 0:512],
                                  l=AW[:, dt * 4096 + ct * 128:][:, :128], r=XLNT[dt], s=(dt == 0),
                                  st=(dt == NDT - 1):
                                  e_.matmul(o, l, r, start=s, stop=st),
                                  waits=([ev_xln, ev_winT] + psb[bx_m].wr()) if dt == 0 else [],
                                  inc=(c_pe if dt == NDT - 1 else None))
                        ev_mmr = None
                        for dt in range(NDT):
                            ev_mmr = p.add("pe", lambda e_, o=PS[bx_r][0:128, 0:512],
                                  l=AW[:, dt * 4096 + (NCT + ct) * 128:][:, :128], r=XLNT[dt],
                                  s=(dt == 0), st=(dt == NDT - 1):
                                  e_.matmul(o, l, r, start=s, stop=st),
                                  waits=psb[bx_r].wr() if dt == 0 else [],
                                  inc=(c_pe if dt == NDT - 1 else None))

                        ev_xs = p.add("vec", lambda e_, o=XS[:, 3:515], i=PS[bx_m][0:128, 0:512]:
                                      e_.tensor_copy(out=o, in_=i),
                                      waits=[ev_mmx] + xs_b.wr(), inc=c_vec, bar=True)
                        if seq == 0 and chk == 2 and ct == 0:
                            dump(18, XS[:, 3:515], ev_xs, xs_b)
                        psb[bx_m].frees.append(ev_xs)
                        p.add("vec", lambda e_, o=XS[:, 0:3],
                              i=SMALL[:, O_TAIL + ct * 3:O_TAIL + ct * 3 + 3]:
                              e_.tensor_copy(out=o, in_=i), waits=[tail_ev[ct]])
                        p.add("vec", lambda e_, o=CACC, i=XS[:, 0:512],
                              s=STATS[:, O_CW + ct * 4:O_CW + ct * 4 + 1]:
                              e_.tensor_scalar_mul(o, i, s),
                              waits=cacc_b.wr() + [ev_cw])
                        for k in (1, 2, 3):
                            p.add("vec", lambda e_, o=CACC, i=XS[:, k:k + 512],
                                  s=STATS[:, O_CW + ct * 4 + k:O_CW + ct * 4 + k + 1], m=CACC:
                                  e_.scalar_tensor_tensor(out=o, in0=i, scalar=s, in1=m,
                                                          op0=ALU.mult, op1=ALU.add),
                                  inc=(c_vec if k == 3 else None))
                        ev_conv = (c_vec, c_vec.count)
                        tail_ev[ct] = p.add("vec", lambda e_,
                                            o=SMALL[:, O_TAIL + ct * 3:O_TAIL + ct * 3 + 3],
                                            i=XS[:, 512:515]:
                                            e_.tensor_copy(out=o, in_=i), inc=c_vec)

                        ev_xc = p.add("sca", lambda e_, o=XCONV, i=CACC,
                                      b=STATS[:, O_CB + ct:O_CB + ct + 1]:
                                      e_.activation(out=o, in_=i, func=AF.Silu, bias=b),
                                      waits=[ev_conv] + xconv_b.wr(), inc=c_sca)
                        if seq == 0 and chk == 2 and ct == 0:
                            dump(19, XCONV, ev_xc, xconv_b)
                        ev_rsl = p.add("sca", lambda e_, o=RSIL, i=PS[bx_r][0:128, 0:512]:
                                       e_.activation(out=o, in_=i, func=AF.Silu),
                                       waits=[ev_mmr] + rsil_b.wr(), inc=c_sca, bar=True)
                        psb[bx_r].frees.append(ev_rsl)
                        SPD = XS[:, 0:512]
                        ev_ab = p.add("sca", lambda e_, o=SPD, i=XCONV:
                                      e_.activation(out=o, in_=i, func=AF.Abs),
                                      waits=[ev_conv], inc=c_sca)
                        ev_u = p.add("sca", lambda e_, o=DELTA, i=SPD:
                                     e_.activation(out=o, in_=i, func=AF.Exp, scale=-1.0),
                                     waits=[ev_ab], inc=c_sca, nobar=True)
                        ev_w = p.add("vec", lambda e_, o=SPD, i=DELTA:
                                     e_.tensor_scalar_add(o, i, 1.0), waits=[ev_u], inc=c_vec)
                        ev_l = p.add("sca", lambda e_, o=DELTA, i=SPD:
                                     e_.activation(out=o, in_=i, func=AF.Ln),
                                     waits=[ev_w], inc=c_sca, nobar=True)
                        ev_r = p.add("vec", lambda e_, o=CACC, i=XCONV:
                                     e_.tensor_scalar_max(o, i, 0.0),
                                     waits=[ev_xc] + cacc_b.wr(), inc=c_vec)
                        ev_dl = p.add("vec", lambda e_, o=SPD, a=DELTA, b=CACC:
                                      e_.tensor_add(o, a, b), waits=[ev_l], inc=c_vec)
                        cacc_b.frees.append(ev_dl)
                        ev_bx = p.add("sca", lambda e_, o=BX, i=XCONV:
                                      e_.activation(out=o, in_=i, func=AF.Square), inc=c_sca,
                                      nobar=True)
                        if seq == 0 and chk == 2 and ct == 0:
                            dump(20, SPD, ev_dl, xs_b)
                            dump(21, BX, ev_bx)
                            dump(22, RSIL, ev_rsl, rsil_b)

                        ev_h_even = None
                        ev_ysum = None
                        for n in range(NST):
                            asl = n % 2
                            hsl = n % 2
                            ev_a = p.add("sca", lambda e_, o=ATIL[asl], i=SPD,
                                         s=STATS[:, O_ANEG + ct * NST + n:O_ANEG + ct * NST + n + 1]:
                                         e_.activation(out=o, in_=i, func=AF.Exp, scale=s),
                                         waits=[ev_dl, ev_aneg] + atil_b[asl].wr(), inc=c_sca,
                                         nobar=True)
                            ini = STATS[:, O_HST + (seq * NCT + ct) * NST + n:][:, :1]
                            ev_h = p.add("vec", lambda e_, o=HT[hsl], a=ATIL[asl], b=BX, i=ini:
                                         e_.tensor_tensor_scan(out=o, data0=a, data1=b, initial=i,
                                                               op0=ALU.mult, op1=ALU.add),
                                         waits=[ev_a, ev_bx, carry_ev[seq][ct] or ev_hst0]
                                               + ht_b[hsl].wr(),
                                         inc=c_vec, nobar=True)
                            atil_b[asl].frees.append(ev_h)
                            if n % 2 == 0:
                                ev_h_even = ev_h

                            ev_cs = p.add("gp", lambda e_, o=ini, i=HT[hsl][:, CH - 1:CH]:
                                          e_.tensor_copy(out=o, in_=i), waits=[ev_h], inc=c_gp)
                            if n == NST - 1:
                                carry_ev[seq][ct] = ev_cs
                            ht_b[hsl].frees.append(ev_cs)
                            if n % 2 == 1:
                                k = n // 2
                                B1, B2, B3, T, T2 = TREE

                                def padd(o_, buf_i, extra=()):
                                    ev_ = p.add("vec", lambda e_, o=o_, a=HT[0], b=HT[1]:
                                                e_.tensor_add(o, a, b),
                                                waits=[ev_h] + tree_b[buf_i].wr()
                                                      + list(extra),
                                                inc=c_vec)
                                    ht_b[0].frees.append(ev_)
                                    ht_b[1].frees.append(ev_)
                                    return ev_

                                def tadd(o_, a_, b_, buf_o, frees):
                                    ev_ = p.add("vec", lambda e_, o=o_, a=a_, b=b_:
                                                e_.tensor_add(o, a, b),
                                                waits=tree_b[buf_o].wr() if buf_o is not None else
                                                      yacc_b.wr(),
                                                inc=c_vec)
                                    for fi in frees:
                                        tree_b[fi].frees.append(ev_)
                                    return ev_

                                if k % 2 == 0:
                                    padd(B1, 0)
                                else:
                                    padd(T, 3)
                                    if k == 1 or k == 5:
                                        tadd(B2, B1, T, 1, (0, 3))
                                    elif k == 3:
                                        tadd(T2, B1, T, 4, (0, 3))
                                        tadd(B3, B2, T2, 2, (1, 4))
                                    elif k == 7:
                                        tadd(T2, B1, T, 4, (0, 3))
                                        tadd(T, B2, T2, 3, (1, 4))
                                        ev_ysum = tadd(YACC, B3, T, None, (2, 3))
                        xs_b.frees.append((c_sca, c_sca.count))
                        ev_t1 = p.add("vec", lambda e_, o=YACC, a=YACC, b=RSIL:
                                      e_.tensor_mul(o, a, b), waits=[ev_ysum, ev_rsl],
                                      inc=c_vec)
                        rsil_b.frees.append(ev_t1)
                        YFT = ABF[:, YF_OFF + ct * CH:YF_OFF + (ct + 1) * CH]
                        ev_yf = p.add("vec", lambda e_, o=YFT, i=YACC,
                                      s=STATS[:, O_D + ct:O_D + ct + 1], x=XCONV:
                                      e_.scalar_tensor_tensor(out=o, in0=i, scalar=s, in1=x,
                                                              op0=ALU.add, op1=ALU.mult),
                                      waits=[ev_xc] + yf_b[ct].wr(), inc=c_vec)
                        xconv_b.frees.append(ev_yf)
                        if seq == 0 and chk == 2:
                            dump(ct, YFT, ev_yf, yf_b[ct])
                        if seq == 0 and chk == 0 and ct == 0:
                            dump(16, YACC, ev_t1, yacc_b)
                            dump(17, YFT, ev_yf, yf_b[ct])
                        yacc_b.frees.append(ev_yf)
                        yf_b[ct].ready = ev_yf

                    ev_bank = [None] * NDT
                    for ct in range(LCT):
                        sl = ct % 2
                        for dt in range(NDT):
                            ev_m = p.add("pe", lambda e_, o=PS[dt][0:128, 0:512],
                                         l=WOT[sl][:, dt * 128:(dt + 1) * 128],
                                         r=ABF[:, YF_OFF + ct * CH:YF_OFF + (ct + 1) * CH],
                                         s=(ct == 0), st=(ct == LCT - 1):
                                         e_.matmul(o, l, r, start=s, stop=st),
                                         waits=([yf_b[ct].ready, wot_ld[ct]] +
                                                (psb[dt].wr() if ct == 0 else [])),
                                         inc=(c_pe if (dt == NDT - 1 or ct == LCT - 1) else None))
                            if ct == LCT - 1:
                                ev_bank[dt] = ev_m
                            if dt == NDT - 1:
                                wot_b[sl].frees.append(ev_m)
                        if ct + 2 < LCT:
                            sl2 = (ct + 2) % 2
                            wot_ld[ct + 2] = p.add("sy",
                                                   lambda e_, o=WOT[sl2],
                                                   i=woutt_h[(ct + 2) * 128:(ct + 3) * 128, :]:
                                                   e_.dma_start(out=o, in_=i),
                                                   waits=wot_b[sl2].wr(), inc=c_wot[sl2])
                    for dt in range(NDT):
                        ev_ra = p.add("vec", lambda e_, o=RES[:, dt * TCORE + t0:][:, :CH],
                                      a=PS[dt][0:128, 0:512], b=RES[:, dt * TCORE + t0:][:, :CH]:
                                      e_.tensor_add(o, a, b), waits=[ev_bank[dt]], inc=c_vec,
                                      bar=True)

                        psb[dt].frees.append(ev_ra)
                        if dt == NDT - 1:
                            for ct in range(NCT):
                                yf_b[ct].frees.append(ev_ra)
                            xln_b.frees.append(ev_ra)

        def barrier():
            return [(c_pe, c_pe.count), (c_vec, c_vec.count),
                    (c_sca, c_sca.count), (c_gp, c_gp.count)]

        if n_blocks >= 2 and not _SKIP_P2:
            bar1 = barrier()
            for dt in range(NDT):
                for e2 in range(EXP_N):
                    p.add("gp", lambda e_, o=RWTB[:, dt * 8 + e2:dt * 8 + e2 + 1],
                          i=rw[e2, dt * 128:(dt + 1) * 128].rearrange("(p one) -> p one", one=1):
                          e_.dma_start(out=o, in_=i), inc=c_misc2)
            p.add("gp", lambda e_, o=RBROW[:, :],
                  i=rb[:].rearrange("(one f) -> one f", one=1):
                  e_.dma_start(out=o, in_=i), inc=c_misc2)
            p.add("gp", lambda e_, o=DBB[:, :], i=dwb[:, :]:
                  e_.dma_start(out=o, in_=i), inc=c_misc2)
            ev_misc2 = (c_misc2, c_misc2.count)
            ev_rwt = ev_rbb = ev_dbb = ev_misc2

            moe = dict(
                xln2_b=Buf(), glu_b=[Buf() for _ in range(8)], actw_b=[Buf() for _ in range(8)],
                gt_b=Buf(), up_b=Buf(), sig_b=Buf(), wbt_b=Buf(), wtb_b=Buf(), wrow_b=Buf(),
                wsc_b=Buf(),
                dacc_b=[Buf() for _ in range(NDT)], gub_b=Buf(),
                ring_b=[Buf() for _ in range(4)],
            )
            GLUA = [ABF[:, YF_OFF + i * CH:YF_OFF + (i + 1) * CH] for i in range(8)]
            ACTW = [ABF[:, YF_OFF + (8 + i) * CH:YF_OFF + (9 + i) * CH] for i in range(8)]
            WBT = ABF[:, HT_OFF:HT_OFF + CH]
            GTT = TREE[3]
            UP1 = TREE[4]
            SIGT = ABF[:, YACC_OFF:YACC_OFF + CH]
            DACC = [AF32[:, ht * 512:(ht + 1) * 512] for ht in range(NDT)]
            RING = [AW[:, i * 8192:(i + 1) * 8192] for i in range(4)]
            O_GUB = 864
            O_RT = 64

            def ring_load(sloti, e2, kind, extra_waits=()):
                if kind == "d":
                    src = dwnb_h[e2].rearrange("(a p) f -> p a f", p=128)
                else:
                    base = {"g": 0, "u": IMOE}[kind]
                    src = gupb_h[e2][:, base:base + IMOE].rearrange("(a p) f -> p a f", p=128)
                return p.add("sy", lambda e_,
                             o=RING[sloti].rearrange("p (a f) -> p a f", a=NDT), i=src:
                             e_.dma_start(out=o, in_=i),
                             waits=moe["ring_b"][sloti].wr() + list(extra_waits) +
                                   [ev_pre_all],
                             inc=c_ring[sloti])

            ring_plan = []
            for e2 in range(EXP_N):
                ring_plan += [(e2, "g"), (e2, "u"), (e2, "d")]

            LSC, LEXP = _LIMIT2 if _LIMIT2 is not None else (NDT, EXP_N)
            for sc in range(LSC):
                t0 = sc * CH
                ev_xln2 = emit_ln(t0, (bar1 if sc == 0 else []) + moe["xln2_b"].wr())
                if sc == 0:
                    dump(9, XLNT[0], ev_xln2, moe["xln2_b"])
                ev_wsc = []
                for mi in range(4):
                    bk = 4 + mi % 2
                    wv_ = psb[bk].wr()
                    ev_lg = None
                    for dt in range(NDT):
                        ev_lg = p.add("pe", lambda e_, o=PS[bk][0:128, 0:8],
                                      l=XLNT[dt][:, mi * 128:(mi + 1) * 128],
                                      r=RWTB[:, dt * 8:(dt + 1) * 8], s=(dt == 0):
                                      e_.matmul(o, l, r, start=s, stop=False),
                                      waits=([ev_xln2, ev_rwt] + wv_) if dt == 0 else [])
                    ev_lg = p.add("pe", lambda e_, o=PS[bk][0:128, 0:8],
                                  l=ONESB[0:1, 0:128], r=RBROW[:, :]:
                                  e_.matmul(o, l, r, start=False, stop=True),
                                  waits=[ev_rbb], inc=c_pe)
                    NM1 = SMALL[:, O_RT:O_RT + 1]
                    M2 = SMALL[:, O_RT + 1:O_RT + 2]
                    RZ = SMALL[:, O_RT + 2:O_RT + 3]
                    ZZ = SMALL[:, O_RT + 3:O_RT + 4]
                    PEXP = SMALL[:, O_RT + 4:O_RT + 12]
                    DD = SMALL[:, O_RT + 12:O_RT + 20]
                    EQ = SMALL[:, O_RT + 20:O_RT + 28]
                    WSC = SMALL[:, 96 + mi * 8:96 + (mi + 1) * 8]
                    ev_nm1 = p.add("vec", lambda e_, o=NM1, i=PS[bk][0:128, 0:8]:
                          e_.tensor_reduce(o, i, mybir.AxisListType.X, ALU.max, negate=True),
                          waits=[ev_lg], inc=c_vec, bar=True)
                    ev_px = p.add("sca", lambda e_, o=PEXP, i=PS[bk][0:128, 0:8], b=NM1:
                                  e_.activation(out=o, in_=i, func=AF.Exp, bias=b),
                                  waits=[ev_lg, ev_nm1], inc=c_sca, bar=True)
                    p.add("vec", lambda e_, o=DD, i=PS[bk][0:128, 0:8], s=NM1:
                          e_.tensor_scalar_add(o, i, s), waits=[ev_lg], bar=True)
                    ev_fr = p.add("vec", lambda e_, o=EQ, i=DD:
                                  e_.tensor_single_scalar(o, i, 0.0, ALU.is_ge), inc=c_vec)
                    psb[bk].frees.append(ev_fr)
                    psb[bk].frees.append(ev_px)
                    p.add("vec", lambda e_, o=EQ, i=EQ, s=DD:
                          e_.scalar_tensor_tensor(out=o, in0=i, scalar=-1e30, in1=s,
                                                  op0=ALU.mult, op1=ALU.add))
                    p.add("vec", lambda e_, o=M2, i=EQ:
                          e_.tensor_reduce(o, i, mybir.AxisListType.X, ALU.max), bar=True)
                    p.add("vec", lambda e_, o=EQ, i=DD, s=M2:
                          e_.tensor_single_scalar(o, i, s, ALU.is_ge))
                    p.add("vec", lambda e_, o=ZZ, i=PEXP:
                          e_.tensor_reduce(o, i, mybir.AxisListType.X, ALU.add), waits=[ev_px])
                    p.add("vec", lambda e_, o=RZ, i=ZZ: e_.reciprocal(o, i), bar=True)
                    ev_w = p.add("vec", lambda e_, o=WSC, i=PEXP, s=RZ, m=EQ:
                                 e_.scalar_tensor_tensor(out=o, in0=i, scalar=s, in1=m,
                                                         op0=ALU.mult, op1=ALU.mult),
                                 waits=moe["wsc_b"].wr(), inc=c_vec)
                    ev_wsc.append(ev_w)
                    if sc == 0 and mi == 0:
                        dump(10, SMALL[:, 64:92], ev_w)
                if sc == 0:
                    dump(0, SMALL[:, 96:128], ev_wsc[3])
                ev_wtb = None
                for mi in range(4):
                    pst = PS[6]
                    evt = p.add("pe", lambda e_, o=pst[0:8, 0:128],
                                i=SMALL[:, 96 + mi * 8:96 + (mi + 1) * 8]:
                                e_.transpose(o, i, EYE[:, :]),
                                waits=[ev_wsc[mi]] + psb[6].wr(), inc=c_pe)
                    ev_wtb = p.add("vec", lambda e_, o=WTB[0:8, mi * 128:(mi + 1) * 128],
                                   i=pst[0:8, 0:128]:
                                   e_.tensor_copy(out=o, in_=i),
                                   waits=[evt] + (moe["wtb_b"].wr() if mi == 0 else []),
                                   inc=c_vec, bar=True)
                    psb[6].frees.append(ev_wtb)
                    if mi == 3:
                        moe["wsc_b"].frees.append((c_pe, c_pe.count))
                if sc == 0:
                    ring_ld = {}
                    ring_pos = 0
                    for k in range(4):
                        e2, kind = ring_plan[ring_pos]
                        ring_ld[(e2, kind)] = ring_load(k, e2, kind, extra_waits=bar1)
                        ring_pos += 1

                for e2 in range(LEXP):
                    sg = (3 * e2) % 4
                    su = (3 * e2 + 1) % 4
                    sd = (3 * e2 + 2) % 4
                    ev_g1 = p.add("sy", lambda e_, o=GUBSTG[:, :],
                                  i=gub[e2, :].rearrange("(a p) -> a p", p=128):
                                  e_.dma_start(out=o, in_=i),
                                  waits=moe["gub_b"].wr(), inc=c_gub)
                    pst = PS[7]
                    ev_g2 = p.add("pe", lambda e_, o=pst[0:128, 0:16], i=GUBSTG[:, :]:
                                  e_.transpose(o, i, EYE[0:16, 0:16]),
                                  waits=[ev_g1] + psb[7].wr(), inc=c_pe)
                    moe["gub_b"].frees.append(ev_g2)
                    ev_g3 = p.add("vec", lambda e_, o=STATS[:, O_GUB:O_GUB + 16], i=pst[0:128, 0:16]:
                                  e_.tensor_copy(out=o, in_=i), waits=[ev_g2], inc=c_vec, bar=True)
                    psb[7].frees.append(ev_g3)
                    ev_g4 = p.add("vec", lambda e_, o=STATS[:, O_GUB + 16:O_GUB + 24],
                                  i=STATS[:, O_GUB + 8:O_GUB + 16]:
                                  e_.tensor_scalar_add(o, i, 1.0), inc=c_vec)
                    if sc == 0 and e2 == 0:
                        dump(6, STATS[:, O_GUB:O_GUB + 24], ev_g4)
                    bk = 4 + e2 % 2
                    ev_wr = p.add("sy", lambda e_, o=WROW[:, :], i=WTB[e2:e2 + 1, 0:512]:
                                  e_.dma_start(out=o, in_=i),
                                  waits=[ev_wtb] + moe["wrow_b"].wr(), inc=c_wrow)
                    ev_wb = p.add("pe", lambda e_, o=PS[bk][0:128, 0:512],
                                  l=ONESB[0:1, 0:128], r=WROW[:, :]:
                                  e_.matmul(o, l, r, start=True, stop=True),
                                  waits=[ev_wr] + psb[bk].wr(), inc=c_pe)
                    moe["wrow_b"].frees.append(ev_wb)
                    ev_wbc = p.add("vec", lambda e_, o=WBT, i=PS[bk][0:128, 0:512]:
                                   e_.tensor_copy(out=o, in_=i),
                                   waits=[ev_wb] + moe["wbt_b"].wr(), inc=c_vec, bar=True)
                    if sc == 0 and e2 == 0:
                        dump(7, WBT, ev_wbc, moe["wbt_b"])
                    psb[bk].frees.append(ev_wbc)
                    if e2 == LEXP - 1:
                        moe["wtb_b"].frees.append(ev_wb)

                    evg = ring_ld[(e2, "g")]
                    for ft in range(8):
                        bk2 = ft % 2
                        ev_mg = None
                        for dt in range(NDT):
                            ev_mg = p.add("pe", lambda e_, o=PS[bk2][0:128, 0:512],
                                          l=RING[sg][:, dt * 1024 + ft * 128:][:, :128],
                                          r=XLNT[dt], s=(dt == 0), st=(dt == NDT - 1):
                                          e_.matmul(o, l, r, start=s, stop=st),
                                          waits=([evg, ev_xln2] + psb[bk2].wr()) if dt == 0 else [],
                                          inc=(c_pe if dt == NDT - 1 else None))
                            if ft == 7 and dt == NDT - 1:
                                moe["ring_b"][sg].frees.append(ev_mg)
                        ev_gt = p.add("vec", lambda e_, o=GTT, i=PS[bk2][0:128, 0:512],
                                      s=STATS[:, O_GUB + ft:O_GUB + ft + 1]:
                                      e_.tensor_scalar(o, i, s, 7.0, ALU.add, ALU.min),
                                      waits=[ev_mg, ev_g3] + moe["gt_b"].wr(), inc=c_vec, bar=True)
                        psb[bk2].frees.append(ev_gt)
                        ev_sig = p.add("sca", lambda e_, o=SIGT, i=GTT:
                                       e_.activation(out=o, in_=i, func=AF.Sigmoid, scale=1.702),
                                       waits=[ev_gt] + moe["sig_b"].wr(), inc=c_sca)
                        ev_glu = p.add("vec", lambda e_, o=GLUA[ft], a=GTT, b=SIGT:
                                       e_.tensor_mul(o, a, b),
                                       waits=[ev_sig] + moe["glu_b"][ft].wr(), inc=c_vec)
                        if sc == 0 and e2 == 0 and ft == 0:
                            dump(1, GTT, ev_gt, moe["gt_b"])
                            dump(2, GLUA[ft], ev_glu, moe["glu_b"][ft])
                        moe["gt_b"].frees.append(ev_glu)
                        moe["sig_b"].frees.append(ev_glu)
                        moe["glu_b"][ft].ready = ev_glu
                    evu = ring_ld[(e2, "u")]
                    for ft in range(8):
                        bk2 = 2 + ft % 2
                        ev_mu = None
                        for dt in range(NDT):
                            ev_mu = p.add("pe", lambda e_, o=PS[bk2][0:128, 0:512],
                                          l=RING[su][:, dt * 1024 + ft * 128:][:, :128],
                                          r=XLNT[dt], s=(dt == 0), st=(dt == NDT - 1):
                                          e_.matmul(o, l, r, start=s, stop=st),
                                          waits=([evu] + psb[bk2].wr()) if dt == 0 else [],
                                          inc=(c_pe if dt == NDT - 1 else None))
                            if ft == 7 and dt == NDT - 1:
                                moe["ring_b"][su].frees.append(ev_mu)
                            if ft == 7 and dt == NDT - 1 and e2 == LEXP - 1:
                                moe["xln2_b"].frees.append(ev_mu)
                        ev_u1 = p.add("vec", lambda e_, o=UP1, i=PS[bk2][0:128, 0:512],
                                      s=STATS[:, O_GUB + 16 + ft:O_GUB + 17 + ft]:
                                      e_.tensor_scalar(o, i, s, 8.0, ALU.add, ALU.min),
                                      waits=[ev_mu, ev_g4] + moe["up_b"].wr(), inc=c_vec, bar=True)
                        psb[bk2].frees.append(ev_u1)
                        p.add("vec", lambda e_, o=UP1, i=UP1:
                              e_.tensor_scalar_max(o, i, -6.0))
                        p.add("vec", lambda e_, o=UP1, a=UP1, b=GLUA[ft]:
                              e_.tensor_mul(o, a, b), waits=[moe["glu_b"][ft].ready])
                        ev_aw = p.add("vec", lambda e_, o=ACTW[ft], a=UP1, b=WBT:
                                      e_.tensor_mul(o, a, b),
                                      waits=[ev_wbc] + moe["actw_b"][ft].wr(), inc=c_vec)
                        moe["up_b"].frees.append(ev_aw)
                        moe["glu_b"][ft].frees.append(ev_aw)
                        moe["actw_b"][ft].ready = ev_aw
                        if sc == 0 and e2 == 0 and ft == 0:
                            dump(3, ACTW[ft], ev_aw, moe["actw_b"][ft])
                        if e2 == LEXP - 1:
                            moe["wbt_b"].frees.append(ev_aw)
                    moe["wbt_b"].frees.append((c_vec, c_vec.count))
                    evd = ring_ld[(e2, "d")]
                    for ht in range(NDT):
                        bk2 = ht % 2
                        wv_ = psb[bk2].wr()
                        ev_md = None
                        first = True
                        if e2 == 0:
                            ev_md = p.add("pe", lambda e_, o=PS[bk2][0:128, 0:512],
                                          l=DBB[:, ht * 128:(ht + 1) * 128], r=WTB[0:8, 0:512]:
                                          e_.matmul(o, l, r, start=True, stop=False),
                                          waits=[ev_dbb, ev_wtb] + wv_, inc=None)
                            first = False
                        for it in range(8):
                            ev_md = p.add("pe", lambda e_, o=PS[bk2][0:128, 0:512],
                                          l=RING[sd][:, it * 1024 + ht * 128:][:, :128],
                                          r=ACTW[it], s=first and it == 0, st=(it == 7):
                                          e_.matmul(o, l, r, start=s, stop=st),
                                          waits=([evd] + (wv_ if first else []) +
                                                 [moe["actw_b"][i2].ready for i2 in range(8)])
                                                if it == 0 else [],
                                          inc=(c_pe if it == 7 else None))
                            if ht == NDT - 1 and it == 7:
                                moe["ring_b"][sd].frees.append(ev_md)
                        if e2 == 0:
                            ev_da = p.add("vec", lambda e_, o=DACC[ht], i=PS[bk2][0:128, 0:512]:
                                          e_.tensor_copy(out=o, in_=i),
                                          waits=[ev_md] + moe["dacc_b"][ht].wr(), inc=c_vec,
                                          bar=True)
                        else:
                            ev_da = p.add("vec", lambda e_, o=DACC[ht], a=PS[bk2][0:128, 0:512],
                                          b=DACC[ht]:
                                          e_.tensor_add(o, a, b), waits=[ev_md], inc=c_vec,
                                          bar=True)
                        psb[bk2].frees.append(ev_da)
                        if sc == 0 and ht == 0 and e2 == 0:
                            dump(4, DACC[ht], ev_da, moe["dacc_b"][ht])
                        if e2 == LEXP - 1:
                            moe["dacc_b"][ht].ready = ev_da
                            if sc == 0 and ht == 0:
                                dump(5, DACC[ht], ev_da, moe["dacc_b"][ht])
                    for i2 in range(8):
                        moe["actw_b"][i2].frees.append((c_pe, c_pe.count))
                    for _ in range(3):
                        if ring_pos < len(ring_plan):
                            e3, kind3 = ring_plan[ring_pos]
                            slot3 = (3 * e3 + {"g": 0, "u": 1, "d": 2}[kind3]) % 4
                            ring_ld[(e3, kind3)] = ring_load(slot3, e3, kind3)
                            ring_pos += 1
                for ht in range(NDT):
                    ev_r2 = p.add("vec", lambda e_, o=RES[:, ht * TCORE + t0:][:, :CH],
                                  a=DACC[ht], b=RES[:, ht * TCORE + t0:][:, :CH]:
                                  e_.tensor_add(o, a, b), inc=c_vec)
                    moe["dacc_b"][ht].frees.append(ev_r2)
                    if ht == NDT - 1:
                        moe["xln2_b"].frees.append(ev_r2)
                    if sc == 0 and ht == 0:
                        dump(8, RES[:, 0:512], ev_r2)
                ring_plan = []
                for e2 in range(EXP_N):
                    ring_plan += [(e2, "g"), (e2, "u"), (e2, "d")]
                ring_pos_next = 0
                if sc < LSC - 1:
                    ring_ld = {}
                    ring_pos = 0
                    for k in range(4):
                        e3, kind3 = ring_plan[ring_pos]
                        slot3 = (3 * e3 + {"g": 0, "u": 1, "d": 2}[kind3]) % 4
                        ring_ld[(e3, kind3)] = ring_load(slot3, e3, kind3)
                        ring_pos += 1

        if n_blocks >= 3 and not _SKIP_P3:
            bar3 = barrier()
            QB = 512
            NQB = S // QB
            QTS_OFF = YF_OFF
            CTXT_OFF = YF_OFF + 4096
            ATT_OFF = HT_OFF
            RNB_OFF = TREE_OFF
            WQK_OFF = TREE_OFF + 3 * 512
            WVS_OFF = WQK_OFF + 3 * 1024
            WOS_OFF = TREE_OFF
            QTS = ABF[:, QTS_OFF:QTS_OFF + 4096]
            CTXT = ABF[:, CTXT_OFF:CTXT_OFF + 4096]
            ATT = [ABF[:, ATT_OFF + i * 512:ATT_OFF + (i + 1) * 512] for i in range(2)]
            RNB = ABF[:, RNB_OFF:RNB_OFF + 512]
            WQKS = [ABF[:, WQK_OFF + i * 1024:WQK_OFF + (i + 1) * 1024] for i in range(3)]
            WVS = [ABF[:, WVS_OFF + i * 512:WVS_OFF + (i + 1) * 512] for i in range(2)]
            WOS = [ABF[:, WOS_OFF + i * 128:WOS_OFF + (i + 1) * 128] for i in range(4)]
            KTC = AW[:, 16384:32768]
            VC = AW[:, 0:16384]
            O_BQK = 864
            O_BO = 880

            att = dict(qts_b=Buf(), ctxt_b=[Buf() for _ in range(16)], att_b=[Buf(), Buf()],
                       rnb_b=Buf(), wqk_b=[Buf() for _ in range(3)],
                       wv_b=[Buf() for _ in range(2)], wo_b=[Buf() for _ in range(4)],
                       xln3_b=Buf(), rnt_b=Buf(), bstg_b=Buf(), kv_free_ev=None)
            c_wqk = [p.chan(f"wqk{i}", 16, True) for i in range(3)]
            c_wv = [p.chan(f"wv{i}", 16, True) for i in range(2)]
            c_wo = [p.chan(f"wo{i}", 16, True) for i in range(4)]
            c_misc3 = p.chan("m3", 16, True)
            c_bstg = p.chan("bstg", 16, True)
            for c in (c_wqk + c_wv + c_wo + [c_misc3, c_bstg]):
                c.h = st.enter_context(nc.semaphore(c.name))

            for dt in range(NDT):
                p.add("gp", lambda e_, o=WNB[:, dt * 16:(dt + 1) * 16],
                      i=wn[dt * 128:(dt + 1) * 128, :]:
                      e_.dma_start(out=o, in_=i), inc=c_misc3)
            p.add("gp", lambda e_, o=BNROW[:, :], i=bn[:].rearrange("(one f) -> one f", one=1):
                  e_.dma_start(out=o, in_=i), inc=c_misc3)
            p.add("gp", lambda e_, o=BVROW[:, :], i=bv[:].rearrange("(one f) -> one f", one=1):
                  e_.dma_start(out=o, in_=i), inc=c_misc3)
            ev_misc3 = (c_misc3, c_misc3.count)
            def col_load(vec_ap, n16, dst_off, wt_extra):
                ev1 = p.add("sy", lambda e_, o=GUBSTG[0:n16, :], i=vec_ap:
                            e_.dma_start(out=o, in_=i),
                            waits=att["bstg_b"].wr() + wt_extra, inc=c_bstg)
                ev2 = p.add("pe", lambda e_, o=PS[7][0:128, 0:n16], i=GUBSTG[0:n16, :]:
                            e_.transpose(o, i, EYE[0:n16, 0:n16]),
                            waits=[ev1] + psb[7].wr(), inc=c_pe)
                att["bstg_b"].frees.append(ev2)
                ev3 = p.add("vec", lambda e_, o=STATS[:, dst_off:dst_off + n16],
                            i=PS[7][0:128, 0:n16]:
                            e_.tensor_copy(out=o, in_=i), waits=[ev2], inc=c_vec)
                psb[7].frees.append(ev3)
                return ev3
            ev_bqk = col_load(bqk[:].rearrange("(a p) -> a p", p=128), 16, O_BQK, bar3)
            ev_bo = col_load(bo[:].rearrange("(a p) -> a p", p=128), 8, O_BO, [])

            def wqk_load(sloti, ft, extra=()):
                return p.add("sy", lambda e_,
                             o=WQKS[sloti].rearrange("p (a f) -> p a f", a=NDT),
                             i=wqkb_h[:, ft * 128:(ft + 1) * 128].rearrange(
                                 "(a p) f -> p a f", p=128):
                             e_.dma_start(out=o, in_=i),
                             waits=att["wqk_b"][sloti].wr() + list(extra) + [ev_pre_all],
                             inc=c_wqk[sloti])

            def wv_load(sloti, dt, g, extra=()):
                return p.add("sy", lambda e_, o=WVS[sloti],
                             i=wvb_h[dt * 128:(dt + 1) * 128, g * 512:(g + 1) * 512]:
                             e_.dma_start(out=o, in_=i),
                             waits=att["wv_b"][sloti].wr() + list(extra) + [ev_pre_all],
                             inc=c_wv[sloti])

            def wo_load(sloti, di, do, extra=()):
                return p.add("sy", lambda e_, o=WOS[sloti],
                             i=wob_h[di * 128:(di + 1) * 128, do * 128:(do + 1) * 128]:
                             e_.dma_start(out=o, in_=i),
                             waits=att["wo_b"][sloti].wr() + list(extra) + [ev_pre_all],
                             inc=c_wo[sloti])

            LSEQ3, LQB3 = _LIMIT3 if _LIMIT3 is not None else (SEQN, NQB)
            for seq in range(LSEQ3):
                for qb in range(LQB3):
                    t0 = seq * S + qb * QB
                    ev_xln3 = emit_ln(t0, (bar3 if (seq == 0 and qb == 0) else []) +
                                      att["xln3_b"].wr())
                    ev_rn4 = []
                    for mi in range(4):
                        bk = 4 + mi % 2
                        wv_ = psb[bk].wr()
                        ev_n = None
                        for dt in range(NDT):
                            ev_n = p.add("pe", lambda e_, o=PS[bk][0:128, 0:16],
                                         l=XLNT[dt][:, mi * 128:(mi + 1) * 128],
                                         r=WNB[:, dt * 16:(dt + 1) * 16], s=(dt == 0):
                                         e_.matmul(o, l, r, start=s, stop=False),
                                         waits=([ev_xln3, ev_misc3] + wv_) if dt == 0 else [])
                        ev_n = p.add("pe", lambda e_, o=PS[bk][0:128, 0:16],
                                     l=ONESB[0:1, 0:128], r=BNROW[:, :]:
                                     e_.matmul(o, l, r, start=False, stop=True), inc=c_pe)
                        EXPN = SMALL[:, 80:96]
                        ev_e = p.add("sca", lambda e_, o=EXPN, i=PS[bk][0:128, 0:16]:
                                     e_.activation(out=o, in_=i, func=AF.Exp), waits=[ev_n])
                        psb[bk].frees.append(ev_e)
                        RNm = SMALL[:, 64 + 0:80] if False else SMALL[:, 64:80]
                        ev_r = p.add("vec", lambda e_, o=RNm, i=EXPN:
                                     e_.reciprocal(o, i), waits=[ev_e, (c_sca, c_sca.count)])
                        ev_t = p.add("pe", lambda e_, o=PS[6][0:16, 0:128], i=RNm:
                                     e_.transpose(o, i, EYE[:, :]),
                                     waits=[ev_r] + psb[6].wr(), inc=c_pe)
                        ev_c = p.add("vec", lambda e_, o=RNT[:, mi * 128:(mi + 1) * 128],
                                     i=PS[6][0:16, 0:128]:
                                     e_.tensor_copy(out=o, in_=i),
                                     waits=[ev_t] + (att["rnt_b"].wr() if mi == 0 else []),
                                     inc=c_vec)
                        psb[6].frees.append(ev_c)
                        ev_rn4.append(ev_c)
                    ev_rnt = ev_rn4[-1]
                    if seq == 0 and qb == 1:
                        dump(11, RNT[0:16, 0:512], ev_rnt, att["rnt_b"])

                    if seq == 0 and qb == 0:
                        wqk_ld = {}
                        for k3 in range(3):
                            wqk_ld[k3] = wqk_load(k3, k3, extra=bar3)
                        p3_ftg = [0]
                    ev_qts_last = None
                    ev_ktev = None
                    for ft in range(16):
                        sl = p3_ftg[0] % 3
                        bk = ft % 2
                        evw = wqk_ld[sl]
                        ev_m = None
                        for dt in range(NDT):
                            ev_m = p.add("pe", lambda e_, o=PS[bk][0:128, 0:512],
                                         l=WQKS[sl][:, dt * 128:(dt + 1) * 128], r=XLNT[dt],
                                         s=(dt == 0), st=(dt == NDT - 1):
                                         e_.matmul(o, l, r, start=s, stop=st),
                                         waits=([evw, ev_xln3] + psb[bk].wr()) if dt == 0 else [],
                                         inc=(c_pe if dt == NDT - 1 else None))
                        att["wqk_b"][sl].frees.append(ev_m)
                        if ft < 8:
                            bk2 = 4 + ft % 2
                            ev_b = p.add("pe", lambda e_, o=PS[bk2][0:128, 0:512],
                                         l=EHB[0:16, ft * 128:(ft + 1) * 128], r=RNT[:, 0:512]:
                                         e_.matmul(o, l, r, start=True, stop=True),
                                         waits=[ev_rnt] + psb[bk2].wr(), inc=c_pe)
                            ev_rb = p.add("vec", lambda e_, o=RNB, i=PS[bk2][0:128, 0:512]:
                                          e_.tensor_copy(out=o, in_=i),
                                          waits=[ev_b] + att["rnb_b"].wr(), inc=c_vec)
                            psb[bk2].frees.append(ev_rb)
                            ev_q = p.add("vec", lambda e_, o=ABF[:, QTS_OFF + ft * 512:][:, :512],
                                         i=PS[bk][0:128, 0:512],
                                         s=STATS[:, O_BQK + ft:O_BQK + ft + 1], m=RNB:
                                         e_.scalar_tensor_tensor(out=o, in0=i, scalar=s, in1=m,
                                                                 op0=ALU.add, op1=ALU.mult),
                                         waits=[ev_m, ev_bqk, ev_rb] + (att["qts_b"].wr()
                                                                        if ft == 0 else []),
                                         inc=c_vec)
                            att["rnb_b"].frees.append(ev_q)
                            psb[bk].frees.append(ev_q)
                            ev_qts_last = ev_q
                            if seq == 0 and qb == 1 and ft == 0:
                                dump(12, ABF[:, QTS_OFF:QTS_OFF + 512], ev_q, att["qts_b"])
                                dump(13, RNB, ev_rb, att["rnb_b"])
                        else:
                            ftk = ft - 8
                            ev_k = p.add("vec", lambda e_,
                                         o=KTC[:, ftk * 2048 + qb * 512:][:, :512],
                                         i=PS[bk][0:128, 0:512],
                                         s=STATS[:, O_BQK + ft:O_BQK + ft + 1]:
                                         e_.tensor_scalar_add(o, i, s),
                                         waits=[ev_m, ev_bqk] + ([att["kv_free_ev"]]
                                                                  if (ft == 8 and qb == 0
                                                                      and seq > 0) else []),
                                         inc=c_vec)
                            psb[bk].frees.append(ev_k)
                            ev_ktev = ev_k
                            if seq == 0 and qb == 1 and ft == 8:
                                dump(14, KTC[:, 0:512], ev_k)
                                dump(19, KTC[:, 512:1024], ev_k)
                        tgt = p3_ftg[0] + 3
                        p3_ftg[0] += 1
                        if tgt < 16 * LQB3 * LSEQ3:
                            wqk_ld[tgt % 3] = wqk_load(tgt % 3, tgt % 16)

                    ev_vev = None
                    for g in range(2):
                        for dt in range(NDT):
                            sl = dt % 2
                            evl = wv_load(sl, dt, g,
                                          extra=(bar3 if (seq == 0 and qb == 0 and g == 0
                                                          and dt == 0) else []))
                            for mi in range(4):
                                bk = 2 + mi if False else (2 + mi)
                                ev_m = p.add("pe", lambda e_, o=PS[2 + mi][0:128, 0:512],
                                             l=XLNT[dt][:, mi * 128:(mi + 1) * 128],
                                             r=WVS[sl], s=(dt == 0):
                                             e_.matmul(o, l, r, start=s, stop=False),
                                             waits=([evl] + (psb[2 + mi].wr() if dt == 0 else [])),
                                             inc=(c_pe if mi == 3 else None))
                            att["wv_b"][sl].frees.append(ev_m)
                        for mi in range(4):
                            ev_m = p.add("pe", lambda e_, o=PS[2 + mi][0:128, 0:512],
                                         l=ONESB[0:1, 0:128], r=BVROW[:, g * 512:(g + 1) * 512]:
                                         e_.matmul(o, l, r, start=False, stop=True),
                                         waits=[ev_misc3], inc=c_pe)
                            ev_v = p.add("vec", lambda e_,
                                         o=VC[:, (qb * 4 + mi) * 1024 + g * 512:][:, :512],
                                         i=PS[2 + mi][0:128, 0:512]:
                                         e_.tensor_copy(out=o, in_=i),
                                         waits=[ev_m] + ([att["kv_free_ev"]]
                                                         if (qb == 0 and g == 0 and mi == 0
                                                             and seq > 0) else []),
                                         inc=c_vec)
                            psb[2 + mi].frees.append(ev_v)
                            ev_vev = ev_v
                            if seq == 0 and qb == 1 and g == 0 and mi == 0:
                                dump(15, VC[:, 0:512], ev_v)

                    nkt = 4 * (qb + 1)
                    ev_ctx_done = []
                    for h in range(16):
                        po = (h % 2) * 64
                        ftk = h // 2
                        bkc = 2 + h % 2
                        wvc = psb[bkc].wr()
                        ev_cm = None
                        for kt in range(nkt):
                            bks = kt % 2
                            ev_s = None
                            wv_ = psb[bks].wr()
                            ev_s = p.add("pe", lambda e_, o=PS[bks][0:128, 0:512],
                                         l=KTC[po:po + 64, ftk * 2048 + kt * 128:][:, :128],
                                         r=QTS[po:po + 64, :][:, (h // 2) * 512:(h // 2) * 512 + 512]
                                         if False else
                                         ABF[po:po + 64, QTS_OFF + ftk * 512:QTS_OFF + ftk * 512 + 512],
                                         s=True:
                                         e_.matmul(o, l, r, start=s, stop=True),
                                         waits=[ev_qts_last, ev_ktev] + wv_, inc=c_pe)
                            asl = kt % 2
                            if kt >= 4 * qb:
                                j = kt - 4 * qb
                                ev_a = p.add("vec", lambda e_, o=ATT[asl], a=PS[bks][0:128, 0:512],
                                             b=PTRIU[:, 384 - 128 * j:][:, :512]:
                                             e_.tensor_mul(o, a, b),
                                             waits=[ev_s] + att["att_b"][asl].wr(), inc=c_vec)
                                if seq == 0 and qb == 1 and h == 0 and kt == 4:
                                    dump(17, ATT[asl], ev_a, att["att_b"][asl])
                            else:
                                ev_a = p.add("vec", lambda e_, o=ATT[asl], i=PS[bks][0:128, 0:512]:
                                             e_.tensor_copy(out=o, in_=i),
                                             waits=[ev_s] + att["att_b"][asl].wr(), inc=c_vec)
                                if seq == 0 and qb == 1 and h == 0 and kt == 0:
                                    dump(18, ATT[asl], ev_a, att["att_b"][asl])
                            psb[bks].frees.append(ev_a)
                            ev_cm = p.add("pe", lambda e_, o=PS[bkc][po:po + 64, 0:512],
                                          l=VC[:, kt * 1024 + h * 64:][:, :64], r=ATT[asl],
                                          s=(kt == 0), st=(kt == nkt - 1):
                                          e_.matmul(o, l, r, start=s, stop=st),
                                          waits=[ev_a, ev_vev] + (wvc if kt == 0 else []),
                                          inc=c_pe)
                            att["att_b"][asl].frees.append(ev_cm)
                        ev_cc = p.add("vec", lambda e_,
                                      o=ABF[po:po + 64, CTXT_OFF + ftk * 512:CTXT_OFF + ftk * 512 + 512],
                                      i=PS[bkc][po:po + 64, 0:512]:
                                      e_.tensor_copy(out=o, in_=i),
                                      waits=[ev_cm] + (att["ctxt_b"][h].wr()), inc=c_vec)
                        psb[bkc].frees.append(ev_cc)
                        att["ctxt_b"][h].ready = ev_cc
                        if seq == 0 and qb == 1 and h == 1:
                            dump(16, ABF[:, CTXT_OFF:CTXT_OFF + 512], ev_cc, att["ctxt_b"][0])
                        if seq == 0 and qb == 1 and h == 15:
                            dump(20, ABF[:, CTXT_OFF + 7 * 512:CTXT_OFF + 8 * 512], ev_cc,
                                 att["ctxt_b"][15])

                    wo_ld = {}
                    for k3 in range(4):
                        wo_ld[k3] = wo_load(k3, k3, 0, extra=[ev_vev, ev_qts_last])
                    for do in range(NDT):
                        bk = 4 + do % 2
                        wv_ = psb[bk].wr()
                        ev_m = None
                        if seq == 0 and qb == 1 and do == 0:
                            dump(21, WOS[0], wo_ld[0], att["wo_b"][0])
                        for di in range(NDT):
                            idx = do * 8 + di
                            sl = idx % 4
                            evw = wo_ld[sl]
                            ev_m = p.add("pe", lambda e_, o=PS[bk][0:128, 0:512],
                                         l=WOS[sl],
                                         r=ABF[:, CTXT_OFF + di * 512:CTXT_OFF + (di + 1) * 512],
                                         s=(di == 0), st=(di == NDT - 1):
                                         e_.matmul(o, l, r, start=s, stop=st),
                                         waits=([evw] +
                                                [att["ctxt_b"][2 * di].ready,
                                                 att["ctxt_b"][2 * di + 1].ready] +
                                                (wv_ if di == 0 else [])),
                                         inc=c_pe)
                            att["wo_b"][sl].frees.append(ev_m)
                            nidx = idx + 4
                            if nidx < 64:
                                wo_ld[nidx % 4] = wo_load(nidx % 4, nidx % 8, nidx // 8)
                        ev_ra = p.add("vec", lambda e_, o=RES[:, do * TCORE + t0:][:, :512],
                                      a=PS[bk][0:128, 0:512],
                                      s=STATS[:, O_BO + do:O_BO + do + 1],
                                      b=RES[:, do * TCORE + t0:][:, :512]:
                                      e_.scalar_tensor_tensor(out=o, in0=a, scalar=s, in1=b,
                                                              op0=ALU.add, op1=ALU.add),
                                      waits=[ev_m, ev_bo], inc=c_vec)
                        if seq == 0 and qb == 1 and do == 0:
                            dump(23, RES[:, 0 * TCORE + t0:][:, :512], ev_m)
                        psb[bk].frees.append(ev_ra)
                        if seq == 0 and qb == 1 and do == 0:
                            dump(22, RES[:, 0 * TCORE + t0:][:, :512], ev_ra)
                        if do == NDT - 1:
                            ev_wo_done = (c_pe, c_pe.count)
                            tree_b[0].frees.append(ev_wo_done)
                            att["rnb_b"].frees.append(ev_wo_done)
                            att["xln3_b"].frees.append(ev_ra)
                            att["qts_b"].frees.append(ev_ra)
                            for h in range(16):
                                att["ctxt_b"][h].frees.append(ev_ra)
                    att["rnt_b"].frees.append((c_pe, c_pe.count))
                att["kv_free_ev"] = (c_pe, c_pe.count)

        if n_blocks >= 4:
            bar4 = barrier()
            c_wq = p.chan("wq", 16, True)
            c_wqs = [p.chan(f"wqs{i}", 16, True) for i in range(2)]
            c_hsp = p.chan("hsp", 16, True)
            c_hld = [p.chan(f"hld{i}", 16, True) for i in range(2)]
            c_wg = [p.chan(f"wg{i}", 16, True) for i in range(3)]
            c_wu = [p.chan(f"wu{i}", 16, True) for i in range(3)]
            c_wd = [p.chan(f"wd{i}", 16, True) for i in range(2)]
            c_cc = p.chan("ccio", 16, True)
            for c in ([c_wq, c_hsp, c_cc] + c_wqs + c_hld + c_wg + c_wu + c_wd):
                c.h = st.enter_context(nc.semaphore(c.name))

            SG_OFF, UP_OFF = 0, 512
            H_OFF, HQ_OFF = 1024, 2048
            WG_OFF, WU_OFF = 5120, 8192
            WD_OFF = 11264
            XQ_OFF = 13312
            WGS = [ABF[:, WG_OFF + i * 1024:WG_OFF + (i + 1) * 1024] for i in range(3)]
            WUS = [ABF[:, WU_OFF + i * 1024:WU_OFF + (i + 1) * 1024] for i in range(3)]
            SGT = ABF[:, SG_OFF:SG_OFF + 512]
            UPT = ABF[:, UP_OFF:UP_OFF + 512]
            HT4 = [ABF[:, H_OFF + i * 512:H_OFF + (i + 1) * 512] for i in range(2)]
            HQ4 = [ABF[:, HQ_OFF + i * 512:HQ_OFF + (i + 1) * 512] for i in range(2)]
            WDS = [ABF[:, WD_OFF + i * 1024:WD_OFF + (i + 1) * 1024] for i in range(2)]
            XQT = [ABF[:, XQ_OFF + i * 512:XQ_OFF + (i + 1) * 512] for i in range(NDT)]
            XLN4 = [AF32[:, i * 512:(i + 1) * 512] for i in range(NDT)]
            CCROW = SMALL[0:1, 80:96]

            ffn = dict(wq_b=Buf(), wg_b=[Buf() for _ in range(3)], wu_b=[Buf() for _ in range(3)],
                       wd_b=[Buf() for _ in range(2)], sg_b=Buf(), up_b=Buf(),
                       h_b=[Buf(), Buf()], hq_b=[Buf(), Buf()], xq_b=Buf(), ccrow_b=Buf())

            WQSTG = AF32[:, 0:2816]
            WQBF2 = [ABF[:, 0:128], ABF[:, 128:256]]
            wqbf_b = [Buf(), Buf()]
            ev_wqs_done = None
            for widx, (wsrc, wdst, nrow_t, ncol) in enumerate(
                    [(w_gate, wqg_h, NIT, D), (w_up, wqu_h, NIT, D), (w_down, wqd_h, NDT, IFFN)]):
                for rt in range(nrow_t):
                    evl = p.add("sy", lambda e_, o=AF32[:, 0:ncol][:, 0:ncol],
                                i=wsrc[rt * 128:(rt + 1) * 128, :]:
                                e_.dma_start(out=o, in_=i),
                                waits=ffn["wq_b"].wr() + (bar4 if widx == 0 and rt == 0 else []),
                                inc=c_wq)
                    ev_mx = p.add("vec", lambda e_, o=SMALL[:, 66:67], i=AF32[:, 0:ncol]:
                                  e_.tensor_reduce(o, i, mybir.AxisListType.X, ALU.max,
                                                   apply_absolute_value=True),
                                  waits=[evl], inc=c_vec)
                    p.add("vec", lambda e_, o=SMALL[:, 71:72], i=SMALL[:, 66:67]:
                          e_.tensor_scalar_mul(o, i, 1.0 / QMAX))
                    p.add("vec", lambda e_, o=SMALL[:, 72:73], i=SMALL[:, 71:72]:
                          e_.reciprocal(o, i))
                    p.add("vec", lambda e_, o=AF32[:, 0:ncol], i=AF32[:, 0:ncol],
                          s=SMALL[:, 72:73]:
                          e_.tensor_scalar_mul(o, i, s))
                    p.add("vec", lambda e_, o=AF32[:, 0:ncol], i=AF32[:, 0:ncol]:
                          e_.tensor_scalar(o, i, ROUND_C, -ROUND_C, ALU.add, ALU.add))
                    ev_q = p.add("vec", lambda e_, o=AF32[:, 0:ncol], i=AF32[:, 0:ncol],
                                 s=SMALL[:, 71:72]:
                                 e_.tensor_scalar_mul(o, i, s), inc=c_vec)
                    for ct4 in range(ncol // 128):
                        pslot = 6 + ct4 % 2
                        evt = p.add("pe", lambda e_, o=PS[pslot][0:128, 0:128],
                                    i=AF32[:, ct4 * 128:(ct4 + 1) * 128]:
                                    e_.transpose(o, i, EYE[:, :]),
                                    waits=[ev_q] + psb[pslot].wr(), inc=c_pe)
                        if ct4 == ncol // 128 - 1:
                            ffn["wq_b"].frees.append(evt)
                        qsl = ct4 % 2
                        evc = p.add("vec", lambda e_, o=WQBF2[qsl], i=PS[pslot][0:128, 0:128]:
                                    e_.tensor_copy(out=o, in_=i),
                                    waits=[evt] + wqbf_b[qsl].wr(), inc=c_vec)
                        psb[pslot].frees.append(evc)
                        if widx < 2:
                            odst = wdst[rt][:, ct4 * 128:(ct4 + 1) * 128]
                        else:
                            odst = wdst[ct4 * 128:(ct4 + 1) * 128, rt * 128:(rt + 1) * 128]
                        ev_s = p.add("sy", lambda e_, o=odst, i=WQBF2[qsl]:
                                     e_.dma_start(out=o, in_=i), waits=[evc], inc=c_wqs[qsl])
                        wqbf_b[qsl].frees.append(ev_s)
            ev_wqs_done = (c_wqs[0], c_wqs[0].count)
            ev_wqs_done2 = (c_wqs[1], c_wqs[1].count)

            ev_m0 = p.add("vec", lambda e_: e_.memset(SMALL[:, 64:128], 0.0),
                          waits=bar4, inc=c_vec)
            p.add("vec", lambda e_: e_.memset(CCROW, 0.0), inc=c_vec)
            for sc in range(NDT):
                t0 = sc * CH
                ev_x4 = emit_ln(t0, ([ev_m0] + bar4) if sc == 0 else [], outs=XLN4)
                for dt in range(NDT):
                    ev_mx = p.add("vec", lambda e_, o=SMALL[:, 66:67], i=XLN4[dt]:
                                  e_.tensor_reduce(o, i, mybir.AxisListType.X, ALU.max,
                                                   apply_absolute_value=True),
                                  waits=[ev_x4], inc=c_vec)
                    p.add("vec", lambda e_, o=SMALL[:, 64:65], a=SMALL[:, 64:65],
                          b=SMALL[:, 66:67]:
                          e_.tensor_max(o, a, b))
            ev_t = p.add("pe", lambda e_, o=PS[6][0:1, 0:128], i=SMALL[:, 64:65]:
                         e_.transpose(o, i, EYE[:, :]),
                         waits=[(c_vec, c_vec.count)] + psb[6].wr(), inc=c_pe)
            ev_g = p.add("vec", lambda e_, o=CCROW, i=PS[6][0:1, 0:128].rearrange("a (b c) -> a (b c)")
                         if False else PS[6][0:1, 0:128]:
                         e_.tensor_reduce(o[:, 0:1] if False else CCROW[:, 0:1], i,
                                          mybir.AxisListType.X, ALU.max),
                         waits=[ev_t], inc=c_vec)
            psb[6].frees.append(ev_g)
            ev_ccw = p.add("sy", lambda e_, o=cc1i[:, :], i=CCROW:
                           e_.dma_start(out=o, in_=i), waits=[ev_g], inc=c_cc)
            ev_cc1 = p.add("gp", lambda e_: e_.collective_compute(
                "AllReduce", ALU.max, replica_groups=[[0, 1, 2, 3, 4, 5, 6, 7]],
                ins=[cc1i[:, :]], outs=[cc1o[:, :]]),
                waits=[ev_ccw], inc=c_gp)
            ev_ccr = p.add("sy", lambda e_, o=CCROW, i=cc1o[:, :]:
                           e_.dma_start(out=o, in_=i),
                           waits=[ev_cc1] + ffn["ccrow_b"].wr(), inc=c_cc)
            ev_cl = p.add("sy", lambda e_, o=SMALL[0:1, 96:97],
                          i=clip_x[:].rearrange("(a b) -> a b", a=1):
                          e_.dma_start(out=o, in_=i), inc=c_cc)
            ev_sg = p.add("sca", lambda e_, o=SMALL[0:1, 97:98], i=SMALL[0:1, 96:97]:
                          e_.activation(out=o, in_=i, func=AF.Sigmoid),
                          waits=[ev_cl], inc=c_sca)
            ev_s1 = p.add("vec", lambda e_, o=SMALL[0:1, 98:99], a=SMALL[0:1, 97:98], b=CCROW[:, 0:1]:
                          e_.tensor_mul(o, a, b), waits=[ev_sg, ev_ccr], inc=c_vec)
            p.add("vec", lambda e_, o=SMALL[0:1, 98:99], i=SMALL[0:1, 98:99]:
                  e_.tensor_scalar_mul(o, i, 1.0 / QMAX))
            ev_rs1p = p.add("vec", lambda e_, o=SMALL[0:1, 99:100], i=SMALL[0:1, 98:99]:
                            e_.reciprocal(o, i), inc=c_vec)
            ev_onesf = p.add("vec", lambda e_: e_.memset(GUBSTG[0:1, :], 1.0),
                             waits=bar4, inc=c_vec)

            def bcast_col(src_row, dst_col, wt):
                evb = p.add("pe", lambda e_, o=PS[6][0:128, 0:1], l=GUBSTG[0:1, 0:128], r=src_row:
                            e_.matmul(o, l, r, start=True, stop=True),
                            waits=[wt, ev_onesf] + psb[6].wr(), inc=c_pe)
                evc2 = p.add("vec", lambda e_, o=dst_col, i=PS[6][0:128, 0:1]:
                             e_.tensor_copy(out=o, in_=i), waits=[evb], inc=c_vec)
                psb[6].frees.append(evc2)
                return evc2
            ev_s1c = bcast_col(SMALL[0:1, 98:99], SMALL[:, 67:68], ev_rs1p)
            ev_rs1c = bcast_col(SMALL[0:1, 99:100], SMALL[:, 68:69], ev_rs1p)
            dump(0, SMALL[:, 64:100], ev_rs1c)

            p.add("vec", lambda e_: e_.memset(SMALL[:, 65:66], 0.0), inc=c_vec)
            wg_ld, wu_ld = {}, {}

            def wg_load(sloti, it, extra=()):
                return p.add("sy", lambda e_, o=WGS[sloti], i=wqg_h[it]:
                             e_.dma_start(out=o, in_=i),
                             waits=ffn["wg_b"][sloti].wr() + list(extra), inc=c_wg[sloti])

            def wu_load(sloti, it, extra=()):
                return p.add("sy", lambda e_, o=WUS[sloti], i=wqu_h[it]:
                             e_.dma_start(out=o, in_=i),
                             waits=ffn["wu_b"][sloti].wr() + list(extra), inc=c_wu[sloti])

            for k3 in range(3):
                wg_ld[k3] = wg_load(k3, k3, extra=[ev_wqs_done, ev_wqs_done2])
                wu_ld[k3] = wu_load(k3, k3, extra=[ev_wqs_done, ev_wqs_done2])
            dump(7, WGS[0][:, 0:512], wg_ld[0], ffn["wg_b"][0])
            gq_pos = [0]
            n_hsp = 0
            for sc in range(NDT):
                t0 = sc * CH
                ev_x4 = emit_ln(t0, [], outs=XLN4, rstd_mul_col=SMALL[0:1, 68:69],
                                rstd_mul_ev=ev_rs1c)
                ev_xq = None
                for dt in range(NDT):
                    p.add("vec", lambda e_, o=XLN4[dt], i=XLN4[dt]:
                          e_.tensor_scalar(o, i, ROUND_C, -ROUND_C, ALU.add, ALU.add),
                          waits=([ev_x4] + ffn["xq_b"].wr()) if dt == 0 else [])
                    p.add("vec", lambda e_, o=XLN4[dt], i=XLN4[dt]:
                          e_.tensor_scalar(o, i, QMAX, -QMAX - 1.0, ALU.min, ALU.max))
                    ev_xq = p.add("vec", lambda e_, o=XQT[dt], i=XLN4[dt], s=SMALL[:, 67:68]:
                                  e_.tensor_scalar_mul(o, i, s),
                                  waits=[ev_s1c], inc=c_vec)
                    if sc == 0 and dt == 0:
                        dump(1, XQT[0], ev_xq, ffn["xq_b"])
                for it in range(NIT):
                    slg = gq_pos[0] % 3
                    evwg, evwu = wg_ld[slg], wu_ld[slg]
                    bkg = it % 2
                    bku = 2 + it % 2
                    wvg = psb[bkg].wr()
                    wvu = psb[bku].wr()
                    ev_mg = None
                    ev_mu = None
                    for dt in range(NDT):
                        ev_mg = p.add("pe", lambda e_, o=PS[bkg][0:128, 0:512],
                                      l=WGS[slg][:, dt * 128:(dt + 1) * 128], r=XQT[dt],
                                      s=(dt == 0), st=(dt == NDT - 1):
                                      e_.matmul(o, l, r, start=s, stop=st),
                                      waits=([evwg, ev_xq] + wvg) if dt == 0 else [],
                                      inc=(c_pe if dt == NDT - 1 else None))
                    ffn["wg_b"][slg].frees.append(ev_mg)
                    for dt in range(NDT):
                        ev_mu = p.add("pe", lambda e_, o=PS[bku][0:128, 0:512],
                                      l=WUS[slg][:, dt * 128:(dt + 1) * 128], r=XQT[dt],
                                      s=(dt == 0), st=(dt == NDT - 1):
                                      e_.matmul(o, l, r, start=s, stop=st),
                                      waits=([evwu, ev_xq] + wvu) if dt == 0 else [],
                                      inc=(c_pe if dt == NDT - 1 else None))
                    ffn["wu_b"][slg].frees.append(ev_mu)
                    tgt = gq_pos[0] + 3
                    gq_pos[0] += 1
                    if tgt < NIT * NDT:
                        wg_ld[tgt % 3] = wg_load(tgt % 3, tgt % NIT)
                        wu_ld[tgt % 3] = wu_load(tgt % 3, tgt % NIT)
                    ev_sg2 = p.add("sca", lambda e_, o=SGT, i=PS[bkg][0:128, 0:512]:
                                   e_.activation(out=o, in_=i, func=AF.Silu),
                                   waits=[ev_mg] + ffn["sg_b"].wr(), inc=c_sca)
                    psb[bkg].frees.append(ev_sg2)
                    ev_up = p.add("vec", lambda e_, o=UPT, i=PS[bku][0:128, 0:512]:
                                  e_.tensor_copy(out=o, in_=i),
                                  waits=[ev_mu] + ffn["up_b"].wr(), inc=c_vec)
                    psb[bku].frees.append(ev_up)
                    hsl = it % 2
                    ev_h = p.add("vec", lambda e_, o=HT4[hsl], a=SGT, b=UPT:
                                 e_.tensor_mul(o, a, b),
                                 waits=[ev_sg2, ev_up] + ffn["h_b"][hsl].wr(), inc=c_vec)
                    if sc == 0 and it == 0:
                        dump(2, SGT, ev_sg2, ffn["sg_b"])
                        dump(3, UPT, ev_up, ffn["up_b"])
                        dump(4, HT4[hsl], ev_h, ffn["h_b"][hsl])
                    ffn["sg_b"].frees.append(ev_h)
                    ffn["up_b"].frees.append(ev_h)
                    ev_hs = p.add("sy", lambda e_,
                                  o=hsp_h[it * 128:(it + 1) * 128, sc * 512:(sc + 1) * 512],
                                  i=HT4[hsl]:
                                  e_.dma_start(out=o, in_=i), waits=[ev_h], inc=c_hsp)
                    n_hsp += 1
                    ffn["h_b"][hsl].frees.append(ev_hs)
                    ev_mx = p.add("vec", lambda e_, o=SMALL[:, 66:67], i=HT4[hsl]:
                                  e_.tensor_reduce(o, i, mybir.AxisListType.X, ALU.max,
                                                   apply_absolute_value=True),
                                  waits=[ev_h], inc=c_vec)
                    ev_mh = p.add("vec", lambda e_, o=SMALL[:, 65:66], a=SMALL[:, 65:66],
                                  b=SMALL[:, 66:67]:
                                  e_.tensor_max(o, a, b), inc=c_vec)
                    ffn["h_b"][hsl].frees.append(ev_mh)
                if sc == NDT - 1:
                    ffn["xq_b"].frees.append((c_pe, c_pe.count))

            dump(8, SMALL[:, 64:72], (c_vec, c_vec.count))
            ev_t2 = p.add("pe", lambda e_, o=PS[6][0:1, 0:128], i=SMALL[:, 65:66]:
                          e_.transpose(o, i, EYE[:, :]),
                          waits=[(c_vec, c_vec.count)] + psb[6].wr(), inc=c_pe)
            ev_g2 = p.add("vec", lambda e_, o=CCROW[:, 0:1], i=PS[6][0:1, 0:128]:
                          e_.tensor_reduce(o, i, mybir.AxisListType.X, ALU.max),
                          waits=[ev_t2] + ffn["ccrow_b"].wr(), inc=c_vec)
            psb[6].frees.append(ev_g2)
            ev_ccw2 = p.add("sy", lambda e_, o=cc2i[:, :], i=CCROW:
                            e_.dma_start(out=o, in_=i), waits=[ev_g2], inc=c_cc)
            ev_cc2 = p.add("gp", lambda e_: e_.collective_compute(
                "AllReduce", ALU.max, replica_groups=[[0, 1, 2, 3, 4, 5, 6, 7]],
                ins=[cc2i[:, :]], outs=[cc2o[:, :]]),
                waits=[ev_ccw2], inc=c_gp)
            ev_ccr2 = p.add("sy", lambda e_, o=CCROW, i=cc2o[:, :]:
                            e_.dma_start(out=o, in_=i), waits=[ev_cc2], inc=c_cc)
            ev_cl2 = p.add("sy", lambda e_, o=SMALL[0:1, 96:97],
                           i=clip_down[:].rearrange("(a b) -> a b", a=1):
                           e_.dma_start(out=o, in_=i), waits=[ev_ccr2], inc=c_cc)
            ev_sg3 = p.add("sca", lambda e_, o=SMALL[0:1, 97:98], i=SMALL[0:1, 96:97]:
                           e_.activation(out=o, in_=i, func=AF.Sigmoid),
                           waits=[ev_cl2], inc=c_sca)
            ev_s2 = p.add("vec", lambda e_, o=SMALL[0:1, 98:99], a=SMALL[0:1, 97:98],
                          b=CCROW[:, 0:1]:
                          e_.tensor_mul(o, a, b), waits=[ev_sg3, ev_ccr2], inc=c_vec)
            p.add("vec", lambda e_, o=SMALL[0:1, 98:99], i=SMALL[0:1, 98:99]:
                  e_.tensor_scalar_mul(o, i, 1.0 / QMAX))
            ev_rs2p = p.add("vec", lambda e_, o=SMALL[0:1, 99:100], i=SMALL[0:1, 98:99]:
                            e_.reciprocal(o, i), inc=c_vec)
            ev_s2c = bcast_col(SMALL[0:1, 98:99], SMALL[:, 69:70], ev_rs2p)
            ev_rs2c = bcast_col(SMALL[0:1, 99:100], SMALL[:, 70:71], ev_rs2p)
            dump(5, SMALL[:, 64:100], ev_rs2c)
            ev_hsp_all = (c_hsp, c_hsp.count)

            wd_ld = {}

            def wd_load2(sloti, it, extra=()):
                return p.add("sy", lambda e_, o=WDS[sloti][:, 0:1024],
                             i=wqd_h[it * 128:(it + 1) * 128, :]:
                             e_.dma_start(out=o, in_=i),
                             waits=ffn["wd_b"][sloti].wr() + list(extra), inc=c_wd[sloti])

            for k3 in range(2):
                wd_ld[k3] = wd_load2(k3, k3, extra=[ev_wqs_done, ev_wqs_done2])
            hl_ld = {}
            for sc in range(NDT):
                ev_bank4 = [None] * NDT
                for it in range(NIT):
                    hsl = it % 2
                    ev_hl = p.add("sy", lambda e_, o=HT4[hsl],
                                  i=hsp_h[it * 128:(it + 1) * 128, sc * 512:(sc + 1) * 512]:
                                  e_.dma_start(out=o, in_=i),
                                  waits=[ev_hsp_all] + ffn["h_b"][hsl].wr(), inc=c_hld[hsl])
                    p.add("vec", lambda e_, o=AF32[:, 0:512], i=HT4[hsl], s=SMALL[:, 70:71]:
                          e_.tensor_scalar_mul(o, i, s),
                          waits=[ev_hl, ev_rs2c] + ffn["hq_b"][hsl].wr())
                    ffn["h_b"][hsl].frees.append((c_vec, c_vec.count))
                    p.add("vec", lambda e_, o=AF32[:, 0:512], i=AF32[:, 0:512]:
                          e_.tensor_scalar(o, i, ROUND_C, -ROUND_C, ALU.add, ALU.add))
                    ev_hq = p.add("vec", lambda e_, o=HQ4[hsl], i=AF32[:, 0:512]:
                                  e_.tensor_scalar(o, i, QMAX, -QMAX - 1.0, ALU.min, ALU.max),
                                  inc=c_vec)
                    if sc == 0 and it == 0:
                        dump(6, HQ4[hsl], ev_hq, ffn["hq_b"][hsl])
                    sld = it % 2
                    evwd = wd_ld[sld]
                    for ht in range(NDT):
                        ev_md = p.add("pe", lambda e_, o=PS[ht][0:128, 0:512],
                                      l=WDS[sld][:, ht * 128:(ht + 1) * 128], r=HQ4[hsl],
                                      s=(it == 0), st=(it == NIT - 1):
                                      e_.matmul(o, l, r, start=s, stop=st),
                                      waits=([evwd, ev_hq] + (psb[ht].wr() if it == 0 else [])),
                                      inc=c_pe)
                        if it == NIT - 1:
                            ev_bank4[ht] = ev_md
                        if ht == NDT - 1:
                            ffn["wd_b"][sld].frees.append(ev_md)
                            ffn["hq_b"][hsl].frees.append(ev_md)
                    nit2 = it + 2
                    if nit2 < NIT:
                        wd_ld[nit2 % 2] = wd_load2(nit2 % 2, nit2)
                    elif sc < NDT - 1:
                        wd_ld[(nit2 - NIT) % 2] = wd_load2((nit2 - NIT) % 2, nit2 - NIT)
                t0 = sc * CH
                for ht in range(NDT):
                    ev_ra = p.add("vec", lambda e_, o=RES[:, ht * TCORE + t0:][:, :512],
                                  a=PS[ht][0:128, 0:512], s=SMALL[:, 69:70],
                                  b=RES[:, ht * TCORE + t0:][:, :512]:
                                  e_.scalar_tensor_tensor(out=o, in0=a, scalar=s, in1=b,
                                                          op0=ALU.mult, op1=ALU.add),
                                  waits=[ev_bank4[ht], ev_s2c], inc=c_vec)
                    psb[ht].frees.append(ev_ra)

        OSTG = [AF32[:, 0:1024], AF32[:, 1024:2048]]
        ostg_b = [Buf(), Buf()]
        n_out = 0
        last_vec_ev = (c_vec, c_vec.count)
        for seq in range(SEQN):
            for tt in range(S // 128):
                sl = (seq * (S // 128) + tt) % 2
                evlast = None
                for dt in range(NDT):
                    evc0 = p.add("vec", lambda e_, o=OSTG[sl][:, dt * 128:(dt + 1) * 128],
                                 i=RES[:, dt * TCORE + seq * S + tt * 128:][:, :128]:
                                 e_.tensor_copy(out=o, in_=i),
                                 waits=[last_vec_ev] + (ostg_b[sl].wr() if dt == 0 else []),
                                 inc=c_vec)
                    evt, evc = transpose_128(OSTG[sl][:, dt * 128:(dt + 1) * 128],
                                             OSTG[sl][:, dt * 128:(dt + 1) * 128],
                                             tcount % 2, [evc0])
                    tcount += 1
                    evlast = evc
                evd = p.add("sca", lambda e_, o=y_out[seq, tt * 128:(tt + 1) * 128, :], i=OSTG[sl]:
                            e_.dma_start(out=o, in_=i), waits=[evlast], inc=c_out[sl])
                n_out += 1
                ostg_b[sl].frees.append(evd)
        p.add("sy", lambda e_: e_.nop(),
              waits=[(c_out[0], c_out[0].count), (c_out[1], c_out[1].count)])

        with nc.Block() as blk:
            blk.tensor(lambda e_: p.emit("pe", e_))
            blk.vector(lambda e_: p.emit("vec", e_))
            blk.scalar(lambda e_: p.emit("sca", e_))
            blk.gpsimd(lambda e_: p.emit("gp", e_))
            blk.sync(lambda e_: p.emit("sy", e_))
    return nc, p.tally()


def _constants():
    eye = np.eye(128, dtype=np.float32)
    ones = np.ones((128, 128), dtype=BF)
    r = np.arange(128)[:, None]
    x = np.arange(896)[None, :]
    ptriu = (x >= r + 384).astype(BF)
    eh = np.zeros((NHEAD, D), dtype=BF)
    for f in range(D):
        eh[f // HDIM, f] = 1
    return eye, ones, ptriu, eh


def kernel(**inputs):
    x = np.asarray(inputs["L0_attn_x"], dtype=np.float32)
    assert x.shape == (B, S, D)
    eye, ones, ptriu, eh = _constants()
    weight_names = [k for k in inputs.keys() if k != "L0_attn_x"]
    nc, tally = build_program(_BLOCKS)
    in_maps = []
    for core in range(NCORE):
        m = {"x": np.ascontiguousarray(x[core * SEQN:(core + 1) * SEQN]),
             "c_eye": eye, "c_ones": ones, "c_ptriu": ptriu, "c_eh": eh}
        for k in weight_names:
            m[k] = np.ascontiguousarray(np.asarray(inputs[k], dtype=np.float32))
        in_maps.append(m)
    res = run_bass_kernel_spmd(nc, in_maps, core_ids=list(range(NCORE)))
    out = np.concatenate([r["y"].reshape(SEQN, S, D) for r in res.results], axis=0)
    return out.astype(np.float32)


# version 05
# speedup vs baseline: 1.1135x; 1.0161x over previous
import contextlib
import numpy as np

try:
    import concourse.bass as bass
except ImportError:
    import sys
    sys.path.insert(0, "/opt/trn_rl_repo")
    import concourse.bass as bass
import concourse.mybir as mybir
import ml_dtypes
from concourse.bass_utils import run_bass_kernel_spmd
from concourse.alu_op_type import AluOpType as ALU

F32 = mybir.dt.float32
BF16 = mybir.dt.bfloat16
AF = mybir.ActivationFunctionType

BF = ml_dtypes.bfloat16

NCORE = 8
B, S, D = 16, 2048, 1024
SEQN = 2
TCORE = SEQN * S
NDT = D // 128
CH = 512
NCH_SEQ = S // CH
INNER = 2048
NCT = INNER // 128
NST = 16
KCONV = 4
EXP_N = 8
IMOE = 1024
NHEAD = 16
HDIM = 64
IFFN = 2816
NIT = IFFN // 128
ROUND_C = float(3 * (1 << 22))
QMAX = 127.0

ENGS = ("pe", "vec", "sca", "gp", "sy")

_BLOCKS = 4
_LIMIT = None
_LIMIT2 = None
_SKIP_P1 = False
_SKIP_P2 = False
_LIMIT3 = None
_SKIP_P3 = False
_DEBUG = False


class Chan:
    __slots__ = ("name", "step", "count", "h", "engs", "is_dma", "unbarred")
    def __init__(self, name, step=1, is_dma=False):
        self.name, self.step, self.count = name, step, 0
        self.h = None
        self.engs = set()
        self.is_dma = is_dma
        self.unbarred = set()


class Buf:
    __slots__ = ("ready", "frees")
    def __init__(self):
        self.ready = None
        self.frees = []
    def wr(self):
        w = list(self.frees)
        self.frees = []
        return w


class Prog:
    def __init__(self):
        self.ops = {e: [] for e in ENGS}
        self.chans = []

    def chan(self, name, step=1, is_dma=False):
        c = Chan(name, step, is_dma)
        self.chans.append(c)
        return c

    def add(self, eng, fn, waits=(), inc=None, bar=False, nobar=False):
        ws = [w for w in waits if w is not None]
        if eng in ("vec", "sca") and not nobar:
            bar = True
        if bar and inc is None:
            inc = self.engine_chan[eng]
        ev = None
        if inc is not None:
            inc.count += inc.step
            inc.engs.add(eng)
            ev = (inc, inc.count)
            if nobar:
                inc.unbarred.add(inc.count)
        self.ops[eng].append((ws, fn, inc, bar, None if ev is None else ev[1]))
        return ev

    def emit(self, eng_name, eng):
        satisfied = {}
        pending_bar = None
        for ws, fn, inc, bar, val in self.ops[eng_name]:
            if pending_bar is not None:
                c, v = pending_bar
                if satisfied.get(c, -1) < v:
                    eng.wait_ge(c.h, v)
                    satisfied[c] = v
                pending_bar = None
            for (c, v) in ws:
                if not c.is_dma and c.engs == {eng_name} and v not in c.unbarred:
                    continue
                if satisfied.get(c, -1) >= v:
                    continue
                eng.wait_ge(c.h, v)
                satisfied[c] = v
            inst = fn(eng)
            if inc is not None:
                assert inst is not None
                inst.then_inc(inc.h, inc.step)
            if bar:
                assert inc is not None
                pending_bar = (inc, val)

    def tally(self):
        return {e: len(self.ops[e]) for e in ENGS}


class Carver:
    def __init__(self, total):
        self.off, self.total = 0, total
    def take(self, n, align=8):
        a = (self.off + align - 1) // align * align
        self.off = a + n
        assert self.off <= self.total, f"carve overflow {self.off}>{self.total}"
        return a


def build_program(n_blocks):
    nc = bass.Bass("TRN2", target_bir_lowering=False, debug=False,
                   detect_race_conditions=False)
    p = Prog()

    c_pe = p.chan("pe")
    c_vec = p.chan("vec")
    c_sca = p.chan("sca")
    c_gp = p.chan("gp")
    p.engine_chan = {"pe": c_pe, "vec": c_vec, "sca": c_sca, "gp": c_gp, "sy": None}
    c_dg = p.chan("dg", 16, True)
    c_stg = [p.chan(f"stg{i}", 16, True) for i in range(2)]
    c_wost = p.chan("wost", 16, True)
    c_misc0 = p.chan("m0", 16, True)
    c_wot = [p.chan(f"wot{i}", 16, True) for i in range(2)]
    c_ring = [p.chan(f"rng{i}", 16, True) for i in range(4)]
    c_gub = p.chan("gub", 16, True)
    c_wrow = p.chan("wrow", 16, True)
    c_misc2 = p.chan("m2", 16, True)
    c_out = [p.chan(f"out{i}", 16, True) for i in range(2)]
    c_pre = p.chan("pre", 16, True)

    x_in = nc.dram_tensor("x", [SEQN, S, D], F32, kind="ExternalInput").ap()
    y_out = nc.dram_tensor("y", [SEQN, S, D], F32, kind="ExternalOutput").ap()

    def din(name, shape):
        return nc.dram_tensor(name, shape, F32, kind="ExternalInput").ap()

    w_in = din("L0_attn_W_in", [2 * INNER, D])
    conv_w = din("L0_attn_conv_w", [INNER, 1, KCONV])
    conv_b = din("L0_attn_conv_b", [INNER])
    a_log = din("L0_attn_A_log", [INNER, NST])
    d_vec = din("L0_attn_D", [INNER])
    w_out = din("L0_attn_W_out", [D, INNER])
    rw = din("L0_ffn_router_weight", [EXP_N, D])
    rb = din("L0_ffn_router_bias", [EXP_N])
    gup = din("L0_ffn_gate_up_proj", [EXP_N, D, 2 * IMOE])
    gub = din("L0_ffn_gate_up_bias", [EXP_N, 2 * IMOE])
    dwn = din("L0_ffn_down_proj", [EXP_N, IMOE, D])
    dwb = din("L0_ffn_down_bias", [EXP_N, D])
    wqk = din("L1_attn_Wqk", [D, 2 * D])
    bqk = din("L1_attn_bqk", [2 * D])
    wv = din("L1_attn_Wv", [D, D])
    bv = din("L1_attn_bv", [D])
    wn = din("L1_attn_Wn", [D, NHEAD])
    bn = din("L1_attn_bn", [NHEAD])
    wo = din("L1_attn_Wo", [D, D])
    bo = din("L1_attn_bo", [D])
    clip_x = din("L1_ffn_clip_x", [1])
    clip_down = din("L1_ffn_clip_down", [1])
    w_gate = din("L1_ffn_w_gate", [IFFN, D])
    w_up = din("L1_ffn_w_up", [IFFN, D])
    w_down = din("L1_ffn_w_down", [D, IFFN])

    c_eye = nc.dram_tensor("c_eye", [128, 128], F32, kind="ExternalInput").ap()
    c_ptriu = nc.dram_tensor("c_ptriu", [128, 896], BF16, kind="ExternalInput").ap()
    c_eh = nc.dram_tensor("c_eh", [NHEAD, D], BF16, kind="ExternalInput").ap()
    c_ones = nc.dram_tensor("c_ones", [128, 128], BF16, kind="ExternalInput").ap()

    woutt_h = nc.dram_tensor("woutt_h", [INNER, D], BF16, kind="Internal").ap()
    wqg_h = nc.dram_tensor("wqg_h", [NIT, 128, NDT * 128], BF16, kind="Internal").ap()
    wqu_h = nc.dram_tensor("wqu_h", [NIT, 128, NDT * 128], BF16, kind="Internal").ap()
    wqkb_h = nc.dram_tensor("wqkb_h", [D, 2 * D], BF16, kind="Internal").ap()
    wvb_h = nc.dram_tensor("wvb_h", [D, D], BF16, kind="Internal").ap()
    wob_h = nc.dram_tensor("wob_h", [D, D], BF16, kind="Internal").ap()
    gupb_h = nc.dram_tensor("gupb_h", [EXP_N, D, 2 * IMOE], BF16, kind="Internal").ap()
    dwnb_h = nc.dram_tensor("dwnb_h", [EXP_N, IMOE, D], BF16, kind="Internal").ap()
    wqd_h = nc.dram_tensor("wqd_h", [IFFN, D], BF16, kind="Internal").ap()
    hsp_h = nc.dram_tensor("hsp_h", [IFFN, TCORE], BF16, kind="Internal").ap()
    cc1i = nc.dram_tensor("cc1i", [1, 16], F32, kind="Internal").ap()
    cc1o = nc.dram_tensor("cc1o", [1, 16], F32, kind="Internal", addr_space="Shared").ap()
    cc2i = nc.dram_tensor("cc2i", [1, 16], F32, kind="Internal").ap()
    cc2o = nc.dram_tensor("cc2o", [1, 16], F32, kind="Internal", addr_space="Shared").ap()
    dbg = (nc.dram_tensor("dbg", [24, 128, 512], F32, kind="ExternalOutput").ap()
           if _DEBUG else None)

    with contextlib.ExitStack() as st:
        e = st.enter_context
        RES = e(nc.sbuf_tensor("res", [128, NDT * TCORE], BF16))
        AW = e(nc.sbuf_tensor("aw", [128, 32768], BF16))
        ABF = e(nc.sbuf_tensor("abf", [128, 18944], BF16))
        AF32 = e(nc.sbuf_tensor("af32", [128, 4104], F32))
        STATS = e(nc.sbuf_tensor("stats", [128, 2936], F32))
        LNBF = e(nc.sbuf_tensor("lnbf", [128, 1024], BF16))
        EYE = e(nc.sbuf_tensor("eye", [128, 128], F32))
        ONESB = e(nc.sbuf_tensor("onesb", [128, 128], BF16))
        PTRIU = e(nc.sbuf_tensor("ptriu", [128, 896], BF16))
        EHB = e(nc.sbuf_tensor("ehb", [NHEAD, D], BF16))
        SMALL = e(nc.sbuf_tensor("small", [128, 128], F32))
        DBB = e(nc.sbuf_tensor("dbb", [EXP_N, IMOE], BF16))
        GUBSTG = e(nc.sbuf_tensor("gubstg", [16, 128], F32))
        WTB = e(nc.sbuf_tensor("wtb", [EXP_N, 512], BF16))
        RWTB = e(nc.sbuf_tensor("rwtb", [128, 64], BF16))
        RBROW = e(nc.sbuf_tensor("rbrow", [1, 8], BF16))
        WROW = e(nc.sbuf_tensor("wrow", [1, 512], BF16))
        RNT = e(nc.sbuf_tensor("rnt", [16, 512], BF16))
        WNB = e(nc.sbuf_tensor("wnb", [128, 128], BF16))
        BNROW = e(nc.sbuf_tensor("bnrow", [1, 16], BF16))
        BVROW = e(nc.sbuf_tensor("bvrow", [1, 1024], BF16))

        PS = [e(nc.psum_tensor(f"ps{i}", [128, 512], F32)) for i in range(8)]
        psb = [Buf() for _ in range(8)]

        for c in p.chans:
            c.h = e(nc.semaphore(c.name))

        def dump(slot, ap, ev, buf=None):
            if not _DEBUG:
                return
            pp, ff = ap.shape[0], ap.shape[1]
            evd = p.add("gp", lambda e_, o=dbg[slot, 0:pp, 0:ff], i=ap:
                        e_.dma_start(out=o, in_=i), waits=[ev], inc=c_dg)
            if buf is not None:
                buf.frees.append(evd)

        O_ANEG, O_D, O_CW, O_CB = 0, 256, 272, 336
        O_HST = 352
        LNR = [888 + i * 512 for i in range(4)]
        ln_mean_b = LNBF[0:1, 0:512]
        ln_rstd_b = LNBF[0:1, 512:1024]
        O_TAIL = 16

        p.add("sy", lambda e_: e_.dma_start(out=EYE[:, :], in_=c_eye[:, :]), inc=c_misc0)
        p.add("sy", lambda e_: e_.dma_start(out=ONESB[:, :], in_=c_ones[:, :]), inc=c_misc0)

        for e2 in range(EXP_N):
            p.add("gp", lambda e_, o=gupb_h[e2], i=gup[e2]:
                  e_.dma_start(out=o, in_=i), inc=c_pre)
            p.add("gp", lambda e_, o=dwnb_h[e2], i=dwn[e2]:
                  e_.dma_start(out=o, in_=i), inc=c_pre)
        p.add("gp", lambda e_: e_.dma_start(out=wqkb_h[:, :], in_=wqk[:, :]), inc=c_pre)
        p.add("gp", lambda e_: e_.dma_start(out=wvb_h[:, :], in_=wv[:, :]), inc=c_pre)
        p.add("gp", lambda e_: e_.dma_start(out=wob_h[:, :], in_=wo[:, :]), inc=c_pre)
        ev_pre_all = (c_pre, c_pre.count)
        p.add("sy", lambda e_: e_.dma_start(out=PTRIU[:, :], in_=c_ptriu[:, :]), inc=c_misc0)
        p.add("sy", lambda e_: e_.dma_start(out=EHB[:, :], in_=c_eh[:, :]), inc=c_misc0)

        for ct in range(NCT):
            p.add("sy", lambda e_, o=STATS[:, O_CW + ct * 4: O_CW + ct * 4 + 4],
                  i=conv_w[ct * 128:(ct + 1) * 128, 0, :]:
                  e_.dma_start(out=o, in_=i), inc=c_misc0)
            p.add("sy", lambda e_, o=STATS[:, O_CB + ct: O_CB + ct + 1],
                  i=conv_b[ct * 128:(ct + 1) * 128].rearrange("(p one) -> p one", one=1):
                  e_.dma_start(out=o, in_=i), inc=c_misc0)
            p.add("sy", lambda e_, o=STATS[:, O_D + ct: O_D + ct + 1],
                  i=d_vec[ct * 128:(ct + 1) * 128].rearrange("(p one) -> p one", one=1):
                  e_.dma_start(out=o, in_=i), inc=c_misc0)
            p.add("sy", lambda e_, o=STATS[:, O_ANEG + ct * NST: O_ANEG + (ct + 1) * NST],
                  i=a_log[ct * 128:(ct + 1) * 128, :]:
                  e_.dma_start(out=o, in_=i), inc=c_misc0)
        ev_misc0 = (c_misc0, c_misc0.count)
        ev_eye = ev_ones = ev_cw = ev_misc0
        ev_aneg = None
        for ct in range(NCT):
            evx = p.add("sca", lambda e_, o=SMALL[:, 0:NST],
                        i=STATS[:, O_ANEG + ct * NST: O_ANEG + (ct + 1) * NST]:
                        e_.activation(out=o, in_=i, func=AF.Exp),
                        waits=[ev_misc0, ev_aneg], inc=c_sca)
            ev_aneg = p.add("vec", lambda e_, o=STATS[:, O_ANEG + ct * NST: O_ANEG + (ct + 1) * NST],
                            i=SMALL[:, 0:NST]:
                            e_.tensor_scalar_mul(o, i, -1.0), waits=[evx], inc=c_vec)

        STG = [AF32[:, 0:1024], AF32[:, 1024:2048]]
        stg_b = [Buf(), Buf()]

        def transpose_128(src_ap, dst_ap, pslot, waits, cast_buf=None):
            pst = PS[6 + pslot]
            evt = p.add("pe", lambda e_, o=pst[0:128, 0:128], i=src_ap:
                        e_.transpose(o, i, EYE[:, :]),
                        waits=list(waits) + psb[6 + pslot].wr() + [ev_misc0], inc=c_pe)
            evc = p.add("vec", lambda e_, o=dst_ap, i=pst[0:128, 0:128]:
                        e_.tensor_copy(out=o, in_=i), waits=[evt], inc=c_vec, bar=True)
            psb[6 + pslot].frees.append(evc)
            return evt, evc

        tcount = 0
        for seq in range(SEQN):
            for tt in range(S // 128):
                sl = (seq * (S // 128) + tt) % 2
                evl = p.add("sy", lambda e_, o=STG[sl], i=x_in[seq, tt * 128:(tt + 1) * 128, :]:
                            e_.dma_start(out=o, in_=i),
                            waits=stg_b[sl].wr(), inc=c_stg[sl])
                for dt in range(NDT):
                    evt, evc = transpose_128(
                        STG[sl][:, dt * 128:(dt + 1) * 128],
                        RES[:, dt * TCORE + seq * S + tt * 128:][:, :128],
                        tcount % 2, [evl])
                    tcount += 1
                    if dt == NDT - 1:
                        stg_b[sl].frees.append(evt)
        ev_res_ready = (c_vec, c_vec.count)

        ev_winT = None
        for ct2 in range(2 * NCT):
            sl = ct2 % 2
            evl = p.add("sy", lambda e_, o=STG[sl], i=w_in[ct2 * 128:(ct2 + 1) * 128, :]:
                        e_.dma_start(out=o, in_=i), waits=stg_b[sl].wr(), inc=c_stg[sl])
            for dt in range(NDT):
                evt, evc = transpose_128(
                    STG[sl][:, dt * 128:(dt + 1) * 128],
                    AW[:, dt * 4096 + ct2 * 128:][:, :128],
                    tcount % 2, [evl])
                tcount += 1
                if dt == NDT - 1:
                    stg_b[sl].frees.append(evt)
                ev_winT = evc

        WOTMP = ABF[:, 0:128]
        wotmp_b = Buf()
        ev_woutt = None
        for dt in range(NDT):
            for half in range(2):
                sl = (dt * 2 + half) % 2
                evl = p.add("sy", lambda e_, o=STG[sl],
                            i=w_out[dt * 128:(dt + 1) * 128, half * 1024:(half + 1) * 1024]:
                            e_.dma_start(out=o, in_=i), waits=stg_b[sl].wr(), inc=c_stg[sl])
                for j in range(8):
                    ct2 = half * 8 + j
                    evt, evc = transpose_128(
                        STG[sl][:, j * 128:(j + 1) * 128], WOTMP,
                        tcount % 2, [evl] + wotmp_b.wr())
                    tcount += 1
                    if j == 7:
                        stg_b[sl].frees.append(evt)
                    ev_woutt = p.add("sy", lambda e_,
                                     o=woutt_h[ct2 * 128:(ct2 + 1) * 128, dt * 128:(dt + 1) * 128],
                                     i=WOTMP:
                                     e_.dma_start(out=o, in_=i), waits=[evc], inc=c_wost)
                    wotmp_b.frees.append(ev_woutt)

        cbf = Carver(18944)
        XLN_OFF = cbf.take(8 * CH)
        YF_OFF = cbf.take(NCT * CH)
        HT_OFF = cbf.take(2 * CH)
        TREE_OFF = cbf.take(5 * CH)
        YACC_OFF = cbf.take(CH)
        RSIL_OFF = cbf.take(CH)
        WOT_OFF = cbf.take(2 * 1024)

        XLNT = [ABF[:, XLN_OFF + dt * CH:XLN_OFF + (dt + 1) * CH] for dt in range(NDT)]
        TREE = [ABF[:, TREE_OFF + i * CH:TREE_OFF + (i + 1) * CH] for i in range(5)]
        MB, RBB, TMPB = TREE[0], TREE[1], TREE[2]
        SQ = [ABF[:, YF_OFF + i * CH:YF_OFF + (i + 1) * CH] for i in range(2)]
        HT = [ABF[:, HT_OFF + i * CH:HT_OFF + (i + 1) * CH] for i in range(2)]
        YACC = ABF[:, YACC_OFF:YACC_OFF + CH]
        RSIL = ABF[:, RSIL_OFF:RSIL_OFF + CH]
        WOT = [ABF[:, WOT_OFF + i * 1024:WOT_OFF + (i + 1) * 1024] for i in range(2)]

        caf = Carver(4104)
        caf.off = 2048
        XS_OFF = caf.take(520)
        CACC_OFF = caf.take(512)
        AT_OFF = caf.take(2 * 512)
        XS = AF32[:, XS_OFF:XS_OFF + 515]
        CACC = AF32[:, CACC_OFF:CACC_OFF + 512]
        ATIL = [AF32[:, AT_OFF + i * 512:AT_OFF + (i + 1) * 512] for i in range(2)]
        XCONV = AF32[:, 0:512]
        DELTA = AF32[:, 512:1024]
        BX = AF32[:, 1024:1536]

        sq_b = [Buf(), Buf()]
        atil_b = [Buf(), Buf()]
        ht_b = [Buf(), Buf()]
        tree_b = [Buf() for _ in range(5)]
        yacc_b = Buf()
        rsil_b = Buf()
        cacc_b = Buf()
        xconv_b = Buf()
        xs_b = Buf()
        yf_b = [Buf() for _ in range(NCT)]
        wot_b = [Buf(), Buf()]
        xln_b = Buf()

        def emit_ln(t0, res_ready_evs, rstd_mul_col=None, rstd_mul_ev=None, outs=None):
            bsum, bsq = 0, 1
            w0 = psb[bsum].wr() + res_ready_evs
            ev_s1 = None
            for dt in range(NDT):
                ev_s1 = p.add("pe", lambda e_, o=PS[bsum][0:1, 0:512],
                      l=ONESB[:, 0:1], r=RES[:, dt * TCORE + t0:][:, :CH], s=(dt == 0),
                      st=(dt == NDT - 1):
                      e_.matmul(o, l, r, start=s, stop=st),
                      waits=(w0 + [ev_ones]) if dt == 0 else [],
                      inc=(c_pe if dt == NDT - 1 else None))
            w1 = psb[bsq].wr()
            ev_s2 = None
            for dt in range(NDT):
                evq = p.add("sca", lambda e_, o=SQ[dt % 2], i=RES[:, dt * TCORE + t0:][:, :CH]:
                            e_.activation(out=o, in_=i, func=AF.Square),
                            waits=sq_b[dt % 2].wr() + res_ready_evs, inc=c_sca)
                ev_s2 = p.add("pe", lambda e_, o=PS[bsq][0:1, 0:512], l=ONESB[:, 0:1], r=SQ[dt % 2],
                              s=(dt == 0), st=(dt == NDT - 1):
                              e_.matmul(o, l, r, start=s, stop=st),
                              waits=[evq] + (w1 if dt == 0 else []), inc=c_pe)
                sq_b[dt % 2].frees.append(ev_s2)
            R0, R1, R2, R3 = (STATS[0:1, LNR[i]:LNR[i] + 512] for i in range(4))
            p.add("vec", lambda e_, o=R0, i=PS[bsum][0:1, 0:512]:
                  e_.tensor_scalar_mul(o, i, -1.0 / D), waits=[ev_s1], bar=True)
            p.add("vec", lambda e_, o=R1, a=R0: e_.tensor_mul(o, a, a))
            p.add("vec", lambda e_, o=R2, i=PS[bsq][0:1, 0:512], m=R1:
                  e_.scalar_tensor_tensor(out=o, in0=i, scalar=1.0 / D, in1=m,
                                          op0=ALU.mult, op1=ALU.subtract), waits=[ev_s2], bar=True)
            ev_fr = p.add("vec", lambda e_, o=R3, i=R2:
                          e_.tensor_scalar_add(o, i, 1e-5), inc=c_vec)
            psb[bsum].frees.append(ev_fr)
            psb[bsq].frees.append(ev_fr)
            ev_sq = p.add("sca", lambda e_, o=R1, i=R3:
                          e_.activation(out=o, in_=i, func=AF.Sqrt), waits=[ev_fr], inc=c_sca)
            p.add("vec", lambda e_, o=R2, i=R1: e_.reciprocal(o, i), waits=[ev_sq], bar=True)
            p.add("vec", lambda e_, o=R1, a=R2: e_.tensor_mul(o, a, a))
            p.add("vec", lambda e_, o=R1, a=R3, b=R1: e_.tensor_mul(o, a, b))
            p.add("vec", lambda e_, o=R1, i=R1:
                  e_.tensor_scalar(o, i, -0.5, 1.5, ALU.mult, ALU.add))
            p.add("vec", lambda e_, o=R1, a=R2, b=R1: e_.tensor_mul(o, a, b))
            if rstd_mul_col is not None:
                p.add("vec", lambda e_, o=R1, i=R1, s=rstd_mul_col:
                      e_.tensor_scalar_mul(o, i, s), waits=[rstd_mul_ev])
            ev_nm = p.add("vec", lambda e_, o=ln_mean_b, i=R0:
                          e_.tensor_copy(out=o, in_=i), inc=c_vec)
            ev_rs = p.add("vec", lambda e_, o=ln_rstd_b, i=R1:
                          e_.tensor_copy(out=o, in_=i), inc=c_vec)
            bm, br = 2, 3
            evbm = p.add("pe", lambda e_, o=PS[bm][0:128, 0:512], l=ONESB[0:1, :], r=ln_mean_b:
                         e_.matmul(o, l, r, start=True, stop=True),
                         waits=[ev_nm] + psb[bm].wr(), inc=c_pe)
            evbr = p.add("pe", lambda e_, o=PS[br][0:128, 0:512], l=ONESB[0:1, :], r=ln_rstd_b:
                         e_.matmul(o, l, r, start=True, stop=True),
                         waits=[ev_rs] + psb[br].wr(), inc=c_pe)
            evMB = p.add("vec", lambda e_, o=MB, i=PS[bm][0:128, 0:512]:
                         e_.tensor_copy(out=o, in_=i), waits=[evbm] + tree_b[0].wr(), inc=c_vec,
                         bar=True)
            evRB = p.add("vec", lambda e_, o=RBB, i=PS[br][0:128, 0:512]:
                         e_.tensor_copy(out=o, in_=i), waits=[evbr] + tree_b[1].wr(), inc=c_vec,
                         bar=True)
            psb[bm].frees.append(evMB)
            psb[br].frees.append(evRB)
            ev_last = None
            otiles = outs if outs is not None else XLNT
            for dt in range(NDT):
                p.add("vec", lambda e_, o=TMPB, a=RES[:, dt * TCORE + t0:][:, :CH], b=MB:
                      e_.tensor_add(o, a, b),
                      waits=([evMB, evRB] + tree_b[2].wr()) if dt == 0 else [])
                ev_last = p.add("vec", lambda e_, o=otiles[dt], a=TMPB, b=RBB:
                                e_.tensor_mul(o, a, b),
                                inc=(c_vec if dt == NDT - 1 else None))
            tree_b[0].frees.append(ev_last)
            tree_b[1].frees.append(ev_last)
            tree_b[2].frees.append(ev_last)
            return ev_last

        carry_ev = [[None] * NCT for _ in range(SEQN)]
        tail_ev = [None] * NCT

        ev_hst0 = p.add("vec", lambda e_: e_.memset(STATS[:, O_HST:O_HST + SEQN * NCT * NST], 0.0),
                        inc=c_vec)

        LSEQ, LCHK, LCT = _LIMIT if _LIMIT is not None else (SEQN, NCH_SEQ, NCT)
        if _SKIP_P1:
            LSEQ = 0
        if n_blocks >= 1:
            for seq in range(LSEQ):
                ev_tz = p.add("vec", lambda e_: e_.memset(SMALL[:, O_TAIL:O_TAIL + NCT * 3], 0.0),
                              inc=c_vec)
                for ct in range(NCT):
                    tail_ev[ct] = ev_tz
                for chk in range(LCHK):
                    t0 = seq * S + chk * CH
                    rdy = [ev_res_ready] if (seq == 0 and chk == 0) else []
                    ev_xln = emit_ln(t0, rdy + xln_b.wr())

                    wot_ld = [None] * LCT
                    for ct in range(min(2, LCT)):
                        sl = ct % 2
                        wot_ld[ct] = p.add("sy",
                                           lambda e_, o=WOT[sl], i=woutt_h[ct * 128:(ct + 1) * 128, :]:
                                           e_.dma_start(out=o, in_=i),
                                           waits=wot_b[sl].wr() + [ev_woutt], inc=c_wot[sl])

                    for ct in range(LCT):
                        bx_m = ct % 2
                        bx_r = 4 + ct % 2
                        ev_mmx = None
                        for dt in range(NDT):
                            ev_mmx = p.add("pe", lambda e_, o=PS[bx_m][0:128, 0:512],
                                  l=AW[:, dt * 4096 + ct * 128:][:, :128], r=XLNT[dt], s=(dt == 0),
                                  st=(dt == NDT - 1):
                                  e_.matmul(o, l, r, start=s, stop=st),
                                  waits=([ev_xln, ev_winT] + psb[bx_m].wr()) if dt == 0 else [],
                                  inc=(c_pe if dt == NDT - 1 else None))
                        ev_mmr = None
                        for dt in range(NDT):
                            ev_mmr = p.add("pe", lambda e_, o=PS[bx_r][0:128, 0:512],
                                  l=AW[:, dt * 4096 + (NCT + ct) * 128:][:, :128], r=XLNT[dt],
                                  s=(dt == 0), st=(dt == NDT - 1):
                                  e_.matmul(o, l, r, start=s, stop=st),
                                  waits=psb[bx_r].wr() if dt == 0 else [],
                                  inc=(c_pe if dt == NDT - 1 else None))

                        ev_xs = p.add("vec", lambda e_, o=XS[:, 3:515], i=PS[bx_m][0:128, 0:512]:
                                      e_.tensor_copy(out=o, in_=i),
                                      waits=[ev_mmx] + xs_b.wr(), inc=c_vec, bar=True)
                        if seq == 0 and chk == 2 and ct == 0:
                            dump(18, XS[:, 3:515], ev_xs, xs_b)
                        psb[bx_m].frees.append(ev_xs)
                        p.add("vec", lambda e_, o=XS[:, 0:3],
                              i=SMALL[:, O_TAIL + ct * 3:O_TAIL + ct * 3 + 3]:
                              e_.tensor_copy(out=o, in_=i), waits=[tail_ev[ct]])
                        p.add("vec", lambda e_, o=CACC, i=XS[:, 0:512],
                              s=STATS[:, O_CW + ct * 4:O_CW + ct * 4 + 1]:
                              e_.tensor_scalar_mul(o, i, s),
                              waits=cacc_b.wr() + [ev_cw])
                        for k in (1, 2, 3):
                            p.add("vec", lambda e_, o=CACC, i=XS[:, k:k + 512],
                                  s=STATS[:, O_CW + ct * 4 + k:O_CW + ct * 4 + k + 1], m=CACC:
                                  e_.scalar_tensor_tensor(out=o, in0=i, scalar=s, in1=m,
                                                          op0=ALU.mult, op1=ALU.add),
                                  inc=(c_vec if k == 3 else None))
                        ev_conv = (c_vec, c_vec.count)
                        tail_ev[ct] = p.add("vec", lambda e_,
                                            o=SMALL[:, O_TAIL + ct * 3:O_TAIL + ct * 3 + 3],
                                            i=XS[:, 512:515]:
                                            e_.tensor_copy(out=o, in_=i), inc=c_vec)

                        ev_xc = p.add("sca", lambda e_, o=XCONV, i=CACC,
                                      b=STATS[:, O_CB + ct:O_CB + ct + 1]:
                                      e_.activation(out=o, in_=i, func=AF.Silu, bias=b),
                                      waits=[ev_conv] + xconv_b.wr(), inc=c_sca)
                        if seq == 0 and chk == 2 and ct == 0:
                            dump(19, XCONV, ev_xc, xconv_b)
                        ev_rsl = p.add("sca", lambda e_, o=RSIL, i=PS[bx_r][0:128, 0:512]:
                                       e_.activation(out=o, in_=i, func=AF.Silu),
                                       waits=[ev_mmr] + rsil_b.wr(), inc=c_sca, bar=True)
                        psb[bx_r].frees.append(ev_rsl)
                        SPD = XS[:, 0:512]
                        ev_ab = p.add("sca", lambda e_, o=SPD, i=XCONV:
                                      e_.activation(out=o, in_=i, func=AF.Abs),
                                      waits=[ev_conv], inc=c_sca)
                        ev_u = p.add("sca", lambda e_, o=DELTA, i=SPD:
                                     e_.activation(out=o, in_=i, func=AF.Exp, scale=-1.0),
                                     waits=[ev_ab], inc=c_sca, nobar=True)
                        ev_w = p.add("vec", lambda e_, o=SPD, i=DELTA:
                                     e_.tensor_scalar_add(o, i, 1.0), waits=[ev_u], inc=c_vec)
                        ev_l = p.add("sca", lambda e_, o=DELTA, i=SPD:
                                     e_.activation(out=o, in_=i, func=AF.Ln),
                                     waits=[ev_w], inc=c_sca, nobar=True)
                        ev_r = p.add("vec", lambda e_, o=CACC, i=XCONV:
                                     e_.tensor_scalar_max(o, i, 0.0),
                                     waits=[ev_xc] + cacc_b.wr(), inc=c_vec)
                        ev_dl = p.add("vec", lambda e_, o=SPD, a=DELTA, b=CACC:
                                      e_.tensor_add(o, a, b), waits=[ev_l], inc=c_vec)
                        cacc_b.frees.append(ev_dl)
                        ev_bx = p.add("sca", lambda e_, o=BX, i=XCONV:
                                      e_.activation(out=o, in_=i, func=AF.Square), inc=c_sca,
                                      nobar=True)
                        if seq == 0 and chk == 2 and ct == 0:
                            dump(20, SPD, ev_dl, xs_b)
                            dump(21, BX, ev_bx)
                            dump(22, RSIL, ev_rsl, rsil_b)

                        ev_h_even = None
                        ev_ysum = None
                        lev = {}
                        for n in range(NST):
                            asl = n % 2
                            hsl = n % 2
                            ev_a = p.add("sca", lambda e_, o=ATIL[asl], i=SPD,
                                         s=STATS[:, O_ANEG + ct * NST + n:O_ANEG + ct * NST + n + 1]:
                                         e_.activation(out=o, in_=i, func=AF.Exp, scale=s),
                                         waits=[ev_dl, ev_aneg] + atil_b[asl].wr(), inc=c_sca,
                                         nobar=True)
                            ini = STATS[:, O_HST + (seq * NCT + ct) * NST + n:][:, :1]
                            ev_h = p.add("vec", lambda e_, o=HT[hsl], a=ATIL[asl], b=BX, i=ini:
                                         e_.tensor_tensor_scan(out=o, data0=a, data1=b, initial=i,
                                                               op0=ALU.mult, op1=ALU.add),
                                         waits=[ev_a, ev_bx, carry_ev[seq][ct] or ev_hst0]
                                               + ht_b[hsl].wr(),
                                         inc=c_vec, nobar=True)
                            atil_b[asl].frees.append(ev_h)
                            if n % 2 == 0:
                                ev_h_even = ev_h

                            ev_cs = p.add("gp", lambda e_, o=ini, i=HT[hsl][:, CH - 1:CH]:
                                          e_.tensor_copy(out=o, in_=i), waits=[ev_h], inc=c_gp)
                            if n == NST - 1:
                                carry_ev[seq][ct] = ev_cs
                            ht_b[hsl].frees.append(ev_cs)
                            if n % 2 == 1:
                                k = n // 2
                                B1, B2, B3, T, T2 = TREE

                                def padd(o_, buf_i, extra=()):
                                    ev_ = p.add("vec", lambda e_, o=o_, a=HT[0], b=HT[1]:
                                                e_.tensor_add(o, a, b),
                                                waits=[ev_h, ev_h_even] + tree_b[buf_i].wr()
                                                      + list(extra),
                                                inc=c_vec, nobar=True)
                                    ht_b[0].frees.append(ev_)
                                    ht_b[1].frees.append(ev_)
                                    return ev_

                                def tadd(o_, a_, b_, buf_o, frees, opw):
                                    ev_ = p.add("vec", lambda e_, o=o_, a=a_, b=b_:
                                                e_.tensor_add(o, a, b),
                                                waits=list(opw) +
                                                      (tree_b[buf_o].wr() if buf_o is not None
                                                       else yacc_b.wr()),
                                                inc=c_vec, nobar=True)
                                    for fi in frees:
                                        tree_b[fi].frees.append(ev_)
                                    return ev_

                                if k % 2 == 0:
                                    lev["B1"] = padd(B1, 0)
                                else:
                                    lev["T"] = padd(T, 3)
                                    if k == 1 or k == 5:
                                        lev["B2"] = tadd(B2, B1, T, 1, (0, 3),
                                                         [lev["B1"], lev["T"]])
                                    elif k == 3:
                                        lev["T2"] = tadd(T2, B1, T, 4, (0, 3),
                                                         [lev["B1"], lev["T"]])
                                        lev["B3"] = tadd(B3, B2, T2, 2, (1, 4),
                                                         [lev["B2"], lev["T2"]])
                                    elif k == 7:
                                        lev["T2"] = tadd(T2, B1, T, 4, (0, 3),
                                                         [lev["B1"], lev["T"]])
                                        lev["T"] = tadd(T, B2, T2, 3, (1, 4),
                                                        [lev["B2"], lev["T2"]])
                                        ev_ysum = tadd(YACC, B3, T, None, (2, 3),
                                                       [lev["B3"], lev["T"]])
                        xs_b.frees.append((c_sca, c_sca.count))
                        ev_t1 = p.add("vec", lambda e_, o=YACC, a=YACC, b=RSIL:
                                      e_.tensor_mul(o, a, b), waits=[ev_ysum, ev_rsl],
                                      inc=c_vec)
                        rsil_b.frees.append(ev_t1)
                        YFT = ABF[:, YF_OFF + ct * CH:YF_OFF + (ct + 1) * CH]
                        ev_yf = p.add("vec", lambda e_, o=YFT, i=YACC,
                                      s=STATS[:, O_D + ct:O_D + ct + 1], x=XCONV:
                                      e_.scalar_tensor_tensor(out=o, in0=i, scalar=s, in1=x,
                                                              op0=ALU.add, op1=ALU.mult),
                                      waits=[ev_xc] + yf_b[ct].wr(), inc=c_vec)
                        xconv_b.frees.append(ev_yf)
                        if seq == 0 and chk == 2:
                            dump(ct, YFT, ev_yf, yf_b[ct])
                        if seq == 0 and chk == 0 and ct == 0:
                            dump(16, YACC, ev_t1, yacc_b)
                            dump(17, YFT, ev_yf, yf_b[ct])
                        yacc_b.frees.append(ev_yf)
                        yf_b[ct].ready = ev_yf

                    ev_bank = [None] * NDT
                    for ct in range(LCT):
                        sl = ct % 2
                        for dt in range(NDT):
                            ev_m = p.add("pe", lambda e_, o=PS[dt][0:128, 0:512],
                                         l=WOT[sl][:, dt * 128:(dt + 1) * 128],
                                         r=ABF[:, YF_OFF + ct * CH:YF_OFF + (ct + 1) * CH],
                                         s=(ct == 0), st=(ct == LCT - 1):
                                         e_.matmul(o, l, r, start=s, stop=st),
                                         waits=([yf_b[ct].ready, wot_ld[ct]] +
                                                (psb[dt].wr() if ct == 0 else [])),
                                         inc=(c_pe if (dt == NDT - 1 or ct == LCT - 1) else None))
                            if ct == LCT - 1:
                                ev_bank[dt] = ev_m
                            if dt == NDT - 1:
                                wot_b[sl].frees.append(ev_m)
                        if ct + 2 < LCT:
                            sl2 = (ct + 2) % 2
                            wot_ld[ct + 2] = p.add("sy",
                                                   lambda e_, o=WOT[sl2],
                                                   i=woutt_h[(ct + 2) * 128:(ct + 3) * 128, :]:
                                                   e_.dma_start(out=o, in_=i),
                                                   waits=wot_b[sl2].wr(), inc=c_wot[sl2])
                    for dt in range(NDT):
                        ev_ra = p.add("vec", lambda e_, o=RES[:, dt * TCORE + t0:][:, :CH],
                                      a=PS[dt][0:128, 0:512], b=RES[:, dt * TCORE + t0:][:, :CH]:
                                      e_.tensor_add(o, a, b), waits=[ev_bank[dt]], inc=c_vec,
                                      bar=True)

                        psb[dt].frees.append(ev_ra)
                        if dt == NDT - 1:
                            for ct in range(NCT):
                                yf_b[ct].frees.append(ev_ra)
                            xln_b.frees.append(ev_ra)

        def barrier():
            return [(c_pe, c_pe.count), (c_vec, c_vec.count),
                    (c_sca, c_sca.count), (c_gp, c_gp.count)]

        if n_blocks >= 2 and not _SKIP_P2:
            bar1 = barrier()
            for dt in range(NDT):
                for e2 in range(EXP_N):
                    p.add("gp", lambda e_, o=RWTB[:, dt * 8 + e2:dt * 8 + e2 + 1],
                          i=rw[e2, dt * 128:(dt + 1) * 128].rearrange("(p one) -> p one", one=1):
                          e_.dma_start(out=o, in_=i), inc=c_misc2)
            p.add("gp", lambda e_, o=RBROW[:, :],
                  i=rb[:].rearrange("(one f) -> one f", one=1):
                  e_.dma_start(out=o, in_=i), inc=c_misc2)
            p.add("gp", lambda e_, o=DBB[:, :], i=dwb[:, :]:
                  e_.dma_start(out=o, in_=i), inc=c_misc2)
            ev_misc2 = (c_misc2, c_misc2.count)
            ev_rwt = ev_rbb = ev_dbb = ev_misc2

            moe = dict(
                xln2_b=Buf(), glu_b=[Buf() for _ in range(8)], actw_b=[Buf() for _ in range(8)],
                gt_b=Buf(), up_b=Buf(), sig_b=Buf(), wbt_b=Buf(), wtb_b=Buf(), wrow_b=Buf(),
                wsc_b=Buf(),
                dacc_b=[Buf() for _ in range(NDT)], gub_b=Buf(),
                ring_b=[Buf() for _ in range(4)],
            )
            GLUA = [ABF[:, YF_OFF + i * CH:YF_OFF + (i + 1) * CH] for i in range(8)]
            ACTW = [ABF[:, YF_OFF + (8 + i) * CH:YF_OFF + (9 + i) * CH] for i in range(8)]
            WBT = ABF[:, HT_OFF:HT_OFF + CH]
            GTT = TREE[3]
            UP1 = TREE[4]
            SIGT = ABF[:, YACC_OFF:YACC_OFF + CH]
            DACC = [AF32[:, ht * 512:(ht + 1) * 512] for ht in range(NDT)]
            RING = [AW[:, i * 8192:(i + 1) * 8192] for i in range(4)]
            O_GUB = 864
            O_RT = 64

            def ring_load(sloti, e2, kind, extra_waits=()):
                if kind == "d":
                    src = dwnb_h[e2].rearrange("(a p) f -> p a f", p=128)
                else:
                    base = {"g": 0, "u": IMOE}[kind]
                    src = gupb_h[e2][:, base:base + IMOE].rearrange("(a p) f -> p a f", p=128)
                return p.add("sy", lambda e_,
                             o=RING[sloti].rearrange("p (a f) -> p a f", a=NDT), i=src:
                             e_.dma_start(out=o, in_=i),
                             waits=moe["ring_b"][sloti].wr() + list(extra_waits) +
                                   [ev_pre_all],
                             inc=c_ring[sloti])

            ring_plan = []
            for e2 in range(EXP_N):
                ring_plan += [(e2, "g"), (e2, "u"), (e2, "d")]

            LSC, LEXP = _LIMIT2 if _LIMIT2 is not None else (NDT, EXP_N)
            for sc in range(LSC):
                t0 = sc * CH
                ev_xln2 = emit_ln(t0, (bar1 if sc == 0 else []) + moe["xln2_b"].wr())
                if sc == 0:
                    dump(9, XLNT[0], ev_xln2, moe["xln2_b"])
                ev_wsc = []
                for mi in range(4):
                    bk = 4 + mi % 2
                    wv_ = psb[bk].wr()
                    ev_lg = None
                    for dt in range(NDT):
                        ev_lg = p.add("pe", lambda e_, o=PS[bk][0:128, 0:8],
                                      l=XLNT[dt][:, mi * 128:(mi + 1) * 128],
                                      r=RWTB[:, dt * 8:(dt + 1) * 8], s=(dt == 0):
                                      e_.matmul(o, l, r, start=s, stop=False),
                                      waits=([ev_xln2, ev_rwt] + wv_) if dt == 0 else [])
                    ev_lg = p.add("pe", lambda e_, o=PS[bk][0:128, 0:8],
                                  l=ONESB[0:1, 0:128], r=RBROW[:, :]:
                                  e_.matmul(o, l, r, start=False, stop=True),
                                  waits=[ev_rbb], inc=c_pe)
                    NM1 = SMALL[:, O_RT:O_RT + 1]
                    M2 = SMALL[:, O_RT + 1:O_RT + 2]
                    RZ = SMALL[:, O_RT + 2:O_RT + 3]
                    ZZ = SMALL[:, O_RT + 3:O_RT + 4]
                    PEXP = SMALL[:, O_RT + 4:O_RT + 12]
                    DD = SMALL[:, O_RT + 12:O_RT + 20]
                    EQ = SMALL[:, O_RT + 20:O_RT + 28]
                    WSC = SMALL[:, 96 + mi * 8:96 + (mi + 1) * 8]
                    ev_nm1 = p.add("vec", lambda e_, o=NM1, i=PS[bk][0:128, 0:8]:
                          e_.tensor_reduce(o, i, mybir.AxisListType.X, ALU.max, negate=True),
                          waits=[ev_lg], inc=c_vec, bar=True)
                    ev_px = p.add("sca", lambda e_, o=PEXP, i=PS[bk][0:128, 0:8], b=NM1:
                                  e_.activation(out=o, in_=i, func=AF.Exp, bias=b),
                                  waits=[ev_lg, ev_nm1], inc=c_sca, bar=True)
                    p.add("vec", lambda e_, o=DD, i=PS[bk][0:128, 0:8], s=NM1:
                          e_.tensor_scalar_add(o, i, s), waits=[ev_lg], bar=True)
                    ev_fr = p.add("vec", lambda e_, o=EQ, i=DD:
                                  e_.tensor_single_scalar(o, i, 0.0, ALU.is_ge), inc=c_vec)
                    psb[bk].frees.append(ev_fr)
                    psb[bk].frees.append(ev_px)
                    p.add("vec", lambda e_, o=EQ, i=EQ, s=DD:
                          e_.scalar_tensor_tensor(out=o, in0=i, scalar=-1e30, in1=s,
                                                  op0=ALU.mult, op1=ALU.add))
                    p.add("vec", lambda e_, o=M2, i=EQ:
                          e_.tensor_reduce(o, i, mybir.AxisListType.X, ALU.max), bar=True)
                    p.add("vec", lambda e_, o=EQ, i=DD, s=M2:
                          e_.tensor_single_scalar(o, i, s, ALU.is_ge))
                    p.add("vec", lambda e_, o=ZZ, i=PEXP:
                          e_.tensor_reduce(o, i, mybir.AxisListType.X, ALU.add), waits=[ev_px])
                    p.add("vec", lambda e_, o=RZ, i=ZZ: e_.reciprocal(o, i), bar=True)
                    ev_w = p.add("vec", lambda e_, o=WSC, i=PEXP, s=RZ, m=EQ:
                                 e_.scalar_tensor_tensor(out=o, in0=i, scalar=s, in1=m,
                                                         op0=ALU.mult, op1=ALU.mult),
                                 waits=moe["wsc_b"].wr(), inc=c_vec)
                    ev_wsc.append(ev_w)
                    if sc == 0 and mi == 0:
                        dump(10, SMALL[:, 64:92], ev_w)
                if sc == 0:
                    dump(0, SMALL[:, 96:128], ev_wsc[3])
                ev_wtb = None
                for mi in range(4):
                    pst = PS[6]
                    evt = p.add("pe", lambda e_, o=pst[0:8, 0:128],
                                i=SMALL[:, 96 + mi * 8:96 + (mi + 1) * 8]:
                                e_.transpose(o, i, EYE[:, :]),
                                waits=[ev_wsc[mi]] + psb[6].wr(), inc=c_pe)
                    ev_wtb = p.add("vec", lambda e_, o=WTB[0:8, mi * 128:(mi + 1) * 128],
                                   i=pst[0:8, 0:128]:
                                   e_.tensor_copy(out=o, in_=i),
                                   waits=[evt] + (moe["wtb_b"].wr() if mi == 0 else []),
                                   inc=c_vec, bar=True)
                    psb[6].frees.append(ev_wtb)
                    if mi == 3:
                        moe["wsc_b"].frees.append((c_pe, c_pe.count))
                if sc == 0:
                    ring_ld = {}
                    ring_pos = 0
                    for k in range(4):
                        e2, kind = ring_plan[ring_pos]
                        ring_ld[(e2, kind)] = ring_load(k, e2, kind, extra_waits=bar1)
                        ring_pos += 1

                for e2 in range(LEXP):
                    sg = (3 * e2) % 4
                    su = (3 * e2 + 1) % 4
                    sd = (3 * e2 + 2) % 4
                    ev_g1 = p.add("sy", lambda e_, o=GUBSTG[:, :],
                                  i=gub[e2, :].rearrange("(a p) -> a p", p=128):
                                  e_.dma_start(out=o, in_=i),
                                  waits=moe["gub_b"].wr(), inc=c_gub)
                    pst = PS[7]
                    ev_g2 = p.add("pe", lambda e_, o=pst[0:128, 0:16], i=GUBSTG[:, :]:
                                  e_.transpose(o, i, EYE[0:16, 0:16]),
                                  waits=[ev_g1] + psb[7].wr(), inc=c_pe)
                    moe["gub_b"].frees.append(ev_g2)
                    ev_g3 = p.add("vec", lambda e_, o=STATS[:, O_GUB:O_GUB + 16], i=pst[0:128, 0:16]:
                                  e_.tensor_copy(out=o, in_=i), waits=[ev_g2], inc=c_vec, bar=True)
                    psb[7].frees.append(ev_g3)
                    ev_g4 = p.add("vec", lambda e_, o=STATS[:, O_GUB + 16:O_GUB + 24],
                                  i=STATS[:, O_GUB + 8:O_GUB + 16]:
                                  e_.tensor_scalar_add(o, i, 1.0), inc=c_vec)
                    if sc == 0 and e2 == 0:
                        dump(6, STATS[:, O_GUB:O_GUB + 24], ev_g4)
                    bk = 4 + e2 % 2
                    ev_wr = p.add("sy", lambda e_, o=WROW[:, :], i=WTB[e2:e2 + 1, 0:512]:
                                  e_.dma_start(out=o, in_=i),
                                  waits=[ev_wtb] + moe["wrow_b"].wr(), inc=c_wrow)
                    ev_wb = p.add("pe", lambda e_, o=PS[bk][0:128, 0:512],
                                  l=ONESB[0:1, 0:128], r=WROW[:, :]:
                                  e_.matmul(o, l, r, start=True, stop=True),
                                  waits=[ev_wr] + psb[bk].wr(), inc=c_pe)
                    moe["wrow_b"].frees.append(ev_wb)
                    ev_wbc = p.add("vec", lambda e_, o=WBT, i=PS[bk][0:128, 0:512]:
                                   e_.tensor_copy(out=o, in_=i),
                                   waits=[ev_wb] + moe["wbt_b"].wr(), inc=c_vec, bar=True)
                    if sc == 0 and e2 == 0:
                        dump(7, WBT, ev_wbc, moe["wbt_b"])
                    psb[bk].frees.append(ev_wbc)
                    if e2 == LEXP - 1:
                        moe["wtb_b"].frees.append(ev_wb)

                    evg = ring_ld[(e2, "g")]
                    for ft in range(8):
                        bk2 = ft % 2
                        ev_mg = None
                        for dt in range(NDT):
                            ev_mg = p.add("pe", lambda e_, o=PS[bk2][0:128, 0:512],
                                          l=RING[sg][:, dt * 1024 + ft * 128:][:, :128],
                                          r=XLNT[dt], s=(dt == 0), st=(dt == NDT - 1):
                                          e_.matmul(o, l, r, start=s, stop=st),
                                          waits=([evg, ev_xln2] + psb[bk2].wr()) if dt == 0 else [],
                                          inc=(c_pe if dt == NDT - 1 else None))
                            if ft == 7 and dt == NDT - 1:
                                moe["ring_b"][sg].frees.append(ev_mg)
                        ev_gt = p.add("vec", lambda e_, o=GTT, i=PS[bk2][0:128, 0:512],
                                      s=STATS[:, O_GUB + ft:O_GUB + ft + 1]:
                                      e_.tensor_scalar(o, i, s, 7.0, ALU.add, ALU.min),
                                      waits=[ev_mg, ev_g3] + moe["gt_b"].wr(), inc=c_vec, bar=True)
                        psb[bk2].frees.append(ev_gt)
                        ev_sig = p.add("sca", lambda e_, o=SIGT, i=GTT:
                                       e_.activation(out=o, in_=i, func=AF.Sigmoid, scale=1.702),
                                       waits=[ev_gt] + moe["sig_b"].wr(), inc=c_sca)
                        ev_glu = p.add("vec", lambda e_, o=GLUA[ft], a=GTT, b=SIGT:
                                       e_.tensor_mul(o, a, b),
                                       waits=[ev_sig] + moe["glu_b"][ft].wr(), inc=c_vec)
                        if sc == 0 and e2 == 0 and ft == 0:
                            dump(1, GTT, ev_gt, moe["gt_b"])
                            dump(2, GLUA[ft], ev_glu, moe["glu_b"][ft])
                        moe["gt_b"].frees.append(ev_glu)
                        moe["sig_b"].frees.append(ev_glu)
                        moe["glu_b"][ft].ready = ev_glu
                    evu = ring_ld[(e2, "u")]
                    for ft in range(8):
                        bk2 = 2 + ft % 2
                        ev_mu = None
                        for dt in range(NDT):
                            ev_mu = p.add("pe", lambda e_, o=PS[bk2][0:128, 0:512],
                                          l=RING[su][:, dt * 1024 + ft * 128:][:, :128],
                                          r=XLNT[dt], s=(dt == 0), st=(dt == NDT - 1):
                                          e_.matmul(o, l, r, start=s, stop=st),
                                          waits=([evu] + psb[bk2].wr()) if dt == 0 else [],
                                          inc=(c_pe if dt == NDT - 1 else None))
                            if ft == 7 and dt == NDT - 1:
                                moe["ring_b"][su].frees.append(ev_mu)
                            if ft == 7 and dt == NDT - 1 and e2 == LEXP - 1:
                                moe["xln2_b"].frees.append(ev_mu)
                        ev_u1 = p.add("vec", lambda e_, o=UP1, i=PS[bk2][0:128, 0:512],
                                      s=STATS[:, O_GUB + 16 + ft:O_GUB + 17 + ft]:
                                      e_.tensor_scalar(o, i, s, 8.0, ALU.add, ALU.min),
                                      waits=[ev_mu, ev_g4] + moe["up_b"].wr(), inc=c_vec, bar=True)
                        psb[bk2].frees.append(ev_u1)
                        p.add("vec", lambda e_, o=UP1, i=UP1:
                              e_.tensor_scalar_max(o, i, -6.0))
                        p.add("vec", lambda e_, o=UP1, a=UP1, b=GLUA[ft]:
                              e_.tensor_mul(o, a, b), waits=[moe["glu_b"][ft].ready])
                        ev_aw = p.add("vec", lambda e_, o=ACTW[ft], a=UP1, b=WBT:
                                      e_.tensor_mul(o, a, b),
                                      waits=[ev_wbc] + moe["actw_b"][ft].wr(), inc=c_vec)
                        moe["up_b"].frees.append(ev_aw)
                        moe["glu_b"][ft].frees.append(ev_aw)
                        moe["actw_b"][ft].ready = ev_aw
                        if sc == 0 and e2 == 0 and ft == 0:
                            dump(3, ACTW[ft], ev_aw, moe["actw_b"][ft])
                        if e2 == LEXP - 1:
                            moe["wbt_b"].frees.append(ev_aw)
                    moe["wbt_b"].frees.append((c_vec, c_vec.count))
                    evd = ring_ld[(e2, "d")]
                    for ht in range(NDT):
                        bk2 = ht % 2
                        wv_ = psb[bk2].wr()
                        ev_md = None
                        first = True
                        if e2 == 0:
                            ev_md = p.add("pe", lambda e_, o=PS[bk2][0:128, 0:512],
                                          l=DBB[:, ht * 128:(ht + 1) * 128], r=WTB[0:8, 0:512]:
                                          e_.matmul(o, l, r, start=True, stop=False),
                                          waits=[ev_dbb, ev_wtb] + wv_, inc=None)
                            first = False
                        for it in range(8):
                            ev_md = p.add("pe", lambda e_, o=PS[bk2][0:128, 0:512],
                                          l=RING[sd][:, it * 1024 + ht * 128:][:, :128],
                                          r=ACTW[it], s=first and it == 0, st=(it == 7):
                                          e_.matmul(o, l, r, start=s, stop=st),
                                          waits=([evd] + (wv_ if first else []) +
                                                 [moe["actw_b"][i2].ready for i2 in range(8)])
                                                if it == 0 else [],
                                          inc=(c_pe if it == 7 else None))
                            if ht == NDT - 1 and it == 7:
                                moe["ring_b"][sd].frees.append(ev_md)
                        if e2 == 0:
                            ev_da = p.add("vec", lambda e_, o=DACC[ht], i=PS[bk2][0:128, 0:512]:
                                          e_.tensor_copy(out=o, in_=i),
                                          waits=[ev_md] + moe["dacc_b"][ht].wr(), inc=c_vec,
                                          bar=True)
                        else:
                            ev_da = p.add("vec", lambda e_, o=DACC[ht], a=PS[bk2][0:128, 0:512],
                                          b=DACC[ht]:
                                          e_.tensor_add(o, a, b), waits=[ev_md], inc=c_vec,
                                          bar=True)
                        psb[bk2].frees.append(ev_da)
                        if sc == 0 and ht == 0 and e2 == 0:
                            dump(4, DACC[ht], ev_da, moe["dacc_b"][ht])
                        if e2 == LEXP - 1:
                            moe["dacc_b"][ht].ready = ev_da
                            if sc == 0 and ht == 0:
                                dump(5, DACC[ht], ev_da, moe["dacc_b"][ht])
                    for i2 in range(8):
                        moe["actw_b"][i2].frees.append((c_pe, c_pe.count))
                    for _ in range(3):
                        if ring_pos < len(ring_plan):
                            e3, kind3 = ring_plan[ring_pos]
                            slot3 = (3 * e3 + {"g": 0, "u": 1, "d": 2}[kind3]) % 4
                            ring_ld[(e3, kind3)] = ring_load(slot3, e3, kind3)
                            ring_pos += 1
                for ht in range(NDT):
                    ev_r2 = p.add("vec", lambda e_, o=RES[:, ht * TCORE + t0:][:, :CH],
                                  a=DACC[ht], b=RES[:, ht * TCORE + t0:][:, :CH]:
                                  e_.tensor_add(o, a, b), inc=c_vec)
                    moe["dacc_b"][ht].frees.append(ev_r2)
                    if ht == NDT - 1:
                        moe["xln2_b"].frees.append(ev_r2)
                    if sc == 0 and ht == 0:
                        dump(8, RES[:, 0:512], ev_r2)
                ring_plan = []
                for e2 in range(EXP_N):
                    ring_plan += [(e2, "g"), (e2, "u"), (e2, "d")]
                ring_pos_next = 0
                if sc < LSC - 1:
                    ring_ld = {}
                    ring_pos = 0
                    for k in range(4):
                        e3, kind3 = ring_plan[ring_pos]
                        slot3 = (3 * e3 + {"g": 0, "u": 1, "d": 2}[kind3]) % 4
                        ring_ld[(e3, kind3)] = ring_load(slot3, e3, kind3)
                        ring_pos += 1

        if n_blocks >= 3 and not _SKIP_P3:
            bar3 = barrier()
            QB = 512
            NQB = S // QB
            QTS_OFF = YF_OFF
            CTXT_OFF = YF_OFF + 4096
            ATT_OFF = HT_OFF
            RNB_OFF = TREE_OFF
            WQK_OFF = TREE_OFF + 3 * 512
            WVS_OFF = WQK_OFF + 3 * 1024
            WOS_OFF = TREE_OFF
            QTS = ABF[:, QTS_OFF:QTS_OFF + 4096]
            CTXT = ABF[:, CTXT_OFF:CTXT_OFF + 4096]
            ATT = [ABF[:, ATT_OFF + i * 512:ATT_OFF + (i + 1) * 512] for i in range(2)]
            RNB = ABF[:, RNB_OFF:RNB_OFF + 512]
            WQKS = [ABF[:, WQK_OFF + i * 1024:WQK_OFF + (i + 1) * 1024] for i in range(3)]
            WVS = [ABF[:, WVS_OFF + i * 512:WVS_OFF + (i + 1) * 512] for i in range(2)]
            WOS = [ABF[:, WOS_OFF + i * 128:WOS_OFF + (i + 1) * 128] for i in range(4)]
            KTC = AW[:, 16384:32768]
            VC = AW[:, 0:16384]
            O_BQK = 864
            O_BO = 880

            att = dict(qts_b=Buf(), ctxt_b=[Buf() for _ in range(16)], att_b=[Buf(), Buf()],
                       rnb_b=Buf(), wqk_b=[Buf() for _ in range(3)],
                       wv_b=[Buf() for _ in range(2)], wo_b=[Buf() for _ in range(4)],
                       xln3_b=Buf(), rnt_b=Buf(), bstg_b=Buf(), kv_free_ev=None)
            c_wqk = [p.chan(f"wqk{i}", 16, True) for i in range(3)]
            c_wv = [p.chan(f"wv{i}", 16, True) for i in range(2)]
            c_wo = [p.chan(f"wo{i}", 16, True) for i in range(4)]
            c_misc3 = p.chan("m3", 16, True)
            c_bstg = p.chan("bstg", 16, True)
            for c in (c_wqk + c_wv + c_wo + [c_misc3, c_bstg]):
                c.h = st.enter_context(nc.semaphore(c.name))

            for dt in range(NDT):
                p.add("gp", lambda e_, o=WNB[:, dt * 16:(dt + 1) * 16],
                      i=wn[dt * 128:(dt + 1) * 128, :]:
                      e_.dma_start(out=o, in_=i), inc=c_misc3)
            p.add("gp", lambda e_, o=BNROW[:, :], i=bn[:].rearrange("(one f) -> one f", one=1):
                  e_.dma_start(out=o, in_=i), inc=c_misc3)
            p.add("gp", lambda e_, o=BVROW[:, :], i=bv[:].rearrange("(one f) -> one f", one=1):
                  e_.dma_start(out=o, in_=i), inc=c_misc3)
            ev_misc3 = (c_misc3, c_misc3.count)
            def col_load(vec_ap, n16, dst_off, wt_extra):
                ev1 = p.add("sy", lambda e_, o=GUBSTG[0:n16, :], i=vec_ap:
                            e_.dma_start(out=o, in_=i),
                            waits=att["bstg_b"].wr() + wt_extra, inc=c_bstg)
                ev2 = p.add("pe", lambda e_, o=PS[7][0:128, 0:n16], i=GUBSTG[0:n16, :]:
                            e_.transpose(o, i, EYE[0:n16, 0:n16]),
                            waits=[ev1] + psb[7].wr(), inc=c_pe)
                att["bstg_b"].frees.append(ev2)
                ev3 = p.add("vec", lambda e_, o=STATS[:, dst_off:dst_off + n16],
                            i=PS[7][0:128, 0:n16]:
                            e_.tensor_copy(out=o, in_=i), waits=[ev2], inc=c_vec)
                psb[7].frees.append(ev3)
                return ev3
            ev_bqk = col_load(bqk[:].rearrange("(a p) -> a p", p=128), 16, O_BQK, bar3)
            ev_bo = col_load(bo[:].rearrange("(a p) -> a p", p=128), 8, O_BO, [])

            def wqk_load(sloti, ft, extra=()):
                return p.add("sy", lambda e_,
                             o=WQKS[sloti].rearrange("p (a f) -> p a f", a=NDT),
                             i=wqkb_h[:, ft * 128:(ft + 1) * 128].rearrange(
                                 "(a p) f -> p a f", p=128):
                             e_.dma_start(out=o, in_=i),
                             waits=att["wqk_b"][sloti].wr() + list(extra) + [ev_pre_all],
                             inc=c_wqk[sloti])

            def wv_load(sloti, dt, g, extra=()):
                return p.add("sy", lambda e_, o=WVS[sloti],
                             i=wvb_h[dt * 128:(dt + 1) * 128, g * 512:(g + 1) * 512]:
                             e_.dma_start(out=o, in_=i),
                             waits=att["wv_b"][sloti].wr() + list(extra) + [ev_pre_all],
                             inc=c_wv[sloti])

            def wo_load(sloti, di, do, extra=()):
                return p.add("sy", lambda e_, o=WOS[sloti],
                             i=wob_h[di * 128:(di + 1) * 128, do * 128:(do + 1) * 128]:
                             e_.dma_start(out=o, in_=i),
                             waits=att["wo_b"][sloti].wr() + list(extra) + [ev_pre_all],
                             inc=c_wo[sloti])

            LSEQ3, LQB3 = _LIMIT3 if _LIMIT3 is not None else (SEQN, NQB)
            for seq in range(LSEQ3):
                for qb in range(LQB3):
                    t0 = seq * S + qb * QB
                    ev_xln3 = emit_ln(t0, (bar3 if (seq == 0 and qb == 0) else []) +
                                      att["xln3_b"].wr())
                    ev_rn4 = []
                    for mi in range(4):
                        bk = 4 + mi % 2
                        wv_ = psb[bk].wr()
                        ev_n = None
                        for dt in range(NDT):
                            ev_n = p.add("pe", lambda e_, o=PS[bk][0:128, 0:16],
                                         l=XLNT[dt][:, mi * 128:(mi + 1) * 128],
                                         r=WNB[:, dt * 16:(dt + 1) * 16], s=(dt == 0):
                                         e_.matmul(o, l, r, start=s, stop=False),
                                         waits=([ev_xln3, ev_misc3] + wv_) if dt == 0 else [])
                        ev_n = p.add("pe", lambda e_, o=PS[bk][0:128, 0:16],
                                     l=ONESB[0:1, 0:128], r=BNROW[:, :]:
                                     e_.matmul(o, l, r, start=False, stop=True), inc=c_pe)
                        EXPN = SMALL[:, 80:96]
                        ev_e = p.add("sca", lambda e_, o=EXPN, i=PS[bk][0:128, 0:16]:
                                     e_.activation(out=o, in_=i, func=AF.Exp), waits=[ev_n])
                        psb[bk].frees.append(ev_e)
                        RNm = SMALL[:, 64 + 0:80] if False else SMALL[:, 64:80]
                        ev_r = p.add("vec", lambda e_, o=RNm, i=EXPN:
                                     e_.reciprocal(o, i), waits=[ev_e, (c_sca, c_sca.count)])
                        ev_t = p.add("pe", lambda e_, o=PS[6][0:16, 0:128], i=RNm:
                                     e_.transpose(o, i, EYE[:, :]),
                                     waits=[ev_r] + psb[6].wr(), inc=c_pe)
                        ev_c = p.add("vec", lambda e_, o=RNT[:, mi * 128:(mi + 1) * 128],
                                     i=PS[6][0:16, 0:128]:
                                     e_.tensor_copy(out=o, in_=i),
                                     waits=[ev_t] + (att["rnt_b"].wr() if mi == 0 else []),
                                     inc=c_vec)
                        psb[6].frees.append(ev_c)
                        ev_rn4.append(ev_c)
                    ev_rnt = ev_rn4[-1]
                    if seq == 0 and qb == 1:
                        dump(11, RNT[0:16, 0:512], ev_rnt, att["rnt_b"])

                    if seq == 0 and qb == 0:
                        wqk_ld = {}
                        for k3 in range(3):
                            wqk_ld[k3] = wqk_load(k3, k3, extra=bar3)
                        p3_ftg = [0]
                    ev_qts_last = None
                    ev_ktev = None
                    for ft in range(16):
                        sl = p3_ftg[0] % 3
                        bk = ft % 2
                        evw = wqk_ld[sl]
                        ev_m = None
                        for dt in range(NDT):
                            ev_m = p.add("pe", lambda e_, o=PS[bk][0:128, 0:512],
                                         l=WQKS[sl][:, dt * 128:(dt + 1) * 128], r=XLNT[dt],
                                         s=(dt == 0), st=(dt == NDT - 1):
                                         e_.matmul(o, l, r, start=s, stop=st),
                                         waits=([evw, ev_xln3] + psb[bk].wr()) if dt == 0 else [],
                                         inc=(c_pe if dt == NDT - 1 else None))
                        att["wqk_b"][sl].frees.append(ev_m)
                        if ft < 8:
                            bk2 = 4 + ft % 2
                            ev_b = p.add("pe", lambda e_, o=PS[bk2][0:128, 0:512],
                                         l=EHB[0:16, ft * 128:(ft + 1) * 128], r=RNT[:, 0:512]:
                                         e_.matmul(o, l, r, start=True, stop=True),
                                         waits=[ev_rnt] + psb[bk2].wr(), inc=c_pe)
                            ev_rb = p.add("vec", lambda e_, o=RNB, i=PS[bk2][0:128, 0:512]:
                                          e_.tensor_copy(out=o, in_=i),
                                          waits=[ev_b] + att["rnb_b"].wr(), inc=c_vec)
                            psb[bk2].frees.append(ev_rb)
                            ev_q = p.add("vec", lambda e_, o=ABF[:, QTS_OFF + ft * 512:][:, :512],
                                         i=PS[bk][0:128, 0:512],
                                         s=STATS[:, O_BQK + ft:O_BQK + ft + 1], m=RNB:
                                         e_.scalar_tensor_tensor(out=o, in0=i, scalar=s, in1=m,
                                                                 op0=ALU.add, op1=ALU.mult),
                                         waits=[ev_m, ev_bqk, ev_rb] + (att["qts_b"].wr()
                                                                        if ft == 0 else []),
                                         inc=c_vec)
                            att["rnb_b"].frees.append(ev_q)
                            psb[bk].frees.append(ev_q)
                            ev_qts_last = ev_q
                            if seq == 0 and qb == 1 and ft == 0:
                                dump(12, ABF[:, QTS_OFF:QTS_OFF + 512], ev_q, att["qts_b"])
                                dump(13, RNB, ev_rb, att["rnb_b"])
                        else:
                            ftk = ft - 8
                            ev_k = p.add("vec", lambda e_,
                                         o=KTC[:, ftk * 2048 + qb * 512:][:, :512],
                                         i=PS[bk][0:128, 0:512],
                                         s=STATS[:, O_BQK + ft:O_BQK + ft + 1]:
                                         e_.tensor_scalar_add(o, i, s),
                                         waits=[ev_m, ev_bqk] + ([att["kv_free_ev"]]
                                                                  if (ft == 8 and qb == 0
                                                                      and seq > 0) else []),
                                         inc=c_vec)
                            psb[bk].frees.append(ev_k)
                            ev_ktev = ev_k
                            if seq == 0 and qb == 1 and ft == 8:
                                dump(14, KTC[:, 0:512], ev_k)
                                dump(19, KTC[:, 512:1024], ev_k)
                        tgt = p3_ftg[0] + 3
                        p3_ftg[0] += 1
                        if tgt < 16 * LQB3 * LSEQ3:
                            wqk_ld[tgt % 3] = wqk_load(tgt % 3, tgt % 16)

                    ev_vev = None
                    for g in range(2):
                        for dt in range(NDT):
                            sl = dt % 2
                            evl = wv_load(sl, dt, g,
                                          extra=(bar3 if (seq == 0 and qb == 0 and g == 0
                                                          and dt == 0) else []))
                            for mi in range(4):
                                bk = 2 + mi if False else (2 + mi)
                                ev_m = p.add("pe", lambda e_, o=PS[2 + mi][0:128, 0:512],
                                             l=XLNT[dt][:, mi * 128:(mi + 1) * 128],
                                             r=WVS[sl], s=(dt == 0):
                                             e_.matmul(o, l, r, start=s, stop=False),
                                             waits=([evl] + (psb[2 + mi].wr() if dt == 0 else [])),
                                             inc=(c_pe if mi == 3 else None))
                            att["wv_b"][sl].frees.append(ev_m)
                        for mi in range(4):
                            ev_m = p.add("pe", lambda e_, o=PS[2 + mi][0:128, 0:512],
                                         l=ONESB[0:1, 0:128], r=BVROW[:, g * 512:(g + 1) * 512]:
                                         e_.matmul(o, l, r, start=False, stop=True),
                                         waits=[ev_misc3], inc=c_pe)
                            ev_v = p.add("vec", lambda e_,
                                         o=VC[:, (qb * 4 + mi) * 1024 + g * 512:][:, :512],
                                         i=PS[2 + mi][0:128, 0:512]:
                                         e_.tensor_copy(out=o, in_=i),
                                         waits=[ev_m] + ([att["kv_free_ev"]]
                                                         if (qb == 0 and g == 0 and mi == 0
                                                             and seq > 0) else []),
                                         inc=c_vec)
                            psb[2 + mi].frees.append(ev_v)
                            ev_vev = ev_v
                            if seq == 0 and qb == 1 and g == 0 and mi == 0:
                                dump(15, VC[:, 0:512], ev_v)

                    nkt = 4 * (qb + 1)
                    ev_ctx_done = []
                    for h in range(16):
                        po = (h % 2) * 64
                        ftk = h // 2
                        bkc = 2 + h % 2
                        wvc = psb[bkc].wr()
                        ev_cm = None
                        for kt in range(nkt):
                            bks = kt % 2
                            ev_s = None
                            wv_ = psb[bks].wr()
                            ev_s = p.add("pe", lambda e_, o=PS[bks][0:128, 0:512],
                                         l=KTC[po:po + 64, ftk * 2048 + kt * 128:][:, :128],
                                         r=QTS[po:po + 64, :][:, (h // 2) * 512:(h // 2) * 512 + 512]
                                         if False else
                                         ABF[po:po + 64, QTS_OFF + ftk * 512:QTS_OFF + ftk * 512 + 512],
                                         s=True:
                                         e_.matmul(o, l, r, start=s, stop=True),
                                         waits=[ev_qts_last, ev_ktev] + wv_, inc=c_pe)
                            asl = kt % 2
                            if kt >= 4 * qb:
                                j = kt - 4 * qb
                                ev_a = p.add("vec", lambda e_, o=ATT[asl], a=PS[bks][0:128, 0:512],
                                             b=PTRIU[:, 384 - 128 * j:][:, :512]:
                                             e_.tensor_mul(o, a, b),
                                             waits=[ev_s] + att["att_b"][asl].wr(), inc=c_vec)
                                if seq == 0 and qb == 1 and h == 0 and kt == 4:
                                    dump(17, ATT[asl], ev_a, att["att_b"][asl])
                            else:
                                ev_a = p.add("vec", lambda e_, o=ATT[asl], i=PS[bks][0:128, 0:512]:
                                             e_.tensor_copy(out=o, in_=i),
                                             waits=[ev_s] + att["att_b"][asl].wr(), inc=c_vec)
                                if seq == 0 and qb == 1 and h == 0 and kt == 0:
                                    dump(18, ATT[asl], ev_a, att["att_b"][asl])
                            psb[bks].frees.append(ev_a)
                            ev_cm = p.add("pe", lambda e_, o=PS[bkc][po:po + 64, 0:512],
                                          l=VC[:, kt * 1024 + h * 64:][:, :64], r=ATT[asl],
                                          s=(kt == 0), st=(kt == nkt - 1):
                                          e_.matmul(o, l, r, start=s, stop=st),
                                          waits=[ev_a, ev_vev] + (wvc if kt == 0 else []),
                                          inc=c_pe)
                            att["att_b"][asl].frees.append(ev_cm)
                        ev_cc = p.add("vec", lambda e_,
                                      o=ABF[po:po + 64, CTXT_OFF + ftk * 512:CTXT_OFF + ftk * 512 + 512],
                                      i=PS[bkc][po:po + 64, 0:512]:
                                      e_.tensor_copy(out=o, in_=i),
                                      waits=[ev_cm] + (att["ctxt_b"][h].wr()), inc=c_vec)
                        psb[bkc].frees.append(ev_cc)
                        att["ctxt_b"][h].ready = ev_cc
                        if seq == 0 and qb == 1 and h == 1:
                            dump(16, ABF[:, CTXT_OFF:CTXT_OFF + 512], ev_cc, att["ctxt_b"][0])
                        if seq == 0 and qb == 1 and h == 15:
                            dump(20, ABF[:, CTXT_OFF + 7 * 512:CTXT_OFF + 8 * 512], ev_cc,
                                 att["ctxt_b"][15])

                    wo_ld = {}
                    for k3 in range(4):
                        wo_ld[k3] = wo_load(k3, k3, 0, extra=[ev_vev, ev_qts_last])
                    for do in range(NDT):
                        bk = 4 + do % 2
                        wv_ = psb[bk].wr()
                        ev_m = None
                        if seq == 0 and qb == 1 and do == 0:
                            dump(21, WOS[0], wo_ld[0], att["wo_b"][0])
                        for di in range(NDT):
                            idx = do * 8 + di
                            sl = idx % 4
                            evw = wo_ld[sl]
                            ev_m = p.add("pe", lambda e_, o=PS[bk][0:128, 0:512],
                                         l=WOS[sl],
                                         r=ABF[:, CTXT_OFF + di * 512:CTXT_OFF + (di + 1) * 512],
                                         s=(di == 0), st=(di == NDT - 1):
                                         e_.matmul(o, l, r, start=s, stop=st),
                                         waits=([evw] +
                                                [att["ctxt_b"][2 * di].ready,
                                                 att["ctxt_b"][2 * di + 1].ready] +
                                                (wv_ if di == 0 else [])),
                                         inc=c_pe)
                            att["wo_b"][sl].frees.append(ev_m)
                            nidx = idx + 4
                            if nidx < 64:
                                wo_ld[nidx % 4] = wo_load(nidx % 4, nidx % 8, nidx // 8)
                        ev_ra = p.add("vec", lambda e_, o=RES[:, do * TCORE + t0:][:, :512],
                                      a=PS[bk][0:128, 0:512],
                                      s=STATS[:, O_BO + do:O_BO + do + 1],
                                      b=RES[:, do * TCORE + t0:][:, :512]:
                                      e_.scalar_tensor_tensor(out=o, in0=a, scalar=s, in1=b,
                                                              op0=ALU.add, op1=ALU.add),
                                      waits=[ev_m, ev_bo], inc=c_vec)
                        if seq == 0 and qb == 1 and do == 0:
                            dump(23, RES[:, 0 * TCORE + t0:][:, :512], ev_m)
                        psb[bk].frees.append(ev_ra)
                        if seq == 0 and qb == 1 and do == 0:
                            dump(22, RES[:, 0 * TCORE + t0:][:, :512], ev_ra)
                        if do == NDT - 1:
                            ev_wo_done = (c_pe, c_pe.count)
                            tree_b[0].frees.append(ev_wo_done)
                            att["rnb_b"].frees.append(ev_wo_done)
                            att["xln3_b"].frees.append(ev_ra)
                            att["qts_b"].frees.append(ev_ra)
                            for h in range(16):
                                att["ctxt_b"][h].frees.append(ev_ra)
                    att["rnt_b"].frees.append((c_pe, c_pe.count))
                att["kv_free_ev"] = (c_pe, c_pe.count)

        if n_blocks >= 4:
            bar4 = barrier()
            c_wq = [p.chan(f"wq{i}", 16, True) for i in range(2)]
            c_wqs = [p.chan(f"wqs{i}", 16, True) for i in range(2)]
            c_hsp = p.chan("hsp", 16, True)
            c_hld = [p.chan(f"hld{i}", 16, True) for i in range(2)]
            c_wg = [p.chan(f"wg{i}", 16, True) for i in range(3)]
            c_wu = [p.chan(f"wu{i}", 16, True) for i in range(3)]
            c_wd = [p.chan(f"wd{i}", 16, True) for i in range(2)]
            c_cc = p.chan("ccio", 16, True)
            for c in ([c_hsp, c_cc] + c_wq + c_wqs + c_hld + c_wg + c_wu + c_wd):
                c.h = st.enter_context(nc.semaphore(c.name))

            SG_OFF, UP_OFF = 0, 512
            H_OFF, HQ_OFF = 1024, 2048
            WG_OFF, WU_OFF = 5120, 8192
            WD_OFF = 11264
            XQ_OFF = 13312
            WGS = [ABF[:, WG_OFF + i * 1024:WG_OFF + (i + 1) * 1024] for i in range(3)]
            WUS = [ABF[:, WU_OFF + i * 1024:WU_OFF + (i + 1) * 1024] for i in range(3)]
            SGT = ABF[:, SG_OFF:SG_OFF + 512]
            UPT = ABF[:, UP_OFF:UP_OFF + 512]
            HT4 = [ABF[:, H_OFF + i * 512:H_OFF + (i + 1) * 512] for i in range(2)]
            HQ4 = [ABF[:, HQ_OFF + i * 512:HQ_OFF + (i + 1) * 512] for i in range(2)]
            WDS = [ABF[:, WD_OFF + i * 1024:WD_OFF + (i + 1) * 1024] for i in range(2)]
            XQT = [ABF[:, XQ_OFF + i * 512:XQ_OFF + (i + 1) * 512] for i in range(NDT)]
            XLN4 = [AF32[:, i * 512:(i + 1) * 512] for i in range(NDT)]
            CCROW = SMALL[0:1, 80:96]

            ffn = dict(wq_b=Buf(), wg_b=[Buf() for _ in range(3)], wu_b=[Buf() for _ in range(3)],
                       wd_b=[Buf() for _ in range(2)], sg_b=Buf(), up_b=Buf(),
                       h_b=[Buf(), Buf()], hq_b=[Buf(), Buf()], xq_b=Buf(), ccrow_b=Buf())

            WQBF2 = [ABF[:, 0:128], ABF[:, 128:256]]
            wqbf_b = [Buf(), Buf()]
            wq_b2 = [Buf(), Buf()]
            ev_wqs_done = None
            for widx, (wsrc, wdst, nrow_t, ncol) in enumerate(
                    [(w_gate, wqg_h, NIT, D), (w_up, wqu_h, NIT, D), (w_down, wqd_h, NDT, IFFN)]):
                for rt in range(nrow_t):
                    if widx < 2:
                        ssl = rt % 2
                        sb = ssl * 1024
                        lw = wq_b2[ssl].wr()
                    else:
                        ssl = 0
                        sb = 0
                        lw = wq_b2[0].wr() + wq_b2[1].wr()
                    STG = AF32[:, sb:sb + ncol]
                    evl = p.add("sy", lambda e_, o=STG,
                                i=wsrc[rt * 128:(rt + 1) * 128, :]:
                                e_.dma_start(out=o, in_=i),
                                waits=lw + (bar4 if widx == 0 and rt == 0 else []),
                                inc=c_wq[ssl])
                    ev_mx = p.add("vec", lambda e_, o=SMALL[:, 66:67], i=STG:
                                  e_.tensor_reduce(o, i, mybir.AxisListType.X, ALU.max,
                                                   apply_absolute_value=True),
                                  waits=[evl], inc=c_vec)
                    p.add("vec", lambda e_, o=SMALL[:, 71:72], i=SMALL[:, 66:67]:
                          e_.tensor_scalar_mul(o, i, 1.0 / QMAX))
                    p.add("vec", lambda e_, o=SMALL[:, 72:73], i=SMALL[:, 71:72]:
                          e_.reciprocal(o, i))
                    p.add("vec", lambda e_, o=STG, i=STG,
                          s=SMALL[:, 72:73]:
                          e_.tensor_scalar_mul(o, i, s))
                    p.add("vec", lambda e_, o=STG, i=STG:
                          e_.tensor_scalar(o, i, ROUND_C, -ROUND_C, ALU.add, ALU.add))
                    ev_q = p.add("vec", lambda e_, o=STG, i=STG,
                                 s=SMALL[:, 71:72]:
                                 e_.tensor_scalar_mul(o, i, s), inc=c_vec)
                    for ct4 in range(ncol // 128):
                        pslot = 6 + ct4 % 2
                        evt = p.add("pe", lambda e_, o=PS[pslot][0:128, 0:128],
                                    i=AF32[:, sb + ct4 * 128:sb + (ct4 + 1) * 128]:
                                    e_.transpose(o, i, EYE[:, :]),
                                    waits=[ev_q] + psb[pslot].wr(), inc=c_pe)
                        if ct4 == ncol // 128 - 1:
                            if widx == 2 and ncol // 128 > 8:
                                wq_b2[0].frees.append(evt)
                                wq_b2[1].frees.append(evt)
                            else:
                                wq_b2[ssl].frees.append(evt)
                        qsl = ct4 % 2
                        evc = p.add("vec", lambda e_, o=WQBF2[qsl], i=PS[pslot][0:128, 0:128]:
                                    e_.tensor_copy(out=o, in_=i),
                                    waits=[evt] + wqbf_b[qsl].wr(), inc=c_vec)
                        psb[pslot].frees.append(evc)
                        if widx < 2:
                            odst = wdst[rt][:, ct4 * 128:(ct4 + 1) * 128]
                        else:
                            odst = wdst[ct4 * 128:(ct4 + 1) * 128, rt * 128:(rt + 1) * 128]
                        ev_s = p.add("sy", lambda e_, o=odst, i=WQBF2[qsl]:
                                     e_.dma_start(out=o, in_=i), waits=[evc], inc=c_wqs[qsl])
                        wqbf_b[qsl].frees.append(ev_s)
            ev_wqs_done = (c_wqs[0], c_wqs[0].count)
            ev_wqs_done2 = (c_wqs[1], c_wqs[1].count)

            ev_m0 = p.add("vec", lambda e_: e_.memset(SMALL[:, 64:128], 0.0),
                          waits=bar4, inc=c_vec)
            p.add("vec", lambda e_: e_.memset(CCROW, 0.0), inc=c_vec)
            for sc in range(NDT):
                t0 = sc * CH
                ev_x4 = emit_ln(t0, ([ev_m0] + bar4) if sc == 0 else [], outs=XLN4)
                for dt in range(NDT):
                    ev_mx = p.add("vec", lambda e_, o=SMALL[:, 66:67], i=XLN4[dt]:
                                  e_.tensor_reduce(o, i, mybir.AxisListType.X, ALU.max,
                                                   apply_absolute_value=True),
                                  waits=[ev_x4], inc=c_vec)
                    p.add("vec", lambda e_, o=SMALL[:, 64:65], a=SMALL[:, 64:65],
                          b=SMALL[:, 66:67]:
                          e_.tensor_max(o, a, b))
            ev_t = p.add("pe", lambda e_, o=PS[6][0:1, 0:128], i=SMALL[:, 64:65]:
                         e_.transpose(o, i, EYE[:, :]),
                         waits=[(c_vec, c_vec.count)] + psb[6].wr(), inc=c_pe)
            ev_g = p.add("vec", lambda e_, o=CCROW, i=PS[6][0:1, 0:128].rearrange("a (b c) -> a (b c)")
                         if False else PS[6][0:1, 0:128]:
                         e_.tensor_reduce(o[:, 0:1] if False else CCROW[:, 0:1], i,
                                          mybir.AxisListType.X, ALU.max),
                         waits=[ev_t], inc=c_vec)
            psb[6].frees.append(ev_g)
            ev_ccw = p.add("sy", lambda e_, o=cc1i[:, :], i=CCROW:
                           e_.dma_start(out=o, in_=i), waits=[ev_g], inc=c_cc)
            ev_cc1 = p.add("gp", lambda e_: e_.collective_compute(
                "AllReduce", ALU.max, replica_groups=[[0, 1, 2, 3, 4, 5, 6, 7]],
                ins=[cc1i[:, :]], outs=[cc1o[:, :]]),
                waits=[ev_ccw], inc=c_gp)
            ev_ccr = p.add("sy", lambda e_, o=CCROW, i=cc1o[:, :]:
                           e_.dma_start(out=o, in_=i),
                           waits=[ev_cc1] + ffn["ccrow_b"].wr(), inc=c_cc)
            ev_cl = p.add("sy", lambda e_, o=SMALL[0:1, 96:97],
                          i=clip_x[:].rearrange("(a b) -> a b", a=1):
                          e_.dma_start(out=o, in_=i), inc=c_cc)
            ev_sg = p.add("sca", lambda e_, o=SMALL[0:1, 97:98], i=SMALL[0:1, 96:97]:
                          e_.activation(out=o, in_=i, func=AF.Sigmoid),
                          waits=[ev_cl], inc=c_sca)
            ev_s1 = p.add("vec", lambda e_, o=SMALL[0:1, 98:99], a=SMALL[0:1, 97:98], b=CCROW[:, 0:1]:
                          e_.tensor_mul(o, a, b), waits=[ev_sg, ev_ccr], inc=c_vec)
            p.add("vec", lambda e_, o=SMALL[0:1, 98:99], i=SMALL[0:1, 98:99]:
                  e_.tensor_scalar_mul(o, i, 1.0 / QMAX))
            ev_rs1p = p.add("vec", lambda e_, o=SMALL[0:1, 99:100], i=SMALL[0:1, 98:99]:
                            e_.reciprocal(o, i), inc=c_vec)
            ev_onesf = p.add("vec", lambda e_: e_.memset(GUBSTG[0:1, :], 1.0),
                             waits=bar4, inc=c_vec)

            def bcast_col(src_row, dst_col, wt):
                evb = p.add("pe", lambda e_, o=PS[6][0:128, 0:1], l=GUBSTG[0:1, 0:128], r=src_row:
                            e_.matmul(o, l, r, start=True, stop=True),
                            waits=[wt, ev_onesf] + psb[6].wr(), inc=c_pe)
                evc2 = p.add("vec", lambda e_, o=dst_col, i=PS[6][0:128, 0:1]:
                             e_.tensor_copy(out=o, in_=i), waits=[evb], inc=c_vec)
                psb[6].frees.append(evc2)
                return evc2
            ev_s1c = bcast_col(SMALL[0:1, 98:99], SMALL[:, 67:68], ev_rs1p)
            ev_rs1c = bcast_col(SMALL[0:1, 99:100], SMALL[:, 68:69], ev_rs1p)
            dump(0, SMALL[:, 64:100], ev_rs1c)

            p.add("vec", lambda e_: e_.memset(SMALL[:, 65:66], 0.0), inc=c_vec)
            wg_ld, wu_ld = {}, {}

            def wg_load(sloti, it, extra=()):
                return p.add("sy", lambda e_, o=WGS[sloti], i=wqg_h[it]:
                             e_.dma_start(out=o, in_=i),
                             waits=ffn["wg_b"][sloti].wr() + list(extra), inc=c_wg[sloti])

            def wu_load(sloti, it, extra=()):
                return p.add("sy", lambda e_, o=WUS[sloti], i=wqu_h[it]:
                             e_.dma_start(out=o, in_=i),
                             waits=ffn["wu_b"][sloti].wr() + list(extra), inc=c_wu[sloti])

            for k3 in range(3):
                wg_ld[k3] = wg_load(k3, k3, extra=[ev_wqs_done, ev_wqs_done2])
                wu_ld[k3] = wu_load(k3, k3, extra=[ev_wqs_done, ev_wqs_done2])
            dump(7, WGS[0][:, 0:512], wg_ld[0], ffn["wg_b"][0])
            gq_pos = [0]
            n_hsp = 0
            for sc in range(NDT):
                t0 = sc * CH
                ev_x4 = emit_ln(t0, [], outs=XLN4, rstd_mul_col=SMALL[0:1, 68:69],
                                rstd_mul_ev=ev_rs1c)
                ev_xq = None
                for dt in range(NDT):
                    p.add("vec", lambda e_, o=XLN4[dt], i=XLN4[dt]:
                          e_.tensor_scalar(o, i, ROUND_C, -ROUND_C, ALU.add, ALU.add),
                          waits=([ev_x4] + ffn["xq_b"].wr()) if dt == 0 else [])
                    p.add("vec", lambda e_, o=XLN4[dt], i=XLN4[dt]:
                          e_.tensor_scalar(o, i, QMAX, -QMAX - 1.0, ALU.min, ALU.max))
                    ev_xq = p.add("vec", lambda e_, o=XQT[dt], i=XLN4[dt], s=SMALL[:, 67:68]:
                                  e_.tensor_scalar_mul(o, i, s),
                                  waits=[ev_s1c], inc=c_vec)
                    if sc == 0 and dt == 0:
                        dump(1, XQT[0], ev_xq, ffn["xq_b"])
                for it in range(NIT):
                    slg = gq_pos[0] % 3
                    evwg, evwu = wg_ld[slg], wu_ld[slg]
                    bkg = it % 2
                    bku = 2 + it % 2
                    wvg = psb[bkg].wr()
                    wvu = psb[bku].wr()
                    ev_mg = None
                    ev_mu = None
                    for dt in range(NDT):
                        ev_mg = p.add("pe", lambda e_, o=PS[bkg][0:128, 0:512],
                                      l=WGS[slg][:, dt * 128:(dt + 1) * 128], r=XQT[dt],
                                      s=(dt == 0), st=(dt == NDT - 1):
                                      e_.matmul(o, l, r, start=s, stop=st),
                                      waits=([evwg, ev_xq] + wvg) if dt == 0 else [],
                                      inc=(c_pe if dt == NDT - 1 else None))
                    ffn["wg_b"][slg].frees.append(ev_mg)
                    for dt in range(NDT):
                        ev_mu = p.add("pe", lambda e_, o=PS[bku][0:128, 0:512],
                                      l=WUS[slg][:, dt * 128:(dt + 1) * 128], r=XQT[dt],
                                      s=(dt == 0), st=(dt == NDT - 1):
                                      e_.matmul(o, l, r, start=s, stop=st),
                                      waits=([evwu, ev_xq] + wvu) if dt == 0 else [],
                                      inc=(c_pe if dt == NDT - 1 else None))
                    ffn["wu_b"][slg].frees.append(ev_mu)
                    tgt = gq_pos[0] + 3
                    gq_pos[0] += 1
                    if tgt < NIT * NDT:
                        wg_ld[tgt % 3] = wg_load(tgt % 3, tgt % NIT)
                        wu_ld[tgt % 3] = wu_load(tgt % 3, tgt % NIT)
                    ev_sg2 = p.add("sca", lambda e_, o=SGT, i=PS[bkg][0:128, 0:512]:
                                   e_.activation(out=o, in_=i, func=AF.Silu),
                                   waits=[ev_mg] + ffn["sg_b"].wr(), inc=c_sca)
                    psb[bkg].frees.append(ev_sg2)
                    ev_up = p.add("vec", lambda e_, o=UPT, i=PS[bku][0:128, 0:512]:
                                  e_.tensor_copy(out=o, in_=i),
                                  waits=[ev_mu] + ffn["up_b"].wr(), inc=c_vec)
                    psb[bku].frees.append(ev_up)
                    hsl = it % 2
                    ev_h = p.add("vec", lambda e_, o=HT4[hsl], a=SGT, b=UPT:
                                 e_.tensor_mul(o, a, b),
                                 waits=[ev_sg2, ev_up] + ffn["h_b"][hsl].wr(), inc=c_vec)
                    if sc == 0 and it == 0:
                        dump(2, SGT, ev_sg2, ffn["sg_b"])
                        dump(3, UPT, ev_up, ffn["up_b"])
                        dump(4, HT4[hsl], ev_h, ffn["h_b"][hsl])
                    ffn["sg_b"].frees.append(ev_h)
                    ffn["up_b"].frees.append(ev_h)
                    ev_hs = p.add("sy", lambda e_,
                                  o=hsp_h[it * 128:(it + 1) * 128, sc * 512:(sc + 1) * 512],
                                  i=HT4[hsl]:
                                  e_.dma_start(out=o, in_=i), waits=[ev_h], inc=c_hsp)
                    n_hsp += 1
                    ffn["h_b"][hsl].frees.append(ev_hs)
                    ev_mx = p.add("vec", lambda e_, o=SMALL[:, 66:67], i=HT4[hsl]:
                                  e_.tensor_reduce(o, i, mybir.AxisListType.X, ALU.max,
                                                   apply_absolute_value=True),
                                  waits=[ev_h], inc=c_vec)
                    ev_mh = p.add("vec", lambda e_, o=SMALL[:, 65:66], a=SMALL[:, 65:66],
                                  b=SMALL[:, 66:67]:
                                  e_.tensor_max(o, a, b), inc=c_vec)
                    ffn["h_b"][hsl].frees.append(ev_mh)
                if sc == NDT - 1:
                    ffn["xq_b"].frees.append((c_pe, c_pe.count))

            dump(8, SMALL[:, 64:72], (c_vec, c_vec.count))
            ev_t2 = p.add("pe", lambda e_, o=PS[6][0:1, 0:128], i=SMALL[:, 65:66]:
                          e_.transpose(o, i, EYE[:, :]),
                          waits=[(c_vec, c_vec.count)] + psb[6].wr(), inc=c_pe)
            ev_g2 = p.add("vec", lambda e_, o=CCROW[:, 0:1], i=PS[6][0:1, 0:128]:
                          e_.tensor_reduce(o, i, mybir.AxisListType.X, ALU.max),
                          waits=[ev_t2] + ffn["ccrow_b"].wr(), inc=c_vec)
            psb[6].frees.append(ev_g2)
            ev_ccw2 = p.add("sy", lambda e_, o=cc2i[:, :], i=CCROW:
                            e_.dma_start(out=o, in_=i), waits=[ev_g2], inc=c_cc)
            ev_cc2 = p.add("gp", lambda e_: e_.collective_compute(
                "AllReduce", ALU.max, replica_groups=[[0, 1, 2, 3, 4, 5, 6, 7]],
                ins=[cc2i[:, :]], outs=[cc2o[:, :]]),
                waits=[ev_ccw2], inc=c_gp)
            ev_ccr2 = p.add("sy", lambda e_, o=CCROW, i=cc2o[:, :]:
                            e_.dma_start(out=o, in_=i), waits=[ev_cc2], inc=c_cc)
            ev_cl2 = p.add("sy", lambda e_, o=SMALL[0:1, 96:97],
                           i=clip_down[:].rearrange("(a b) -> a b", a=1):
                           e_.dma_start(out=o, in_=i), waits=[ev_ccr2], inc=c_cc)
            ev_sg3 = p.add("sca", lambda e_, o=SMALL[0:1, 97:98], i=SMALL[0:1, 96:97]:
                           e_.activation(out=o, in_=i, func=AF.Sigmoid),
                           waits=[ev_cl2], inc=c_sca)
            ev_s2 = p.add("vec", lambda e_, o=SMALL[0:1, 98:99], a=SMALL[0:1, 97:98],
                          b=CCROW[:, 0:1]:
                          e_.tensor_mul(o, a, b), waits=[ev_sg3, ev_ccr2], inc=c_vec)
            p.add("vec", lambda e_, o=SMALL[0:1, 98:99], i=SMALL[0:1, 98:99]:
                  e_.tensor_scalar_mul(o, i, 1.0 / QMAX))
            ev_rs2p = p.add("vec", lambda e_, o=SMALL[0:1, 99:100], i=SMALL[0:1, 98:99]:
                            e_.reciprocal(o, i), inc=c_vec)
            ev_s2c = bcast_col(SMALL[0:1, 98:99], SMALL[:, 69:70], ev_rs2p)
            ev_rs2c = bcast_col(SMALL[0:1, 99:100], SMALL[:, 70:71], ev_rs2p)
            dump(5, SMALL[:, 64:100], ev_rs2c)
            ev_hsp_all = (c_hsp, c_hsp.count)

            wd_ld = {}

            def wd_load2(sloti, it, extra=()):
                return p.add("sy", lambda e_, o=WDS[sloti][:, 0:1024],
                             i=wqd_h[it * 128:(it + 1) * 128, :]:
                             e_.dma_start(out=o, in_=i),
                             waits=ffn["wd_b"][sloti].wr() + list(extra), inc=c_wd[sloti])

            for k3 in range(2):
                wd_ld[k3] = wd_load2(k3, k3, extra=[ev_wqs_done, ev_wqs_done2])
            hl_ld = {}
            for sc in range(NDT):
                ev_bank4 = [None] * NDT
                for it in range(NIT):
                    hsl = it % 2
                    ev_hl = p.add("sy", lambda e_, o=HT4[hsl],
                                  i=hsp_h[it * 128:(it + 1) * 128, sc * 512:(sc + 1) * 512]:
                                  e_.dma_start(out=o, in_=i),
                                  waits=[ev_hsp_all] + ffn["h_b"][hsl].wr(), inc=c_hld[hsl])
                    p.add("vec", lambda e_, o=AF32[:, 0:512], i=HT4[hsl], s=SMALL[:, 70:71]:
                          e_.tensor_scalar_mul(o, i, s),
                          waits=[ev_hl, ev_rs2c] + ffn["hq_b"][hsl].wr())
                    ffn["h_b"][hsl].frees.append((c_vec, c_vec.count))
                    p.add("vec", lambda e_, o=AF32[:, 0:512], i=AF32[:, 0:512]:
                          e_.tensor_scalar(o, i, ROUND_C, -ROUND_C, ALU.add, ALU.add))
                    ev_hq = p.add("vec", lambda e_, o=HQ4[hsl], i=AF32[:, 0:512]:
                                  e_.tensor_scalar(o, i, QMAX, -QMAX - 1.0, ALU.min, ALU.max),
                                  inc=c_vec)
                    if sc == 0 and it == 0:
                        dump(6, HQ4[hsl], ev_hq, ffn["hq_b"][hsl])
                    sld = it % 2
                    evwd = wd_ld[sld]
                    for ht in range(NDT):
                        ev_md = p.add("pe", lambda e_, o=PS[ht][0:128, 0:512],
                                      l=WDS[sld][:, ht * 128:(ht + 1) * 128], r=HQ4[hsl],
                                      s=(it == 0), st=(it == NIT - 1):
                                      e_.matmul(o, l, r, start=s, stop=st),
                                      waits=([evwd, ev_hq] + (psb[ht].wr() if it == 0 else [])),
                                      inc=c_pe)
                        if it == NIT - 1:
                            ev_bank4[ht] = ev_md
                        if ht == NDT - 1:
                            ffn["wd_b"][sld].frees.append(ev_md)
                            ffn["hq_b"][hsl].frees.append(ev_md)
                    nit2 = it + 2
                    if nit2 < NIT:
                        wd_ld[nit2 % 2] = wd_load2(nit2 % 2, nit2)
                    elif sc < NDT - 1:
                        wd_ld[(nit2 - NIT) % 2] = wd_load2((nit2 - NIT) % 2, nit2 - NIT)
                t0 = sc * CH
                for ht in range(NDT):
                    ev_ra = p.add("vec", lambda e_, o=RES[:, ht * TCORE + t0:][:, :512],
                                  a=PS[ht][0:128, 0:512], s=SMALL[:, 69:70],
                                  b=RES[:, ht * TCORE + t0:][:, :512]:
                                  e_.scalar_tensor_tensor(out=o, in0=a, scalar=s, in1=b,
                                                          op0=ALU.mult, op1=ALU.add),
                                  waits=[ev_bank4[ht], ev_s2c], inc=c_vec)
                    psb[ht].frees.append(ev_ra)

        OSTG = [AF32[:, 0:1024], AF32[:, 1024:2048]]
        ostg_b = [Buf(), Buf()]
        n_out = 0
        last_vec_ev = (c_vec, c_vec.count)
        for seq in range(SEQN):
            for tt in range(S // 128):
                sl = (seq * (S // 128) + tt) % 2
                evlast = None
                for dt in range(NDT):
                    evc0 = p.add("vec", lambda e_, o=OSTG[sl][:, dt * 128:(dt + 1) * 128],
                                 i=RES[:, dt * TCORE + seq * S + tt * 128:][:, :128]:
                                 e_.tensor_copy(out=o, in_=i),
                                 waits=[last_vec_ev] + (ostg_b[sl].wr() if dt == 0 else []),
                                 inc=c_vec)
                    evt, evc = transpose_128(OSTG[sl][:, dt * 128:(dt + 1) * 128],
                                             OSTG[sl][:, dt * 128:(dt + 1) * 128],
                                             tcount % 2, [evc0])
                    tcount += 1
                    evlast = evc
                evd = p.add("sca", lambda e_, o=y_out[seq, tt * 128:(tt + 1) * 128, :], i=OSTG[sl]:
                            e_.dma_start(out=o, in_=i), waits=[evlast], inc=c_out[sl])
                n_out += 1
                ostg_b[sl].frees.append(evd)
        p.add("sy", lambda e_: e_.nop(),
              waits=[(c_out[0], c_out[0].count), (c_out[1], c_out[1].count)])

        with nc.Block() as blk:
            blk.tensor(lambda e_: p.emit("pe", e_))
            blk.vector(lambda e_: p.emit("vec", e_))
            blk.scalar(lambda e_: p.emit("sca", e_))
            blk.gpsimd(lambda e_: p.emit("gp", e_))
            blk.sync(lambda e_: p.emit("sy", e_))
    return nc, p.tally()


def _constants():
    eye = np.eye(128, dtype=np.float32)
    ones = np.ones((128, 128), dtype=BF)
    r = np.arange(128)[:, None]
    x = np.arange(896)[None, :]
    ptriu = (x >= r + 384).astype(BF)
    eh = np.zeros((NHEAD, D), dtype=BF)
    for f in range(D):
        eh[f // HDIM, f] = 1
    return eye, ones, ptriu, eh


def kernel(**inputs):
    x = np.asarray(inputs["L0_attn_x"], dtype=np.float32)
    assert x.shape == (B, S, D)
    eye, ones, ptriu, eh = _constants()
    weight_names = [k for k in inputs.keys() if k != "L0_attn_x"]
    nc, tally = build_program(_BLOCKS)
    in_maps = []
    for core in range(NCORE):
        m = {"x": np.ascontiguousarray(x[core * SEQN:(core + 1) * SEQN]),
             "c_eye": eye, "c_ones": ones, "c_ptriu": ptriu, "c_eh": eh}
        for k in weight_names:
            m[k] = np.ascontiguousarray(np.asarray(inputs[k], dtype=np.float32))
        in_maps.append(m)
    res = run_bass_kernel_spmd(nc, in_maps, core_ids=list(range(NCORE)))
    out = np.concatenate([r["y"].reshape(SEQN, S, D) for r in res.results], axis=0)
    return out.astype(np.float32)
